# version 02
# baseline (speedup 1.0000x reference)
import numpy as np
import concourse.bass as bass
import concourse.mybir as mybir
from concourse.alu_op_type import AluOpType
from concourse.bass_utils import run_bass_kernel_spmd

F32 = mybir.dt.float32
BF16 = mybir.dt.bfloat16
I32 = mybir.dt.int32
AF = mybir.ActivationFunctionType
AX = mybir.AxisListType.X
Mul = AluOpType.mult
Add = AluOpType.add
Sub = AluOpType.subtract
Div = AluOpType.divide
MaxOp = AluOpType.max
MinOp = AluOpType.min
AMax = AluOpType.abs_max

H, DK, DV = 8, 64, 128
D, FF = 1024, 4096
EPS = 1e-5
_RACE_HOOK = None

ENG_SEM = {"sync": "dI", "tensor": "tT", "vector": "tV", "scalar": "tS",
           "gpsimd": "dG"}
ENG_AMT = {"sync": 16, "tensor": 1, "vector": 1, "scalar": 1, "gpsimd": 16}


class Plan:

    def __init__(self):
        self.steps = {e: [] for e in ENG_SEM}
        self.count = {s: 0 for s in set(ENG_SEM.values())}
        self.res = {}

    def _get(self, name):
        return self.res.setdefault(name, {"w": None, "r": []})

    def emit(self, eng, fn, reads=(), writes=(), extra_waits=()):
        sem = ENG_SEM[eng]
        amt = ENG_AMT[eng]
        waits = {}
        for nm in reads:
            r = self._get(nm)
            if r["w"] is not None:
                s, v = r["w"]
                waits[s] = max(waits.get(s, 0), v)
        for nm in writes:
            r = self._get(nm)
            if r["w"] is not None:
                s, v = r["w"]
                waits[s] = max(waits.get(s, 0), v)
            for (s, v) in r["r"]:
                waits[s] = max(waits.get(s, 0), v)
        for (s, v) in extra_waits:
            waits[s] = max(waits.get(s, 0), v)
        waits.pop(sem, None)
        self.count[sem] += amt
        post = self.count[sem]
        for nm in reads:
            self._get(nm)["r"].append((sem, post))
        for nm in writes:
            r = self._get(nm)
            r["w"] = (sem, post)
            r["r"] = []
        self.steps[eng].append((fn, sorted(waits.items())))
        return post


def build_program(TOK, GROUPS, dbg=False):
    import os as _os
    KPH = int(_os.environ.get("KPHASE", "9"))
    NCHLIM = int(_os.environ.get("NCHLIM", "99"))
    SKMB = _os.environ.get("SKMB") == "1"
    SKSOL = _os.environ.get("SKSOL") == "1"
    SKBND = _os.environ.get("SKBND") == "1"
    SKOST = _os.environ.get("SKOST") == "1"
    SKFAC = _os.environ.get("SKFAC") == "1"
    SKUPD = _os.environ.get("SKUPD") == "1"
    SKACC = _os.environ.get("SKACC") == "1"
    FUPALT = _os.environ.get("FUPOLD") != "1"
    FI1ALT = _os.environ.get("FI1OLD") != "1"
    SKMB2 = _os.environ.get("SKMB2") == "1"

    T_ALL = TOK * GROUPS
    NCH = TOK // 64
    nc = bass.Bass("TRN2", target_bir_lowering=False,
                   detect_race_conditions=(_os.environ.get("RACEDET") == "1"))

    dr = {}

    def din(name, shape, dtype=F32):
        dr[name] = nc.dram_tensor(name, list(shape), dtype, kind="ExternalInput").ap()

    din("xT", (D, T_ALL))
    din("wqT", (D, 512)); din("wkT", (D, 512)); din("wvT", (D, 1024))
    din("waT", (D, 1024)); din("wbT", (D, 8)); din("wgT", (D, 1024))
    din("woT", (1024, D))
    din("w1T", (D, FF)); din("w2T", (FF, D))
    din("ba_c", (128, 8)); din("bb_c", (8, 1)); din("b1_c", (128, 32)); din("b2_c", (128, 8))
    din("qcw_c", (128, 16)); din("kcw_c", (128, 16)); din("vcw_c", (128, 32))
    din("qcb_c", (128, 4)); din("kcb_c", (128, 4)); din("vcb_c", (128, 8))
    din("lnw_c", (128, 1)); din("lnb_c", (128, 1))
    din("ident", (128, 128)); din("maskU", (64, 512)); din("maskQn", (64, 512))
    din("ones_c", (128, 1)); din("onesr_c", (128, 128)); din("sel8_c", (64, 1024))
    out_ap = nc.dram_tensor("outT", [D, T_ALL], F32, kind="ExternalOutput").ap()
    dbg_aps = {}
    if dbg:
        for n, shape in (("d_zg", (128, 8 * TOK)), ("d_qt", (128, 4 * TOK)),
                         ("d_kt", (128, 4 * TOK)), ("d_vt", (128, 8 * TOK)),
                         ("d_at", (128, 8 * TOK)), ("d_o", (128, 8 * TOK)),
                         ("d_gt", (128, 8 * TOK)), ("d_sv", (128, 512)),
                         ("d_rng", (128, 8 * TOK)), ("d_z2", (128, 8 * TOK)),
                         ("d_cpr", (128, 512)), ("d_gng", (128, 512)),
                         ("d_wv", (128, 512)), ("d_pva", (128, 512)),
                         ("d_mqn", (128, 512)), ("d_mupH", (128, 512)),
                         ("d_pb7t", (128, 128)), ("d_wtb3", (128, 128)),
                         ("d_acc5", (128, 128)), ("d_wtb0", (128, 128)),
                         ("d_acc2", (128, 128)), ("d_wvj0", (128, 512)),
                         ("d_wvj1", (128, 512)), ("d_wvj1b", (128, 512)),
                         ("d_xq", (128, 2048)), ("d_ffh", (128, 4096)),
                         ("d_ffq", (128, 2048)), ("d_deq", (128, 512)),
                         ("d_dq2", (128, 128)), ("d_m1", (1, 1)), ("d_m2", (1, 1))):
            dbg_aps[n] = nc.dram_tensor(n, list(shape), F32, kind="ExternalOutput").ap()
        dbg_aps["d_w1q"] = nc.dram_tensor("d_w1q", [128, 4096], BF16,
                                          kind="ExternalOutput").ap()
    w1q = nc.dram_tensor("w1q", [D, FF], BF16, kind="Internal").ap()
    w2q = nc.dram_tensor("w2q", [FF, D], BF16, kind="Internal").ap()

    pl = Plan()
    pl.count["dO"] = 0
    pl.count["dD"] = 0
    ctx_stack = []
    sb = {}

    def sbuf(name, shape, dtype=F32):
        g = nc.sbuf_tensor(name, list(shape), dtype)
        sb[name] = g.__enter__()
        ctx_stack.append(g)

    def psum(name, shape):
        g = nc.psum_tensor(name, list(shape), F32)
        sb[name] = g.__enter__()
        ctx_stack.append(g)

    sbuf("xg", (128, 8 * TOK))
    sbuf("zg", (128, 8 * TOK))
    sbuf("pre", (128, 8 * (TOK + 4)))
    sbuf("qhal", (128, 16))
    sbuf("khal", (128, 16))
    sbuf("vhal", (128, 32))
    sbuf("qt", (128, 4 * TOK))
    sbuf("kt", (128, 4 * TOK))
    sbuf("vt", (128, 8 * TOK))
    sbuf("at", (128, 8 * TOK))
    sbuf("gt", (128, 8 * TOK))
    sbuf("bb8", (8, TOK))
    sbuf("rng", (128, 8 * TOK))
    sbuf("cpr", (128, 512))
    sbuf("gng", (128, 512))
    sbuf("pva", (128, 512))
    sbuf("wv", (128, 512))
    sbuf("mup", (128, 512))
    sbuf("mupH", (128, 512))
    sbuf("mqn", (128, 512))
    sbuf("wtbP", (128, 128))
    sbuf("kstage", (128, 128))
    for p in range(4):
        sbuf(f"wthP{p}", (128, 128))
        sbuf(f"kthBD{p}", (128, 128))
    sbuf("sv", (128, 8 * DK))
    sbuf("sk", (128, 4 * 128))
    sbuf("s_ident", (128, 128))
    sbuf("onesrow", (128, 128))
    sbuf("s_maskU", (64, 512))
    sbuf("s_maskQn", (64, 512))
    sbuf("ones", (128, 1))
    sbuf("sel8", (64, 1024))
    sbuf("shufM", (128, 512))
    for n, shp in (("s_ba_c", (128, 8)), ("s_bb_c", (8, 1)), ("s_b1_c", (128, 32)),
                   ("s_b2_c", (128, 8)), ("s_qcw_c", (128, 16)), ("s_kcw_c", (128, 16)),
                   ("s_vcw_c", (128, 32)), ("s_qcb_c", (128, 4)), ("s_kcb_c", (128, 4)),
                   ("s_vcb_c", (128, 8)), ("s_lnw_c", (128, 1)), ("s_lnb_c", (128, 1))):
        sbuf(n, shp)
    sbuf("m127", (1, 1))
    sbuf("m227", (1, 1))
    sbuf("swcol", (128, 1))
    sbuf("stat", (1, 1536))
    sbuf("stat2", (1, 512))
    sbuf("strow", (1, 512))
    sbuf("deqb", (128, 512))
    sbuf("wbuf", (128, 4096))
    sbuf("sgn", (128, 512), BF16)
    sbuf("tA", (128, 1024))
    sbuf("tB", (128, 1024))
    sb["ffh"] = sb["vt"]
    for i in range(8):
        psum(f"pb{i}", (128, 512))

    _SBN = {n: "s_" + n for n in ("ident", "maskU", "maskQn", "ba_c", "bb_c", "b1_c", "b2_c", "qcw_c", "kcw_c", "vcw_c", "qcb_c", "kcb_c", "vcb_c", "lnw_c", "lnb_c")}

    def ld_const(name):
        sn = _SBN.get(name, name)
        def fn(eng, name=name, sn=sn):
            return eng.dma_start(out=sb[sn].ap(), in_=dr[name])
        pl.emit("sync", fn, writes=[sn])

    def onesrow_ap(p):
        return sb["onesrow"][p:p + 1, 0:128]

    def bcast_row(bank_name, row_ap, ncols, src_part, reads, pname=None):
        def fn(eng):
            return eng.matmul(sb[bank_name][:, 0:ncols], onesrow_ap(src_part), row_ap,
                              start=True, stop=True)
        pl.emit("tensor", fn, reads=list(reads) + ["onesrow"], writes=[bank_name])

    for n in ("ident", "maskU", "maskQn", "ba_c", "bb_c", "b1_c", "b2_c", "qcw_c",
              "kcw_c", "vcw_c", "qcb_c", "kcb_c", "vcb_c", "lnw_c", "lnb_c"):
        ld_const(n)
    def f_lo(eng):
        return eng.dma_start(out=sb["ones"].ap(), in_=dr["ones_c"])
    pl.emit("sync", f_lo, writes=["ones"])
    def f_lor(eng):
        return eng.dma_start(out=sb["onesrow"].ap(), in_=dr["onesr_c"])
    pl.emit("sync", f_lor, writes=["onesrow"])
    def f_sel(eng):
        return eng.dma_start(out=sb["sel8"].ap(), in_=dr["sel8_c"])
    pl.emit("sync", f_sel, writes=["sel8"])

    def fmemset(name, val, region=None):
        def fn(eng, name=name, val=val, region=region):
            a = sb[name].ap() if region is None else region
            return eng.memset(a, val)
        pl.emit("vector", fn, writes=[name])

    fmemset("sv", 0.0)
    fmemset("mup", 0.0)
    fmemset("kstage", 0.0)
    fmemset("mupH", 0.0)
    for nm in ("qhal", "khal", "vhal"):
        fmemset(nm, 0.0)

    def quant_weights(wsrc, wdst, n_rows, n_cols, mname):
        nslab = n_rows // 128
        for s in range(nslab):
            def fl(eng, s=s):
                return eng.dma_start(out=sb["wbuf"][:, 0:n_cols],
                                     in_=wsrc[s * 128:(s + 1) * 128, :])
            pl.emit("sync", fl, writes=["wbuf", "wbuf0", "wbuf1"])
            for qq in range(n_cols // 512):
                def fa(eng, qq=qq):
                    return eng.activation(sb["tA"][:, 0:512],
                                          sb["wbuf"][:, qq * 512:qq * 512 + 512],
                                          AF.Abs)
                pl.emit("scalar", fa, reads=["wbuf"], writes=["tA0"])
                for k4 in range(4):
                    first = (s == 0 and qq == 0 and k4 == 0)
                    last = (s == nslab - 1 and qq == n_cols // 512 - 1 and k4 == 3)
                    def fm(eng, first=first, last=last, k4=k4):
                        return eng.matmul(sb["pb7"][0:1, 0:128], sb["ones"].ap(),
                                          sb["tA"][:, k4 * 128:(k4 + 1) * 128],
                                          start=first, stop=last)
                    pl.emit("tensor", fm, reads=["tA0", "ones"],
                            writes=["pb7"] if (first or last) else [])
        def frow(eng):
            return eng.tensor_copy(sb["stat"][0:1, 16:144], sb["pb7"][0:1, 0:128])
        pl.emit("vector", frow, reads=["pb7"], writes=["stat"])
        def fx2(eng):
            return eng.matmul(sb["pb6"][:, 0:1], sb["stat"][0:1, 16:144],
                              onesrow_ap(0)[0:1, 0:1], start=True, stop=True)
        pl.emit("tensor", fx2, reads=["stat", "onesrow"], writes=["pb6"])
        def fcp1(eng):
            return eng.tensor_copy(sb["swcol"].ap(), sb["pb6"][:, 0:1])
        pl.emit("vector", fcp1, reads=["pb6"], writes=["swcol"])
        def fx3(eng):
            return eng.matmul(sb["pb7"][0:1, 4:5], sb["swcol"].ap(),
                              sb["ones"].ap(), start=True, stop=True)
        pl.emit("tensor", fx3, reads=["swcol", "ones"], writes=["pb7"])
        def fr1q(eng, n_rows=n_rows, n_cols=n_cols):
            return eng.tensor_scalar(sb["stat"][0:1, 1:2], sb["pb7"][0:1, 4:5],
                                     1.0 / (n_rows * n_cols), 1e-5, Mul, MaxOp)
        pl.emit("vector", fr1q, reads=["pb7"], writes=["stat"])
        def fspq(eng):
            return eng.tensor_copy(sb["shufM"][0:64, 0:512], sb["s_maskU"].ap())
        pl.emit("vector", fspq, reads=["s_maskU"], writes=["shufM"])
        def fr2q(eng, mname=mname):
            eng.tensor_scalar(sb[mname].ap(), sb["stat"][0:1, 1:2], 1.0 / 127.0, None, Mul)
            return eng.reciprocal(sb["stat"][0:1, 2:3], sb["stat"][0:1, 1:2])
        pl.emit("vector", fr2q, reads=["stat"], writes=["stat", mname])
        bcast_row("pb6", sb["stat"][0:1, 2:3], 1, 0, ["stat"])
        def fc(eng):
            return eng.tensor_copy(sb["swcol"].ap(), sb["pb6"][:, 0:1])
        pl.emit("vector", fc, reads=["pb6"], writes=["swcol"])
        for s in range(nslab):
            def fl2(eng, s=s):
                return eng.dma_start(out=sb["wbuf"][:, 0:n_cols],
                                     in_=wsrc[s * 128:(s + 1) * 128, :])
            pl.emit("sync", fl2, writes=["wbuf", "wbuf0", "wbuf1"])
            for qq in range(n_cols // 512):
                pp = qq % 2
                def fmu(eng, qq=qq, pp=pp):
                    return eng.mul(sb["tA"][:, pp * 512:pp * 512 + 512],
                                   sb["wbuf"][:, qq * 512:qq * 512 + 512],
                                   sb["swcol"].ap())
                pl.emit("scalar", fmu, reads=["wbuf", "swcol"],
                        writes=["tA%d" % pp])
                def fst(eng, pp=pp):
                    eng.tensor_scalar(sb["tB"][:, 0:512],
                                      sb["tA"][:, pp * 512:pp * 512 + 512],
                                      12582912.0, None, Add)
                    return eng.tensor_scalar(sb["tB"][:, 0:512], sb["tB"][:, 0:512],
                                             -12582912.0, None, Add)
                pl.emit("vector", fst, reads=["tA%d" % pp], writes=["tB0"])
                def fci(eng):
                    return eng.tensor_copy(sb["tB"].ap().bitcast(I32)[:, 512:1024], sb["tB"][:, 0:512])
                pl.emit("vector", fci, reads=["tB0"], writes=["tB1"])
                def fcl(eng, qq=qq):
                    xqv = sb["kt"].ap().bitcast(BF16)
                    return eng.tensor_scalar(xqv[:, qq * 512:qq * 512 + 512],
                                             sb["tB"].ap().bitcast(I32)[:, 512:1024], -1.0, 1.0, MaxOp, MinOp)
                pl.emit("vector", fcl, reads=["tB1"], writes=["xqs%d" % qq])
            def fw(eng, s=s, n_cols=n_cols, wdst=wdst):
                xqv = sb["kt"].ap().bitcast(BF16)
                return eng.dma_start(out=wdst[s * 128:(s + 1) * 128, :],
                                     in_=xqv[:, 0:n_cols])
            pl.emit("gpsimd", fw,
                    reads=["xqs%d" % q for q in range(n_cols // 512)] + ["kt"])

    quant_weights(dr["w1T"], w1q, D, FF, "m127")
    quant_weights(dr["w2T"], w2q, FF, D, "m227")
    wq_done = pl.count["dG"]
    if dbg:
        def fwl0(eng):
            return eng.dma_start(out=sb["wbuf"].ap().bitcast(BF16)[:, 0:4096],
                                 in_=w1q[0:128, :])
        pl.emit("sync", fwl0, writes=["wbuf", "wbuf0", "wbuf1"],
                extra_waits=[("dG", wq_done)])
        def fn(eng):
            return eng.dma_start(out=dbg_aps["d_w1q"],
                                 in_=sb["wbuf"].ap().bitcast(BF16)[:, 0:4096])
        fn._dma_out = True
        sem = "dD"
        waits = {}
        r = pl._get("wbuf")
        if r["w"] is not None:
            sx, vx = r["w"]
            waits[sx] = vx
        pl.count[sem] += 16
        r["r"].append((sem, pl.count[sem]))
        pl.steps["scalar"].append((fn, sorted(waits.items())))

    def ln_to_zg(src_name):
        for c in range(8):
            pp = c % 2
            def fsq(eng, c=c, pp=pp):
                return eng.activation(sb["tA"][:, pp * 512:pp * 512 + 512],
                                      sb[src_name][:, c * TOK:c * TOK + 512], AF.Square)
            pl.emit("scalar", fsq, reads=[src_name], writes=["tA%d" % pp])
            def fm1(eng, c=c):
                return eng.matmul(sb["pb0"][0:1, 0:512], sb["ones"].ap(),
                                  sb[src_name][:, c * TOK:c * TOK + 512],
                                  start=(c == 0), stop=(c == 7))
            pl.emit("tensor", fm1, reads=[src_name, "ones"],
                    writes=["pb0"] if c in (0, 7) else [])
            def fm2(eng, c=c, pp=pp):
                return eng.matmul(sb["pb1"][0:1, 0:512], sb["ones"].ap(),
                                  sb["tA"][:, pp * 512:pp * 512 + 512],
                                  start=(c == 0), stop=(c == 7))
            pl.emit("tensor", fm2, reads=["tA%d" % pp, "ones"],
                    writes=["pb1"] if c in (0, 7) else [])
        def fv(eng):
            st = sb["stat"]
            eng.tensor_scalar(st[0:1, 0:512], sb["pb0"][0:1, 0:512], 1.0 / D, None, Mul)
            eng.tensor_tensor(sb["strow"][0:1, 0:512], st[0:1, 0:512], st[0:1, 0:512], Mul)
            eng.scalar_tensor_tensor(st[0:1, 512:1024], sb["pb1"][0:1, 0:512],
                                     1.0 / D, sb["strow"][0:1, 0:512], Mul, Sub)
            eng.tensor_scalar(st[0:1, 512:1024], st[0:1, 512:1024], EPS, None, Add)
            return eng.reciprocal(st[0:1, 512:1024], st[0:1, 512:1024])
        pl.emit("vector", fv, reads=["pb0", "pb1"], writes=["stat"])
        def fs(eng):
            return eng.activation(sb["stat"][0:1, 1024:1536], sb["stat"][0:1, 512:1024],
                                  AF.Sqrt)
        pl.emit("scalar", fs, reads=["stat"], writes=["stat"])
        def fmr(eng):
            return eng.tensor_tensor(sb["stat2"][0:1, 0:512], sb["stat"][0:1, 0:512],
                                     sb["stat"][0:1, 1024:1536], Mul)
        pl.emit("vector", fmr, reads=["stat"], writes=["stat", "stat2"])
        bcast_row("pb2", sb["stat"][0:1, 1024:1536], 512, 0, ["stat"])
        bcast_row("pb3", sb["stat2"][0:1, 0:512], 512, 0, ["stat"])
        for c in range(8):
            def fap(eng, c=c):
                eng.tensor_tensor(sb["tB"][:, 0:512], sb[src_name][:, c * TOK:c * TOK + 512],
                                  sb["pb2"][:, 0:512], Mul)
                return eng.scalar_tensor_tensor(sb["zg"][:, c * TOK:c * TOK + 512],
                                                sb["pb3"][:, 0:512], -1.0,
                                                sb["tB"][:, 0:512], Mul, Add)
            pl.emit("vector", fap, reads=[src_name, "pb2", "pb3"],
                    writes=["tB0", "zgc%d" % c])
        return ["zgc%d" % c for c in range(8)]

    def project(wt_name, fout, evac, bof=4):
        if fout == 512:
            for c in range(8):
                def fw(eng, c=c, wt_name=wt_name):
                    return eng.dma_start(out=sb["wbuf"][:, c * 512:(c + 1) * 512],
                                         in_=dr[wt_name][c * 128:(c + 1) * 128, :])
                pl.emit("sync", fw, writes=["wbuf", "wbuf0", "wbuf1"])
            for fi in range(4):
                bank = "pb%d" % (bof + fi % 2)
                for c in range(8):
                    def fm(eng, c=c, fi=fi, bank=bank):
                        return eng.matmul(
                            sb[bank][:, 0:512],
                            sb["wbuf"][:, c * 512 + fi * 128:c * 512 + fi * 128 + 128],
                            sb["zg"][:, c * TOK:c * TOK + 512],
                            start=(c == 0), stop=(c == 7))
                    pl.emit("tensor", fm, reads=["wbuf", "zgc%d" % c],
                            writes=[bank] if c in (0, 7) else [])
                evac(fi, bank)
        else:
            assert fout == 1024
            for half in range(2):
                for cc in range(4):
                    c = half * 4 + cc
                    def fw(eng, c=c, cc=cc, wt_name=wt_name):
                        return eng.dma_start(out=sb["wbuf"][:, cc * 1024:(cc + 1) * 1024],
                                             in_=dr[wt_name][c * 128:(c + 1) * 128, :])
                    pl.emit("sync", fw, writes=["wbuf", "wbuf0", "wbuf1"])
                for fi in range(8):
                    bank = "pb%d" % fi
                    for cc in range(4):
                        c = half * 4 + cc
                        def fm(eng, c=c, cc=cc, fi=fi, bank=bank, half=half):
                            return eng.matmul(
                                sb[bank][:, 0:512],
                                sb["wbuf"][:, cc * 1024 + fi * 128:cc * 1024 + fi * 128 + 128],
                                sb["zg"][:, c * TOK:c * TOK + 512],
                                start=(half == 0 and cc == 0), stop=(half == 1 and cc == 3))
                        pl.emit("tensor", fm, reads=["wbuf", "zgc%d" % c],
                                writes=[bank] if (half == 0 and cc == 0) or (half == 1 and cc == 3) else [])
            for fi in range(8):
                evac(fi, "pb%d" % fi)

    def dump(dname, src_name, cols, extra=()):
        if not dbg:
            return
        def fn(eng, dname=dname, src_name=src_name, cols=cols):
            return eng.dma_start(out=dbg_aps[dname], in_=sb[src_name][:, 0:cols])
        fn._dma_out = True
        sem = "dD"
        waits = {}
        pl.count[sem] += 16
        for nm in (src_name,) + tuple(extra):
            r = pl._get(nm)
            if r["w"] is not None:
                sx, vx = r["w"]
                waits[sx] = max(waits.get(sx, 0), vx)
            r["r"].append((sem, pl.count[sem]))
        pl.steps["scalar"].append((fn, sorted(waits.items())))

    if dbg:
        dump("d_m1", "m127", 1)
        dump("d_m2", "m227", 1)

    def group_body(g):
        t0 = g * TOK
        for c in range(8):
            def fx(eng, c=c, t0=t0):
                return eng.dma_start(out=sb["xg"][:, c * TOK:(c + 1) * TOK],
                                     in_=dr["xT"][c * 128:(c + 1) * 128, t0:t0 + TOK])
            pl.emit("sync", fx, writes=["xg"])

        ln_to_zg("xg")

        w = TOK + 4

        def halo_in(hal, nch):
            def fn(eng, hal=hal, nch=nch):
                a = sb["pre"].ap().rearrange("p (c w) -> p c w", w=w)
                hh = sb[hal].ap().rearrange("p (c f) -> p c f", f=4)
                return eng.tensor_copy(a[:, 0:nch, 0:4], hh[:, 0:nch, :])
            pl.emit("vector", fn, reads=[hal], writes=["pre"])

        def halo_out(hal, nch):
            def fsp(eng):
                return eng.tensor_copy(sb["shufM"][:, 0:512], sb["zg"][:, 0:512])
            pl.emit("vector", fsp, reads=["zgc0"], writes=["shufM"])
            def fn(eng, hal=hal, nch=nch):
                a = sb["pre"].ap().rearrange("p (c w) -> p c w", w=w)
                hh = sb[hal].ap().rearrange("p (c f) -> p c f", f=4)
                return eng.tensor_copy(hh[:, 0:nch, :], a[:, 0:nch, TOK:TOK + 4])
            pl.emit("vector", fn, reads=["pre"], writes=[hal])

        def ev_pre(fi, bank):
            def fe(eng, fi=fi, bank=bank):
                return eng.tensor_copy(sb["pre"][:, fi * w + 4:fi * w + 4 + 512],
                                       sb[bank][:, 0:512])
            pl.emit("vector", fe, reads=[bank], writes=["pre"])

        def conv(name_t, nch, wc, bc, scale=None):
            for c in range(nch):
                pp = c % 2
                def f0(eng, c=c, pp=pp, wc=wc):
                    return eng.mul(sb["tA"][:, pp * 512:pp * 512 + TOK],
                                   sb["pre"][:, c * w + 1:c * w + 1 + TOK],
                                   sb[wc][:, c * 4:c * 4 + 1])
                pl.emit("scalar", f0, reads=["pre", wc], writes=["tA%d" % pp])
                for j in (1, 2, 3):
                    def fj(eng, c=c, j=j, pp=pp, wc=wc):
                        return eng.scalar_tensor_tensor(
                            sb["tA"][:, pp * 512:pp * 512 + TOK],
                            sb["pre"][:, c * w + 1 + j:c * w + 1 + j + TOK],
                            sb[wc][:, c * 4 + j:c * 4 + j + 1],
                            sb["tA"][:, pp * 512:pp * 512 + TOK], Mul, Add)
                    pl.emit("vector", fj, reads=["pre", wc, "tA%d" % pp],
                            writes=["tA%d" % pp])
                def fs(eng, c=c, pp=pp, name_t=name_t, bc=bc):
                    return eng.activation(sb[name_t][:, c * TOK:(c + 1) * TOK],
                                          sb["tA"][:, pp * 512:pp * 512 + TOK],
                                          AF.Silu, bias=sb[bc][:, c:c + 1])
                pl.emit("scalar", fs, reads=["tA%d" % pp, bc], writes=[name_t])
                if scale is not None:
                    def fsc(eng, c=c, name_t=name_t, scale=scale):
                        return eng.tensor_scalar(sb[name_t][:, c * TOK:(c + 1) * TOK],
                                                 sb[name_t][:, c * TOK:(c + 1) * TOK],
                                                 scale, None, Mul)
                    pl.emit("vector", fsc, reads=[name_t], writes=[name_t])

        halo_in("qhal", 4)
        project("wqT", 512, ev_pre, bof=4)
        halo_out("qhal", 4)
        conv("qt", 4, "s_qcw_c", "s_qcb_c")

        dump("d_zg", "zg", 8 * TOK, extra=tuple("zgc%d" % c for c in range(8)))
        halo_in("khal", 4)
        project("wkT", 512, ev_pre, bof=6)
        halo_out("khal", 4)
        conv("kt", 4, "s_kcw_c", "s_kcb_c", scale=DK ** (-0.5))

        halo_in("vhal", 8)
        project("wvT", 1024, ev_pre)
        halo_out("vhal", 8)
        conv("vt", 8, "s_vcw_c", "s_vcb_c")

        dump("d_qt", "qt", 4 * TOK)
        dump("d_kt", "kt", 4 * TOK, extra=tuple("xqs%d" % q for q in range(8)))
        dump("d_vt", "vt", 8 * TOK)

        def ev_a(fi, bank):
            def fe(eng, fi=fi, bank=bank):
                return eng.activation(sb["at"][:, fi * TOK:fi * TOK + 512],
                                      sb[bank][:, 0:512], AF.Sigmoid,
                                      bias=sb["s_ba_c"][:, fi:fi + 1])
            pl.emit("scalar", fe, reads=[bank, "s_ba_c"], writes=["at"])
        project("waT", 1024, ev_a)

        def ev_g(fi, bank):
            def fe(eng, fi=fi, bank=bank):
                return eng.activation(sb["gt"][:, fi * TOK:fi * TOK + 512],
                                      sb[bank][:, 0:512], AF.Sigmoid)
            pl.emit("scalar", fe, reads=[bank], writes=["gt"])
        project("wgT", 1024, ev_g)

        def fwb(eng):
            src = bass.AP(dr["wbT"].tensor, 0, [[8, 128], [1024, 8], [1, 8]])
            return eng.dma_start(out=sb["wbuf"][:, 0:64], in_=src)
        pl.emit("sync", fwb, writes=["wbuf", "wbuf0", "wbuf1"])
        for c in range(8):
            def fm(eng, c=c):
                return eng.matmul(sb["pb6"][0:8, 0:512], sb["wbuf"][:, c * 8:c * 8 + 8],
                                  sb["zg"][:, c * TOK:c * TOK + 512],
                                  start=(c == 0), stop=(c == 7))
            pl.emit("tensor", fm, reads=["wbuf", "zgc%d" % c],
                    writes=["pb6"] if c in (0, 7) else [])
        def fb2(eng):
            return eng.activation(sb["bb8"][0:8, 0:TOK], sb["pb6"][0:8, 0:512],
                                  AF.Sigmoid, bias=sb["s_bb_c"][0:8, 0:1])
        pl.emit("scalar", fb2, reads=["pb6", "s_bb_c"], writes=["bb8"])
        for h in range(H):
            bank = "pb%d" % (6 + h % 2)
            def fbb(eng, h=h, bank=bank):
                return eng.matmul(sb[bank][:, 0:512],
                                  sb["sel8"][0:8, h * 128:(h + 1) * 128],
                                  sb["bb8"][0:8, 0:TOK], start=True, stop=True)
            pl.emit("tensor", fbb, reads=["sel8", "bb8"], writes=[bank])
            def fr1(eng, h=h, bank=bank):
                eng.reciprocal(sb["tA"][:, 0:512], sb["at"][:, h * TOK:h * TOK + 512])
                eng.tensor_tensor(sb["tA"][:, 512:1024], sb["tA"][:, 0:512],
                                  sb[bank][:, 0:512], Mul)
                return eng.tensor_scalar(sb["rng"][:, h * TOK:h * TOK + 512],
                                         sb["tA"][:, 512:1024], -1.0, None, Mul)
            pl.emit("vector", fr1, reads=[bank, "at"], writes=["tA0", "tA1", "rng"])

        dump("d_at", "at", 8 * TOK)
        dump("d_rng", "rng", 8 * TOK)
        if KPH <= 1:
            return
        def kslice(name, h, c0, n=64):
            return sb[name][64 * (h % 2):64 * (h % 2) + 64,
                            (h // 2) * TOK + c0:(h // 2) * TOK + c0 + n]

        for ch in range(min(NCH, NCHLIM)):
            c0 = ch * 64
            for p in range(4):
                bk = "pb%d" % p
                def ftr(eng, p=p, bk=bk):
                    return eng.transpose(sb[bk][:, 0:128],
                                         sb["sv"][:, p * 128:(p + 1) * 128],
                                         sb["s_ident"].ap())
                pl.emit("tensor", ftr, reads=["sv", "s_ident"], writes=[bk])
                def fev(eng, p=p, bk=bk):
                    return eng.tensor_copy(sb["sk"][:, p * 128:(p + 1) * 128],
                                           sb[bk][:, 0:128])
                pl.emit("vector", fev, reads=[bk], writes=["sk"])

            for h in range(H):
                kap = kslice("kt", h, c0)
                qap = kslice("qt", h, c0)
                bank = "pb%d" % (4 + h % 2)
                def fmm(eng, kap=kap, bank=bank):
                    return eng.matmul(sb[bank][0:64, 0:64], kap, kap, start=True, stop=True)
                pl.emit("tensor", fmm, reads=["kt"], writes=[bank])
                def fme(eng, h=h, bank=bank):
                    dst = bass.AP(sb["mup"], h, [[512, 64], [64, 8], [8, 8]])
                    src = bass.AP(sb[bank], 0, [[512, 64], [8, 8], [1, 8]])
                    msk = bass.AP(sb["s_maskU"], h * 64, [[512, 64], [8, 8], [1, 8]])
                    return eng.tensor_tensor(dst, src, msk, Mul)
                pl.emit("vector", fme, reads=[bank, "s_maskU"], writes=["mup"])
                def fmeh(eng, h=h, bank=bank):
                    hh = h % 2
                    dst = sb["mupH"][64 * hh:64 * hh + 64,
                                     (h // 2) * 128 + 64 * hh:(h // 2) * 128 + 64 * hh + 64]
                    return eng.tensor_tensor(dst, sb[bank][0:64, 0:64],
                                             sb["s_maskU"][:, h * 64:(h + 1) * 64], Mul)
                pl.emit("vector", fmeh, reads=[bank, "s_maskU"], writes=["mupH"])
                bank2 = "pb%d" % (6 + h % 2)
                def fmq(eng, kap=kap, qap=qap, bank2=bank2):
                    return eng.matmul(sb[bank2][0:64, 0:64], kap, qap, start=True, stop=True)
                pl.emit("tensor", fmq, reads=["kt", "qt"], writes=[bank2])
                def fqe(eng, h=h, bank2=bank2):
                    eng.tensor_tensor(sb["mqn"][0:64, h * 64:(h + 1) * 64],
                                      sb[bank2][0:64, 0:64],
                                      sb["s_maskQn"][:, h * 64:(h + 1) * 64], Mul)
                    return eng.tensor_copy(sb["mqn"][64:128, h * 64:(h + 1) * 64],
                                           sb["mqn"][0:64, h * 64:(h + 1) * 64])
                pl.emit("vector", fqe, reads=[bank2, "s_maskQn"], writes=["mqn"])

            for p in range(4):
                def fks(eng, p=p, c0=c0):
                    eng.tensor_copy(sb["kstage"][0:64, 0:64],
                                    sb["kt"][0:64, p * TOK + c0:p * TOK + c0 + 64])
                    return eng.tensor_copy(sb["kstage"][64:128, 64:128],
                                           sb["kt"][64:128, p * TOK + c0:p * TOK + c0 + 64])
                pl.emit("vector", fks, reads=["kt"], writes=["kstage"])
                bk = "pb%d" % (4 + p % 2)
                def ftr2(eng, bk=bk):
                    return eng.transpose(sb[bk][:, 0:128], sb["kstage"].ap(),
                                         sb["s_ident"].ap())
                pl.emit("tensor", ftr2, reads=["kstage", "s_ident"], writes=[bk])
                def fke(eng, p=p, bk=bk):
                    return eng.tensor_copy(sb[f"kthBD{p}"].ap(), sb[bk][:, 0:128])
                pl.emit("vector", fke, reads=[bk], writes=[f"kthBD{p}"])

            for h in range(H):
                kap = kslice("kt", h, c0)
                skap = sb["sk"][64 * (h % 2):64 * (h % 2) + 64,
                                (h // 2) * 128:(h // 2) * 128 + 128]
                bank = "pb%d" % (h % 4)
                def fp(eng, skap=skap, kap=kap, bank=bank):
                    return eng.matmul(sb[bank][:, 0:64], skap, kap, start=True, stop=True)
                pl.emit("tensor", fp, reads=["sk", "kt"], writes=[bank])
                def fpe(eng, h=h, bank=bank):
                    return eng.tensor_copy(sb["pva"][:, h * 64:(h + 1) * 64],
                                           sb[bank][:, 0:64])
                pl.emit("vector", fpe, reads=[bank], writes=["pva"])

            def hs3(name, cc0):
                return sb[name].ap().rearrange("p (h t) -> p h t", t=TOK)[:, :, cc0:cc0 + 64]

            def fcp(eng, c0=c0):
                cp = sb["cpr"].ap().rearrange("p (h t) -> p h t", t=64)
                av = hs3("at", c0)
                last = None
                for h in range(H):
                    last = eng.tensor_tensor_scan(cp[:, h, :], av[:, h, :],
                                                  sb["onesrow"][:, 0:64], 1.0, Mul, Mul)
                return last
            pl.emit("vector", fcp, reads=["at", "onesrow"], writes=["cpr"])
            def fgn(eng, c0=c0):
                g3 = sb["gng"].ap().rearrange("p (h t) -> p h t", t=64)
                t3 = sb["tA"].ap().rearrange("p (h t) -> p h t", t=128)[:, :, 0:64]
                p3 = sb["pva"].ap().rearrange("p (h t) -> p h t", t=64)
                eng.reciprocal(sb["tB"][:, 0:512], sb["cpr"][:, 0:512])
                cr = sb["tB"][:, 0:512].rearrange("p (h t) -> p h t", t=64)
                eng.tensor_tensor(t3, hs3("at", c0), cr, Mul)
                eng.tensor_tensor(t3, t3, hs3("vt", c0), Mul)
                eng.tensor_tensor(t3, p3, t3, Sub)
                return eng.tensor_tensor(g3, hs3("rng", c0), t3, Mul)
            pl.emit("vector", fgn, reads=["cpr", "at", "vt", "rng", "pva"],
                    writes=["gng", "tA0", "tA1", "tB0", "tB1"])

            def fwz(eng):
                return eng.memset(sb["wv"].ap(), 0.0)
            pl.emit("vector", fwz, writes=["wv"])
            wv3 = sb["wv"].ap().rearrange("p (h t) -> p h t", t=64)
            rng3 = hs3("rng", c0)
            gng3 = sb["gng"].ap().rearrange("p (h t) -> p h t", t=64)
            mupr = sb["mup"].ap().rearrange("s (h t) -> s h t", t=64)
            for j in range(8):
                jt = j * 8
                if SKMB:
                    continue
                q32 = (jt // 32) * 32
                r8 = jt - q32
                if not SKMB2:
                    def fshf(eng, r8=r8):
                        mask = [(r8 + i) % 32 for i in range(32)]
                        return eng.stream_shuffle(sb["shufM"][:, 0:512],
                                                  sb["mup"][:, 0:512], mask)
                    pl.emit("vector", fshf, reads=["mup"], writes=["shufM"])
                for sp in range(7 if not SKMB2 else 0):
                    ncol = 56 - 8 * sp
                    def fmb(eng, sp=sp, ncol=ncol, jt=jt, q32=q32):
                        return eng.matmul(sb["pb6"][:, sp * 64:sp * 64 + ncol],
                                          sb["sel8"][q32:q32 + 8, sp * 128:sp * 128 + 128],
                                          sb["shufM"][q32:q32 + 8,
                                                      (jt // 8) * 64 + 8 * (sp + 1):
                                                      (jt // 8) * 64 + 64],
                                          start=True, stop=True)
                    pl.emit("tensor", fmb, reads=["sel8", "shufM"],
                            writes=["pb6"])
                if not SKMB2:
                    def ffold(eng, jt=jt, c0=c0):
                        last = None
                        for sp in range(7):
                            n_t = 7 - sp
                            mbs = bass.AP(sb["pb6"], sp * 64, [[512, 128], [8, n_t], [1, 8]])
                            rsl = bass.AP(sb["rng"], c0 + jt + sp + 1,
                                          [[8 * TOK, 128], [1, n_t], [TOK, 8]])
                            dst = bass.AP(sb["pva"], sp * 64, [[512, 128], [8, n_t], [1, 8]])
                            last = eng.tensor_tensor(dst, mbs, rsl, Mul)
                        return last
                    pl.emit("vector", ffold, reads=["pb6", "rng"], writes=["mball"])
                if SKSOL:
                    continue
                if j == 0 or SKACC:
                    def fi0(eng, jt=jt):
                        return eng.tensor_scalar(wv3[:, :, jt:jt + 8], gng3[:, :, jt:jt + 8],
                                                 -1.0, None, Mul)
                    pl.emit("vector", fi0, reads=["gng"], writes=["wv"])
                    def fi0s(eng):
                        return eng.tensor_copy(sb["shufM"][:, 0:512], sb["mup"][:, 0:512])
                    pl.emit("vector", fi0s, reads=["mup"], writes=["shufM"])
                elif FI1ALT:
                    def fi1a(eng, jt=jt, c0=c0):
                        last = None
                        for h in range(8):
                            p = h // 2
                            hh = h % 2
                            accs2 = sb["pb%d" % (2 + p)][:, 64 * hh + jt:64 * hh + jt + 8]
                            rngs2 = sb["rng"][:, h * TOK + c0 + jt:h * TOK + c0 + jt + 8]
                            t2 = sb["tA"][:, h * 8:h * 8 + 8]
                            eng.tensor_tensor(t2, rngs2, accs2, Mul)
                        for h in range(8):
                            gngs2 = sb["gng"][:, h * 64 + jt:h * 64 + jt + 8]
                            wvs2 = sb["wv"][:, h * 64 + jt:h * 64 + jt + 8]
                            t2 = sb["tA"][:, h * 8:h * 8 + 8]
                            last = eng.tensor_tensor(wvs2, t2, gngs2, Sub)
                        return last
                    pl.emit("vector", fi1a,
                            reads=["rng", "pb2", "pb3", "pb4", "pb5", "gng"],
                            writes=["wv", "tA0"])
                    if dbg and ch == 0 and j == 1:
                        dump("d_wvj1", "wv", 512)
                else:
                    def fi1(eng, jt=jt, c0=c0):
                        last = None
                        for p in range(4):
                            accb = sb["pb%d" % (2 + p)]
                            accs = bass.AP(accb, jt, [[512, 128], [64, 2], [1, 8]])
                            rngs = bass.AP(sb["rng"], 2 * p * TOK + c0 + jt,
                                           [[8 * TOK, 128], [TOK, 2], [1, 8]])
                            gngs = bass.AP(sb["gng"], 2 * p * 64 + jt,
                                           [[512, 128], [64, 2], [1, 8]])
                            wvs = bass.AP(sb["wv"], p * 128 + jt,
                                          [[512, 128], [64, 2], [1, 8]])
                            t3 = bass.AP(sb["tA"], p * 16, [[1024, 128], [8, 2], [1, 8]])
                            eng.tensor_tensor(t3, rngs, accs, Mul)
                            last = eng.tensor_tensor(wvs, t3, gngs, Sub)
                        return last
                    pl.emit("vector", fi1,
                            reads=["rng", "pb2", "pb3", "pb4", "pb5", "gng"],
                            writes=["wv", "tA0", "tA1"])
                for sp in range(7 if not SKUPD else 0):
                    n_t = 7 - sp
                    if FUPALT:
                        def fupa(eng, jt=jt, sp=sp, n_t=n_t):
                            last = None
                            for h in range(8):
                                mrow_h = bass.AP(sb["pva"], sp * 64 + h,
                                                 [[512, 128], [8, n_t]])
                                wcol = sb["wv"][:, 64 * h + jt + sp:64 * h + jt + sp + 1]
                                tgt_h = sb["wv"][:, 64 * h + jt + sp + 1:
                                                 64 * h + jt + sp + 1 + n_t]
                                last = eng.scalar_tensor_tensor(
                                    tgt_h, mrow_h, wcol, tgt_h, Mul, Add)
                            return last
                        pl.emit("vector", fupa, reads=["pva", "wv", "mball"],
                                writes=["wv"])
                        continue
                    def fup(eng, jt=jt, sp=sp, n_t=n_t):
                        mrow = bass.AP(sb["pva"], sp * 64, [[512, 128], [8, n_t], [1, 8]])
                        wrep = bass.AP(sb["wv"], jt + sp, [[512, 128], [0, n_t], [64, 8]])
                        t3 = bass.AP(sb["tB"], 0, [[1024, 128], [8, n_t], [1, 8]])
                        tgt = bass.AP(sb["wv"], jt + sp + 1, [[512, 128], [1, n_t], [64, 8]])
                        eng.tensor_tensor(t3, mrow, wrep, Mul)
                        return eng.tensor_tensor(tgt, tgt, t3, Add)
                    pl.emit("vector", fup, reads=["pva", "wv"],
                            writes=["wv", "tB0", "tB1"])
                if j < 7 and not SKBND:
                    for p in range(4):
                        bank7 = ["pb0", "pb1", "pb6", "pb7"][p]
                        accb = "pb%d" % (2 + p)
                        def ftbh(eng, p=p, bank7=bank7):
                            return eng.transpose(sb[bank7][:, 0:128],
                                                 sb["wv"][:, p * 128:(p + 1) * 128],
                                                 sb["s_ident"].ap())
                        pl.emit("tensor", ftbh, reads=["wv", "s_ident"], writes=[bank7])
                        def dump_psum(dname, bank):
                            def fcpy(eng, bank=bank):
                                return eng.tensor_copy(sb["deqb"][:, 0:128],
                                                       sb[bank][:, 0:128])
                            pl.emit("vector", fcpy, reads=[bank], writes=["deqb"])
                            dump(dname, "deqb", 128)
                        if dbg and ch == 0 and j == 0 and p == 3:
                            dump_psum("d_pb7t", bank7)
                            dump("d_wvj0", "wv", 512)
                        def fteh(eng, bank7=bank7):
                            return eng.tensor_copy(sb["wtbP"].ap(), sb[bank7][:, 0:128])
                        pl.emit("vector", fteh, reads=[bank7], writes=["wtbP"])
                        if dbg and ch == 0 and j == 0 and p == 0:
                            dump("d_wtb0", "wtbP", 128)
                        if dbg and ch == 0 and j == 0 and p == 3:
                            dump("d_wtb3", "wtbP", 128)
                        if SKFAC:
                            continue
                        def fac(eng, p=p, accb=accb):
                            return eng.matmul(sb[accb][:, 0:128], sb["wtbP"].ap(),
                                              sb["mupH"][:, p * 128:(p + 1) * 128],
                                              start=True, stop=True)
                        pl.emit("tensor", fac, reads=["wtbP", "mupH"], writes=[accb])
                        if dbg and ch == 0 and j == 0 and p == 0:
                            dump_psum("d_acc2", accb)
                        if dbg and ch == 0 and j == 0 and p == 3:
                            dump_psum("d_acc5", accb)
                        if dbg and ch == 0 and j == 1 and p == 3:
                            dump("d_wvj1b", "wv", 512)

            if SKOST:
                continue
            for p in range(4):
                bk = ["pb2", "pb3", "pb6", "pb7"][p]
                def ftw(eng, p=p, bk=bk):
                    return eng.transpose(sb[bk][:, 0:128],
                                         sb["wv"][:, p * 128:(p + 1) * 128],
                                         sb["s_ident"].ap())
                pl.emit("tensor", ftw, reads=["wv", "s_ident"], writes=[bk])
                def fwe(eng, p=p, bk=bk):
                    return eng.tensor_copy(sb[f"wthP{p}"].ap(), sb[bk][:, 0:128])
                pl.emit("vector", fwe, reads=[bk], writes=[f"wthP{p}"])

            sbanks = ["pb6", "pb7", "pb0", "pb1"]

            def emit_o(h):
                qap = kslice("qt", h, c0)
                skap = sb["sk"][64 * (h % 2):64 * (h % 2) + 64,
                                (h // 2) * 128:(h // 2) * 128 + 128]
                bank = "pb%d" % (2 + h % 4)
                def fo1(eng, skap=skap, qap=qap, bank=bank):
                    return eng.matmul(sb[bank][:, 0:64], skap, qap, start=True, stop=False)
                pl.emit("tensor", fo1, reads=["sk", "qt"], writes=[bank])
                def fo2(eng, h=h, bank=bank):
                    wth = sb["wthP%d" % (h // 2)][64 * (h % 2):64 * (h % 2) + 64, :]
                    b0 = 64 * (h % 2)
                    return eng.matmul(sb[bank][:, 0:64], wth,
                                      sb["mqn"][b0:b0 + 64, h * 64:(h + 1) * 64],
                                      start=False, stop=True)
                pl.emit("tensor", fo2, reads=["wthP%d" % (h // 2), "mqn"], writes=[bank])
                def foe(eng, h=h, bank=bank, c0=c0):
                    cp3 = sb["cpr"].ap().rearrange("p (h t) -> p h t", t=64)
                    av = sb["at"].ap().rearrange("p (h t) -> p h t", t=TOK)
                    return eng.tensor_tensor(av[:, h, c0:c0 + 64], sb[bank][:, 0:64],
                                             cp3[:, h, :], Mul)
                pl.emit("vector", foe, reads=[bank, "cpr"], writes=["at"])
                bank2 = sbanks[h % 4]
                def fs1(eng, h=h, bank2=bank2):
                    wth = sb["wthP%d" % (h // 2)][64 * (h % 2):64 * (h % 2) + 64, :]
                    kbd = sb["kthBD%d" % (h // 2)][64 * (h % 2):64 * (h % 2) + 64,
                                                   64 * (h % 2):64 * (h % 2) + 64]
                    return eng.matmul(sb[bank2][:, 0:64], wth, kbd, start=True, stop=True)
                pl.emit("tensor", fs1, reads=["wthP%d" % (h // 2), "kthBD%d" % (h // 2)],
                        writes=[bank2])

            def emit_phA(h):
                bank2 = sbanks[h % 4]
                def fsa(eng, h=h, bank2=bank2):
                    return eng.tensor_scalar(sb["tA"][:, h * 64:h * 64 + 64],
                                             sb[bank2][:, 0:64],
                                             sb["cpr"][:, h * 64 + 63:h * 64 + 64],
                                             None, Mul)
                pl.emit("vector", fsa, reads=[bank2, "cpr"], writes=["tA0"])

            def emit_phB(h):
                def fsb(eng, h=h):
                    return eng.scalar_tensor_tensor(
                        sb["sv"][:, h * 64:h * 64 + 64],
                        sb["sv"][:, h * 64:h * 64 + 64],
                        sb["cpr"][:, h * 64 + 63:h * 64 + 64],
                        sb["tA"][:, h * 64:h * 64 + 64], Mul, Sub)
                pl.emit("vector", fsb, reads=["sv", "cpr", "tA0"], writes=["sv"])

            for h in range(4):
                emit_o(h)
            for h in range(4):
                emit_phA(h)
            for h in range(4, 8):
                emit_o(h)
            for h in range(4, 8):
                emit_phA(h)
            for h in range(8):
                emit_phB(h)

        dump("d_o", "at", 8 * TOK)
        dump("d_sv", "sv", 512)
        dump("d_cpr", "cpr", 512)
        dump("d_gng", "gng", 512)
        dump("d_wv", "wv", 512)
        dump("d_pva", "pva", 512)
        dump("d_mqn", "mqn", 512)
        dump("d_mupH", "mupH", 512)
        if KPH <= 2:
            return
        for h in range(H):
            pp = h % 2
            def fsq2(eng, h=h, pp=pp):
                return eng.activation(sb["tA"][:, pp * 512:pp * 512 + 512],
                                      sb["at"][:, h * TOK:h * TOK + 512], AF.Square)
            pl.emit("scalar", fsq2, reads=["at"], writes=["tA%d" % pp])
            def fgm1(eng, h=h):
                return eng.matmul(sb["pb0"][0:1, 0:512], sb["ones"].ap(),
                                  sb["at"][:, h * TOK:h * TOK + 512], start=True, stop=True)
            pl.emit("tensor", fgm1, reads=["at", "ones"], writes=["pb0"])
            def fgm2(eng, pp=pp):
                return eng.matmul(sb["pb1"][0:1, 0:512], sb["ones"].ap(),
                                  sb["tA"][:, pp * 512:pp * 512 + 512], start=True, stop=True)
            pl.emit("tensor", fgm2, reads=["tA%d" % pp, "ones"], writes=["pb1"])
            def fgr(eng):
                st = sb["stat"]
                eng.tensor_scalar(st[0:1, 0:512], sb["pb0"][0:1, 0:512], 1.0 / DV, None, Mul)
                eng.tensor_tensor(sb["strow"][0:1, 0:512], st[0:1, 0:512], st[0:1, 0:512], Mul)
                eng.scalar_tensor_tensor(st[0:1, 512:1024], sb["pb1"][0:1, 0:512],
                                         1.0 / DV, sb["strow"][0:1, 0:512], Mul, Sub)
                eng.tensor_scalar(st[0:1, 512:1024], st[0:1, 512:1024], EPS, None, Add)
                return eng.reciprocal(st[0:1, 512:1024], st[0:1, 512:1024])
            pl.emit("vector", fgr, reads=["pb0", "pb1"], writes=["stat"])
            def fgs(eng):
                return eng.activation(sb["stat"][0:1, 1024:1536], sb["stat"][0:1, 512:1024],
                                      AF.Sqrt)
            pl.emit("scalar", fgs, reads=["stat"], writes=["stat"])
            bcast_row("pb2", sb["stat"][0:1, 0:512], 512, 0, ["stat"])
            bcast_row("pb3", sb["stat"][0:1, 1024:1536], 512, 0, ["stat"])
            def fga(eng, h=h):
                eng.tensor_tensor(sb["tB"][:, 0:512], sb["at"][:, h * TOK:h * TOK + 512],
                                  sb["pb2"][:, 0:512], Sub)
                eng.tensor_tensor(sb["tB"][:, 0:512], sb["tB"][:, 0:512],
                                  sb["pb3"][:, 0:512], Mul)
                eng.tensor_scalar(sb["tB"][:, 0:512], sb["tB"][:, 0:512],
                                  sb["s_lnw_c"].ap(), sb["s_lnb_c"].ap(), Mul, Add)
                return eng.tensor_tensor(sb["gt"][:, h * TOK:h * TOK + 512],
                                         sb["tB"][:, 0:512],
                                         sb["gt"][:, h * TOK:h * TOK + 512], Mul)
            pl.emit("vector", fga, reads=["at", "pb2", "pb3", "s_lnw_c", "s_lnb_c", "gt"],
                    writes=["tB0", "gt"])

        dump("d_gt", "gt", 8 * TOK)
        for half in range(2):
            for cc in range(4):
                c = half * 4 + cc
                def fw(eng, c=c, cc=cc):
                    return eng.dma_start(out=sb["wbuf"][:, cc * 1024:(cc + 1) * 1024],
                                         in_=dr["woT"][c * 128:(c + 1) * 128, :])
                pl.emit("sync", fw, writes=["wbuf", "wbuf0", "wbuf1"])
            for fi in range(8):
                bank = "pb%d" % fi
                for cc in range(4):
                    c = half * 4 + cc
                    def fm(eng, c=c, cc=cc, fi=fi, bank=bank, half=half):
                        return eng.matmul(
                            sb[bank][:, 0:512],
                            sb["wbuf"][:, cc * 1024 + fi * 128:cc * 1024 + fi * 128 + 128],
                            sb["gt"][:, c * TOK:c * TOK + 512],
                            start=(half == 0 and cc == 0), stop=(half == 1 and cc == 3))
                    pl.emit("tensor", fm, reads=["wbuf", "gt"],
                            writes=[bank] if (half == 0 and cc == 0)
                            or (half == 1 and cc == 3) else [])
        for fi in range(8):
            bank = "pb%d" % fi
            def fres(eng, fi=fi, bank=bank):
                return eng.tensor_tensor(sb["xg"][:, fi * TOK:fi * TOK + 512],
                                         sb["xg"][:, fi * TOK:fi * TOK + 512],
                                         sb[bank][:, 0:512], Add)
            pl.emit("vector", fres, reads=[bank, "xg"], writes=["xg"])

        ln_to_zg("xg")

        def xqv():
            return sb["kt"].ap().bitcast(BF16)

        def ffqv():
            return sb["qt"].ap().bitcast(BF16)

        def part_max(eng, t, u, n, dst):
            eng.tensor_copy(u[0:64, 0:n], t[64:128, 0:n])
            eng.tensor_tensor(t[0:64, 0:n], t[0:64, 0:n], u[0:64, 0:n], MaxOp)
            eng.tensor_copy(u[0:32, 0:n], t[32:64, 0:n])
            eng.tensor_tensor(t[0:32, 0:n], t[0:32, 0:n], u[0:32, 0:n], MaxOp)
            for k in (16, 8, 4, 2, 1):
                mask = [(i + k) % 32 for i in range(32)]
                eng.stream_shuffle(u[:, 0:n], t[:, 0:n], mask)
                eng.tensor_tensor(t[0:32, 0:n], t[0:32, 0:n], u[0:32, 0:n], MaxOp)
            return eng.tensor_copy(dst, t[0:1, 0:n])

        dump("d_z2", "zg", 8 * TOK, extra=tuple("zgc%d" % c for c in range(8)))
        if KPH <= 3:
            return
        for c in range(8):
            pp = c % 2
            def fsq3(eng, c=c, pp=pp):
                return eng.activation(sb["tB"][:, pp * 512:pp * 512 + 512],
                                      sb["zg"][:, c * TOK:c * TOK + 512], AF.Square)
            pl.emit("scalar", fsq3, reads=["zgc%d" % c], writes=["tB%d" % pp])
            def fm3(eng, c=c, pp=pp):
                return eng.matmul(sb["pb1"][0:1, 0:512], sb["ones"].ap(),
                                  sb["tB"][:, pp * 512:pp * 512 + 512],
                                  start=(c == 0), stop=(c == 7))
            pl.emit("tensor", fm3, reads=["tB%d" % pp, "ones"],
                    writes=["pb1"] if c in (0, 7) else [])
            if c == 0:
                def fmx(eng, pp=pp):
                    return eng.tensor_copy(sb["tA"][:, 0:512],
                                           sb["tB"][:, pp * 512:pp * 512 + 512])
            else:
                def fmx(eng, pp=pp):
                    return eng.tensor_tensor(sb["tA"][:, 0:512], sb["tA"][:, 0:512],
                                             sb["tB"][:, pp * 512:pp * 512 + 512], MaxOp)
            pl.emit("vector", fmx, reads=["tB%d" % pp, "tA0"], writes=["tA0"])
        def ft1(eng):
            part_max(eng, sb["tA"], sb["tB"], 512, sb["stat"][0:1, 0:512])
            return eng.activation(sb["stat"][0:1, 0:512], sb["stat"][0:1, 0:512],
                                  AF.Sqrt) if False else eng.tensor_copy(
                sb["stat"][0:1, 0:512], sb["stat"][0:1, 0:512])
        pl.emit("vector", ft1, reads=["tA0"],
                writes=["tA0", "tA1", "tB0", "tB1", "stat"])
        def ft1s(eng):
            return eng.activation(sb["stat"][0:1, 0:512], sb["stat"][0:1, 0:512], AF.Sqrt)
        pl.emit("scalar", ft1s, reads=["stat"], writes=["stat"])
        def fqv(eng):
            st = sb["stat"]
            eng.tensor_scalar(sb["strow"][0:1, 0:512], sb["pb1"][0:1, 0:512], 1.0 / D, EPS,
                              Mul, Add)
            return eng.reciprocal(st[0:1, 512:1024], sb["strow"][0:1, 0:512])
        pl.emit("vector", fqv, reads=["pb1"], writes=["stat"])
        def fqs(eng):
            return eng.activation(sb["stat"][0:1, 512:1024], sb["stat"][0:1, 512:1024],
                                  AF.Sqrt)
        pl.emit("scalar", fqs, reads=["stat"], writes=["stat"])
        def fq2(eng):
            st = sb["stat"]
            eng.tensor_tensor(st[0:1, 1024:1536], st[0:1, 512:1024], st[0:1, 0:512], Mul)
            eng.tensor_scalar(st[0:1, 1024:1536], st[0:1, 1024:1536], 1e-5, None, MaxOp)
            eng.reciprocal(sb["stat2"][0:1, 0:512], st[0:1, 1024:1536])
            eng.tensor_tensor(sb["stat2"][0:1, 0:512], sb["stat2"][0:1, 0:512], st[0:1, 512:1024], Mul)
            eng.tensor_scalar(sb["stat2"][0:1, 0:512], sb["stat2"][0:1, 0:512], 127.0, None, Mul)
            return eng.tensor_scalar(sb["strow"][0:1, 0:512], st[0:1, 1024:1536],
                                     sb["m127"][0:1, 0:1], None, Mul)
        pl.emit("vector", fq2, reads=["stat", "m127"], writes=["stat", "stat2", "strow"])
        bcast_row("pb2", sb["stat2"][0:1, 0:512], 512, 0, ["stat"])
        bcast_row("pb3", sb["strow"][0:1, 0:512], 512, 0, ["strow"])
        def fdq(eng):
            return eng.tensor_copy(sb["deqb"][:, 0:512], sb["pb3"][:, 0:512])
        pl.emit("vector", fdq, reads=["pb3"], writes=["deqb"])
        for c in range(8):
            pp = c % 2
            def fz1(eng, c=c, pp=pp):
                return eng.tensor_tensor(sb["tA"][:, pp * 512:pp * 512 + 512],
                                         sb["zg"][:, c * TOK:c * TOK + 512],
                                         sb["pb2"][:, 0:512], Mul)
            pl.emit("vector", fz1, reads=["zgc%d" % c, "pb2"], writes=["tA%d" % pp])
            def fz3(eng, pp=pp):
                eng.tensor_scalar(sb["tB"][:, 0:512],
                                  sb["tA"][:, pp * 512:pp * 512 + 512],
                                  12582912.0, None, Add)
                return eng.tensor_scalar(sb["tB"][:, 0:512], sb["tB"][:, 0:512],
                                         -12582912.0, None, Add)
            pl.emit("vector", fz3, reads=["tA%d" % pp], writes=["tB0"])
            def fz4(eng):
                return eng.tensor_copy(sb["tB"].ap().bitcast(I32)[:, 512:1024], sb["tB"][:, 0:512])
            pl.emit("vector", fz4, reads=["tB0"], writes=["tB1"])
            def fz5(eng, c=c):
                return eng.tensor_scalar(xqv()[:, c * TOK:c * TOK + 512],
                                         sb["tB"].ap().bitcast(I32)[:, 512:1024], -128.0, 127.0, MaxOp, MinOp)
            pl.emit("vector", fz5, reads=["tB1"],
                    writes=["kt"] + (["xqs%d" % q for q in range(8)] if c == 0 else []))
        dump("d_xq", "kt", 2048)
        dump("d_deq", "deqb", 512, extra=("deqb2",))

        for qtr in range(TOK // 128):
            ht0 = qtr * 128
            for fs in range(8):
                def fwl(eng, fs=fs):
                    src = bass.AP(w1q.tensor, fs * 512,
                                  [[4096, 128], [524288, 8], [1, 512]])
                    return eng.dma_start(
                        out=sb["wbuf"].ap().bitcast(BF16)[:, 0:4096], in_=src)
                pl.emit("sync", fwl, writes=["wbuf", "wbuf0", "wbuf1"],
                        extra_waits=[("dG", wq_done)])
                for fi in range(4):
                    bank = "pb%d" % (4 + fi % 2)
                    for c in range(8):
                        def fm(eng, c=c, fi=fi, bank=bank, ht0=ht0):
                            wb = sb["wbuf"].ap().bitcast(BF16)
                            return eng.matmul(
                                sb[bank][:, 0:128],
                                wb[:, c * 512 + fi * 128:c * 512 + fi * 128 + 128],
                                xqv()[:, c * TOK + ht0:c * TOK + ht0 + 128],
                                start=(c == 0), stop=(c == 7))
                        pl.emit("tensor", fm, reads=["wbuf", "kt"],
                                writes=[bank] if c in (0, 7) else [])
                    fidx = fs * 4 + fi
                    def fev(eng, fidx=fidx, bank=bank, ht0=ht0):
                        return eng.tensor_tensor(sb["ffh"][:, fidx * 128:(fidx + 1) * 128],
                                                 sb[bank][:, 0:128],
                                                 sb["deqb"][:, ht0:ht0 + 128], Mul)
                    pl.emit("vector", fev, reads=[bank, "deqb"], writes=["vt"])
                    def fge(eng, fidx=fidx):
                        return eng.activation(sb["ffh"][:, fidx * 128:(fidx + 1) * 128],
                                              sb["ffh"][:, fidx * 128:(fidx + 1) * 128],
                                              AF.Gelu, bias=sb["s_b1_c"][:, fidx:fidx + 1])
                    pl.emit("scalar", fge, reads=["vt", "s_b1_c"], writes=["vt"])

            if qtr == 0:
                dump("d_ffh", "vt", 4096)
            for cg in range(8):
                for ci in range(4):
                    c = cg * 4 + ci
                    def fsq4(eng, c=c, ci=ci):
                        return eng.activation(sb["tB"][:, ci * 128:ci * 128 + 128],
                                              sb["ffh"][:, c * 128:(c + 1) * 128], AF.Square)
                    pl.emit("scalar", fsq4, reads=["vt"], writes=["tB0"])
                    def fm4(eng, c=c, ci=ci):
                        return eng.matmul(sb["pb1"][0:1, 0:128], sb["ones"].ap(),
                                          sb["tB"][:, ci * 128:ci * 128 + 128],
                                          start=(c == 0), stop=(c == 31))
                    pl.emit("tensor", fm4, reads=["tB0", "ones"],
                            writes=["pb1"] if c in (0, 31) else [])
                if cg == 0:
                    def fmx2(eng):
                        return eng.tensor_copy(sb["tA"][:, 0:512], sb["tB"][:, 0:512])
                else:
                    def fmx2(eng):
                        return eng.tensor_tensor(sb["tA"][:, 0:512], sb["tA"][:, 0:512],
                                                 sb["tB"][:, 0:512], MaxOp)
                pl.emit("vector", fmx2, reads=["tB0", "tA0"], writes=["tA0"])
            def fht(eng):
                return part_max(eng, sb["tA"], sb["tB"], 512, sb["stat2"][0:1, 0:512])
            pl.emit("vector", fht, reads=["tA0"],
                    writes=["tA0", "tA1", "tB0", "tB1", "stat2"])
            def fspc1(eng):
                return eng.tensor_copy(sb["shufM"][:, 0:512], sb["rng"][:, 0:512])
            pl.emit("vector", fspc1, reads=["rng"], writes=["shufM"])
            def fr1v(eng):
                return eng.tensor_tensor(sb["strow"][0:1, 256:512],
                                         sb["stat2"][0:1, 0:256],
                                         sb["stat2"][0:1, 256:512], MaxOp)
            pl.emit("vector", fr1v, reads=["stat2"], writes=["strow"])
            def fspc2(eng):
                return eng.tensor_copy(sb["shufM"][:, 0:512], sb["rng"][:, 0:512])
            pl.emit("vector", fspc2, reads=["rng"], writes=["shufM"])
            def fr2v(eng):
                return eng.tensor_tensor(sb["stat"][0:1, 0:128],
                                         sb["strow"][0:1, 256:384],
                                         sb["strow"][0:1, 384:512], MaxOp)
            pl.emit("vector", fr2v, reads=["strow"], writes=["stat"])
            def fhts(eng):
                return eng.activation(sb["stat"][0:1, 0:128], sb["stat"][0:1, 0:128], AF.Sqrt)
            pl.emit("scalar", fhts, reads=["stat"], writes=["stat"])
            def fqv2(eng):
                st = sb["stat"]
                eng.tensor_scalar(sb["strow"][0:1, 0:128], sb["pb1"][0:1, 0:128], 1.0 / FF, EPS,
                                  Mul, Add)
                return eng.reciprocal(st[0:1, 512:640], sb["strow"][0:1, 0:128])
            pl.emit("vector", fqv2, reads=["pb1"], writes=["stat"])
            def fqs2(eng):
                return eng.activation(sb["stat"][0:1, 512:640], sb["stat"][0:1, 512:640],
                                      AF.Sqrt)
            pl.emit("scalar", fqs2, reads=["stat"], writes=["stat"])
            def fq22(eng):
                st = sb["stat"]
                eng.tensor_tensor(st[0:1, 1024:1152], st[0:1, 512:640], st[0:1, 0:128], Mul)
                eng.tensor_scalar(st[0:1, 1024:1152], st[0:1, 1024:1152], 1e-5, None, MaxOp)
                eng.reciprocal(sb["stat2"][0:1, 0:128], st[0:1, 1024:1152])
                eng.tensor_tensor(sb["stat2"][0:1, 0:128], sb["stat2"][0:1, 0:128],
                                  st[0:1, 512:640], Mul)
                eng.tensor_scalar(sb["stat2"][0:1, 0:128], sb["stat2"][0:1, 0:128], 127.0, None, Mul)
                return eng.tensor_scalar(sb["strow"][0:1, 0:128], st[0:1, 1024:1152],
                                         sb["m227"][0:1, 0:1], None, Mul)
            pl.emit("vector", fq22, reads=["stat", "m227"], writes=["stat", "stat2", "strow"])
            bcast_row("pb2", sb["stat2"][0:1, 0:128], 128, 0, ["stat"])
            bcast_row("pb3", sb["strow"][0:1, 0:128], 128, 0, ["strow"])
            def fdq2(eng):
                return eng.tensor_copy(sb["deqb"][:, 0:128], sb["pb3"][:, 0:128])
            pl.emit("vector", fdq2, reads=["pb3"], writes=["deqb2"])
            if qtr == 0:
                dump("d_dq2", "deqb", 128, extra=("deqb2",))
            for cg in range(8):
                c4 = cg * 4
                def fz1b(eng, c4=c4):
                    last = None
                    for ci in range(4):
                        last = eng.tensor_tensor(
                            sb["tA"][:, ci * 128:ci * 128 + 128],
                            sb["ffh"][:, (c4 + ci) * 128:(c4 + ci + 1) * 128],
                            sb["pb2"][:, 0:128], Mul)
                    return last
                pl.emit("vector", fz1b, reads=["vt", "pb2"], writes=["tA0"])
                def fz3b(eng):
                    eng.tensor_scalar(sb["tB"][:, 0:512], sb["tA"][:, 0:512],
                                      12582912.0, None, Add)
                    return eng.tensor_scalar(sb["tB"][:, 0:512], sb["tB"][:, 0:512],
                                             -12582912.0, None, Add)
                pl.emit("vector", fz3b, reads=["tA0"], writes=["tB0"])
                def fz4b(eng):
                    return eng.tensor_copy(sb["tB"].ap().bitcast(I32)[:, 512:1024],
                                           sb["tB"][:, 0:512])
                pl.emit("vector", fz4b, reads=["tB0"], writes=["tB1"])
                def fz5b(eng, c4=c4):
                    return eng.tensor_scalar(ffqv()[:, c4 * 128:(c4 + 4) * 128],
                                             sb["tB"].ap().bitcast(I32)[:, 512:1024],
                                             -128.0, 127.0, MaxOp, MinOp)
                pl.emit("vector", fz5b, reads=["tB1"], writes=["qt"])
            if qtr == 0:
                dump("d_ffq", "qt", 2048)

            for fs in range(8):
                def fwl2(eng, fs=fs):
                    src = bass.AP(w2q.tensor, fs * 524288,
                                  [[1024, 128], [131072, 4], [1, 1024]])
                    return eng.dma_start(
                        out=sb["wbuf"].ap().bitcast(BF16)[:, 0:4096], in_=src)
                pl.emit("sync", fwl2, writes=["wbuf", "wbuf0", "wbuf1"])
                for di in range(8):
                    bank = "pb%d" % di
                    for cc in range(4):
                        c = fs * 4 + cc
                        def fm5(eng, cc=cc, di=di, c=c, bank=bank, fs=fs):
                            wb = sb["wbuf"].ap().bitcast(BF16)
                            return eng.matmul(
                                sb[bank][:, 0:128],
                                wb[:, cc * 1024 + di * 128:cc * 1024 + di * 128 + 128],
                                ffqv()[:, c * 128:(c + 1) * 128],
                                start=(fs == 0 and cc == 0), stop=(fs == 7 and cc == 3))
                        pl.emit("tensor", fm5, reads=["wbuf", "qt"],
                                writes=[bank] if ((fs == 0 and cc == 0)
                                                  or (fs == 7 and cc == 3)) else [])
            for di in range(8):
                bank = "pb%d" % di
                def foA(eng, di=di, bank=bank):
                    return eng.tensor_tensor(sb["tA"][:, di * 128:di * 128 + 128],
                                             sb[bank][:, 0:128],
                                             sb["deqb"][:, 0:128], Mul)
                pl.emit("vector", foA, reads=[bank, "deqb2"],
                        writes=["tA0" if di < 4 else "tA1"])
            for di in range(8):
                def foB(eng, di=di, ht0=ht0):
                    return eng.scalar_tensor_tensor(
                        sb["xg"][:, di * TOK + ht0:di * TOK + ht0 + 128],
                        sb["tA"][:, di * 128:di * 128 + 128],
                        sb["s_b2_c"][:, di:di + 1],
                        sb["xg"][:, di * TOK + ht0:di * TOK + ht0 + 128], Add, Add)
                pl.emit("vector", foB, reads=["tA0", "tA1", "s_b2_c", "xg"],
                        writes=["xg"])

        pl.count["dO"] = pl.count.get("dO", 0)
        for c in range(8):
            def fod(eng, c=c, t0=t0):
                return eng.dma_start(out=out_ap[c * 128:(c + 1) * 128, t0:t0 + TOK],
                                     in_=sb["xg"][:, c * TOK:(c + 1) * TOK])
            sem = "dO"
            waits = {}
            r = pl._get("xg")
            if r["w"] is not None:
                s, v = r["w"]
                waits[s] = max(waits.get(s, 0), v)
            pl.count[sem] += 16
            r["r"].append((sem, pl.count[sem]))
            pl.steps["gpsimd"].append((fod, sorted(waits.items())))

    for g in range(GROUPS):
        group_body(g)
    if pl.count.get("dO", 0) == 0:
        for c in range(8):
            def fod(eng, c=c):
                return eng.dma_start(out=out_ap[c * 128:(c + 1) * 128, 0:TOK],
                                     in_=sb["xg"][:, c * TOK:(c + 1) * TOK])
            waits = {}
            r = pl._get("xg")
            if r["w"] is not None:
                sx, vx = r["w"]
                waits[sx] = vx
            pl.count["dO"] += 16
            r["r"].append(("dO", pl.count["dO"]))
            pl.steps["gpsimd"].append((fod, sorted(waits.items())))

    fin_waits = [("dO", pl.count["dO"])]
    if pl.count.get("dD", 0):
        fin_waits.append(("dD", pl.count["dD"]))
    pl.steps["gpsimd"].append((None, fin_waits))

    sem_objs = {}
    sem_guards = []
    for s in sorted(set(list(ENG_SEM.values()) + ["dO", "dD"])):
        gd = nc.semaphore(s)
        sem_objs[s] = gd.__enter__()
        sem_guards.append(gd)
    blk = nc.Block()
    block = blk.__enter__()

    def stream(eng_name):
        def body(eng):
            own = ENG_SEM[eng_name]
            amt = ENG_AMT[eng_name]
            for item in pl.steps[eng_name]:
                fn, waits = item
                for (s, v) in waits:
                    eng.wait_ge(sem_objs[s], v)
                if fn is None:
                    continue
                inst = fn(eng)
                if fn.__name__ in ("fod", "fn") and eng_name in ("gpsimd", "scalar")                         and getattr(fn, "_dma_out", False):
                    inst.then_inc(sem_objs["dD"], 16)
                elif eng_name == "gpsimd" and fn.__name__ == "fod":
                    inst.then_inc(sem_objs["dO"], 16)
                elif inst is None:
                    eng.sem_inc(sem_objs[own], amt)
                else:
                    inst.then_inc(sem_objs[own], amt)
        return body

    block.sync(stream("sync"))
    block.tensor(stream("tensor"))
    block.vector(stream("vector"))
    block.scalar(stream("scalar"))
    block.gpsimd(stream("gpsimd"))

    blk.__exit__(None, None, None)
    if _RACE_HOOK is not None:
        _RACE_HOOK(pl, sb, nc)
    for gd in reversed(sem_guards):
        gd.__exit__(None, None, None)
    for gd in reversed(ctx_stack):
        gd.__exit__(None, None, None)
    return nc


def bank_ap(x):
    return x


def _host_inputs(core_x, w):
    d = {}
    d["xT"] = np.ascontiguousarray(core_x.T)
    for nm, key in (("wqT", "Wq"), ("wkT", "Wk"), ("wvT", "Wv"), ("waT", "Wa"),
                    ("wbT", "Wb"), ("wgT", "Wg"), ("woT", "Wo"), ("w1T", "w1"),
                    ("w2T", "w2")):
        d[nm] = np.ascontiguousarray(w[key].T)
    d["ba_c"] = np.ascontiguousarray(w["ba"].reshape(8, 128).T)
    d["bb_c"] = np.ascontiguousarray(w["bb"].reshape(8, 1))
    d["b1_c"] = np.ascontiguousarray(w["b1"].reshape(32, 128).T)
    d["b2_c"] = np.ascontiguousarray(w["b2"].reshape(8, 128).T)
    d["qcw_c"] = np.ascontiguousarray(w["qc_w"].reshape(4, 128, 4).transpose(1, 0, 2).reshape(128, 16))
    d["kcw_c"] = np.ascontiguousarray(w["kc_w"].reshape(4, 128, 4).transpose(1, 0, 2).reshape(128, 16))
    d["vcw_c"] = np.ascontiguousarray(w["vc_w"].reshape(8, 128, 4).transpose(1, 0, 2).reshape(128, 32))
    d["qcb_c"] = np.ascontiguousarray(w["qc_b"].reshape(4, 128).T)
    d["kcb_c"] = np.ascontiguousarray(w["kc_b"].reshape(4, 128).T)
    d["vcb_c"] = np.ascontiguousarray(w["vc_b"].reshape(8, 128).T)
    d["lnw_c"] = np.ascontiguousarray(w["ln_w"].reshape(128, 1))
    d["lnb_c"] = np.ascontiguousarray(w["ln_b"].reshape(128, 1))
    d["ident"] = np.eye(128, dtype=np.float32)
    s = np.arange(64)[:, None]
    t = np.arange(64)[None, :]
    d["maskU"] = np.ascontiguousarray(np.tile((s < t).astype(np.float32), (1, 8)))
    d["maskQn"] = np.ascontiguousarray(np.tile(-(s <= t).astype(np.float32), (1, 8)))
    d["ones_c"] = np.ones((128, 1), np.float32)
    d["onesr_c"] = np.ones((128, 128), np.float32)
    sel = np.zeros((64, 1024), np.float32)
    for i in range(8):
        sel[i, i * 128:(i + 1) * 128] = 1.0
        sel[32 + i, i * 128:(i + 1) * 128] = 1.0
    d["sel8_c"] = sel
    for k in d:
        d[k] = np.ascontiguousarray(d[k], dtype=np.float32)
    return d


def kernel(L0_attn_x, L0_attn_Wq, L0_attn_Wk, L0_attn_Wv, L0_attn_Wa, L0_attn_ba,
           L0_attn_Wb, L0_attn_bb, L0_attn_Wg, L0_attn_Wo, L0_attn_qc_w, L0_attn_qc_b,
           L0_attn_kc_w, L0_attn_kc_b, L0_attn_vc_w, L0_attn_vc_b, L0_attn_ln_w,
           L0_attn_ln_b, L0_ffn_w1, L0_ffn_b1, L0_ffn_w2, L0_ffn_b2):
    B, T, _ = L0_attn_x.shape
    w = {"Wq": L0_attn_Wq, "Wk": L0_attn_Wk, "Wv": L0_attn_Wv, "Wa": L0_attn_Wa,
         "ba": L0_attn_ba, "Wb": L0_attn_Wb, "bb": L0_attn_bb, "Wg": L0_attn_Wg,
         "Wo": L0_attn_Wo, "qc_w": np.asarray(L0_attn_qc_w).reshape(512, 4),
         "qc_b": L0_attn_qc_b, "kc_w": np.asarray(L0_attn_kc_w).reshape(512, 4),
         "kc_b": L0_attn_kc_b, "vc_w": np.asarray(L0_attn_vc_w).reshape(1024, 4),
         "vc_b": L0_attn_vc_b, "ln_w": L0_attn_ln_w, "ln_b": L0_attn_ln_b,
         "w1": L0_ffn_w1, "b1": L0_ffn_b1, "w2": L0_ffn_w2, "b2": L0_ffn_b2}
    w = {k: np.asarray(v, dtype=np.float32) for k, v in w.items()}

    TOK = 512
    GROUPS = T // TOK
    nc = build_program(TOK, GROUPS)
    in_maps = [_host_inputs(np.asarray(L0_attn_x[b], np.float32), w) for b in range(B)]
    res = run_bass_kernel_spmd(nc, in_maps, core_ids=list(range(B)))
    out = np.stack([np.asarray(res.results[b]["outT"], np.float32).T for b in range(B)])
    return out


# version 03
# speedup vs baseline: 1.1332x; 1.1332x over previous
import numpy as np
import concourse.bass as bass
import concourse.mybir as mybir
from concourse.alu_op_type import AluOpType
from concourse.bass_utils import run_bass_kernel_spmd

F32 = mybir.dt.float32
BF16 = mybir.dt.bfloat16
I32 = mybir.dt.int32
AF = mybir.ActivationFunctionType
AX = mybir.AxisListType.X
Mul = AluOpType.mult
Add = AluOpType.add
Sub = AluOpType.subtract
Div = AluOpType.divide
MaxOp = AluOpType.max
MinOp = AluOpType.min
AMax = AluOpType.abs_max

H, DK, DV = 8, 64, 128
D, FF = 1024, 4096
EPS = 1e-5
_RACE_HOOK = None

ENG_SEM = {"sync": "dI", "tensor": "tT", "vector": "tV", "scalar": "tS",
           "gpsimd": "dG"}
ENG_AMT = {"sync": 16, "tensor": 1, "vector": 1, "scalar": 1, "gpsimd": 16}


class Plan:

    def __init__(self):
        self.steps = {e: [] for e in ENG_SEM}
        self.count = {s: 0 for s in set(ENG_SEM.values())}
        self.res = {}

    def _get(self, name):
        return self.res.setdefault(name, {"w": None, "r": []})

    def emit(self, eng, fn, reads=(), writes=(), extra_waits=()):
        sem = ENG_SEM[eng]
        amt = ENG_AMT[eng]
        waits = {}
        for nm in reads:
            r = self._get(nm)
            if r["w"] is not None:
                s, v = r["w"]
                waits[s] = max(waits.get(s, 0), v)
        for nm in writes:
            r = self._get(nm)
            if r["w"] is not None:
                s, v = r["w"]
                waits[s] = max(waits.get(s, 0), v)
            for (s, v) in r["r"]:
                waits[s] = max(waits.get(s, 0), v)
        for (s, v) in extra_waits:
            waits[s] = max(waits.get(s, 0), v)
        waits.pop(sem, None)
        self.count[sem] += amt
        post = self.count[sem]
        for nm in reads:
            self._get(nm)["r"].append((sem, post))
        for nm in writes:
            r = self._get(nm)
            r["w"] = (sem, post)
            r["r"] = []
        self.steps[eng].append((fn, sorted(waits.items())))
        return post


def build_program(TOK, GROUPS, dbg=False):
    import os as _os
    KPH = int(_os.environ.get("KPHASE", "9"))
    NCHLIM = int(_os.environ.get("NCHLIM", "99"))
    SKMB = _os.environ.get("SKMB") == "1"
    SKSOL = _os.environ.get("SKSOL") == "1"
    SKBND = _os.environ.get("SKBND") == "1"
    SKOST = _os.environ.get("SKOST") == "1"
    SKFAC = _os.environ.get("SKFAC") == "1"
    SKUPD = _os.environ.get("SKUPD") == "1"
    SKACC = _os.environ.get("SKACC") == "1"
    FUPALT = _os.environ.get("FUPOLD") != "1"
    FI1ALT = _os.environ.get("FI1OLD") != "1"
    SKMB2 = _os.environ.get("SKMB2") == "1"

    T_ALL = TOK * GROUPS
    NCH = TOK // 64
    nc = bass.Bass("TRN2", target_bir_lowering=False,
                   detect_race_conditions=(_os.environ.get("RACEDET") == "1"))

    dr = {}

    def din(name, shape, dtype=F32):
        dr[name] = nc.dram_tensor(name, list(shape), dtype, kind="ExternalInput").ap()

    din("xT", (D, T_ALL))
    din("wqT", (D, 512)); din("wkT", (D, 512)); din("wvT", (D, 1024))
    din("waT", (D, 1024)); din("wbT", (D, 8)); din("wgT", (D, 1024))
    din("woT", (1024, D))
    din("w1T", (D, FF)); din("w2T", (FF, D))
    din("ba_c", (128, 8)); din("bb_c", (8, 1)); din("b1_c", (128, 32)); din("b2_c", (128, 8))
    din("qcw_c", (128, 16)); din("kcw_c", (128, 16)); din("vcw_c", (128, 32))
    din("qcb_c", (128, 4)); din("kcb_c", (128, 4)); din("vcb_c", (128, 8))
    din("lnw_c", (128, 1)); din("lnb_c", (128, 1))
    din("ident", (128, 128)); din("maskU", (64, 512)); din("maskQn", (64, 512))
    din("ones_c", (128, 1)); din("onesr_c", (128, 128)); din("sel8_c", (64, 1024))
    out_ap = nc.dram_tensor("outT", [D, T_ALL], F32, kind="ExternalOutput").ap()
    dbg_aps = {}
    if dbg:
        for n, shape in (("d_zg", (128, 8 * TOK)), ("d_qt", (128, 4 * TOK)),
                         ("d_kt", (128, 4 * TOK)), ("d_vt", (128, 8 * TOK)),
                         ("d_at", (128, 8 * TOK)), ("d_o", (128, 8 * TOK)),
                         ("d_gt", (128, 8 * TOK)), ("d_sv", (128, 512)),
                         ("d_rng", (128, 8 * TOK)), ("d_z2", (128, 8 * TOK)),
                         ("d_cpr", (128, 512)), ("d_gng", (128, 512)),
                         ("d_wv", (128, 512)), ("d_pva", (128, 512)),
                         ("d_mqn", (128, 512)), ("d_mupH", (128, 512)),
                         ("d_pb7t", (128, 128)), ("d_wtb3", (128, 128)),
                         ("d_acc5", (128, 128)), ("d_wtb0", (128, 128)),
                         ("d_acc2", (128, 128)), ("d_wvj0", (128, 512)),
                         ("d_wvj1", (128, 512)), ("d_wvj1b", (128, 512)),
                         ("d_xq", (128, 2048)), ("d_ffh", (128, 4096)),
                         ("d_ffq", (128, 2048)), ("d_deq", (128, 512)),
                         ("d_dq2", (128, 128)), ("d_m1", (1, 1)), ("d_m2", (1, 1))):
            dbg_aps[n] = nc.dram_tensor(n, list(shape), F32, kind="ExternalOutput").ap()
        dbg_aps["d_w1q"] = nc.dram_tensor("d_w1q", [128, 4096], BF16,
                                          kind="ExternalOutput").ap()
    w1q = nc.dram_tensor("w1q", [D, FF], BF16, kind="Internal").ap()
    w2q = nc.dram_tensor("w2q", [FF, D], BF16, kind="Internal").ap()

    pl = Plan()
    pl.count["dO"] = 0
    pl.count["dD"] = 0
    ctx_stack = []
    sb = {}

    def sbuf(name, shape, dtype=F32):
        g = nc.sbuf_tensor(name, list(shape), dtype)
        sb[name] = g.__enter__()
        ctx_stack.append(g)

    def psum(name, shape):
        g = nc.psum_tensor(name, list(shape), F32)
        sb[name] = g.__enter__()
        ctx_stack.append(g)

    sbuf("xg", (128, 8 * TOK))
    sbuf("zg", (128, 8 * TOK))
    sbuf("pre", (128, 8 * (TOK + 4)))
    sbuf("qhal", (128, 16))
    sbuf("khal", (128, 16))
    sbuf("vhal", (128, 32))
    sbuf("qt", (128, 4 * TOK))
    sbuf("kt", (128, 4 * TOK))
    sbuf("vt", (128, 8 * TOK))
    sbuf("at", (128, 8 * TOK))
    sbuf("gt", (128, 8 * TOK))
    sbuf("bb8", (8, TOK))
    sbuf("rng", (128, 8 * TOK))
    sbuf("cpr", (128, 512))
    sbuf("gng", (128, 512))
    sbuf("pva", (128, 512))
    sbuf("wv", (128, 512))
    sbuf("mup", (128, 512))
    sbuf("mupH", (128, 512))
    sbuf("mqn", (128, 512))
    sbuf("wtbP", (128, 128))
    sbuf("kstage", (128, 128))
    for p in range(4):
        sbuf(f"wthP{p}", (128, 128))
        sbuf(f"kthBD{p}", (128, 128))
    sbuf("sv", (128, 8 * DK))
    sbuf("sk", (128, 4 * 128))
    sbuf("s_ident", (128, 128))
    sbuf("onesrow", (128, 128))
    sbuf("s_maskU", (64, 512))
    sbuf("s_maskQn", (64, 512))
    sbuf("ones", (128, 1))
    sbuf("sel8", (64, 1024))
    sbuf("shufM", (128, 512))
    for n, shp in (("s_ba_c", (128, 8)), ("s_bb_c", (8, 1)), ("s_b1_c", (128, 32)),
                   ("s_b2_c", (128, 8)), ("s_qcw_c", (128, 16)), ("s_kcw_c", (128, 16)),
                   ("s_vcw_c", (128, 32)), ("s_qcb_c", (128, 4)), ("s_kcb_c", (128, 4)),
                   ("s_vcb_c", (128, 8)), ("s_lnw_c", (128, 1)), ("s_lnb_c", (128, 1))):
        sbuf(n, shp)
    sbuf("m127", (1, 1))
    sbuf("m227", (1, 1))
    sbuf("swcol", (128, 1))
    sbuf("stat", (1, 1536))
    sbuf("stat2", (1, 512))
    sbuf("strow", (1, 512))
    sbuf("deqb", (128, 512))
    sbuf("wbuf", (128, 4096))
    sbuf("sgn", (128, 512), BF16)
    sbuf("tA", (128, 1024))
    sbuf("tB", (128, 1024))
    sb["ffh"] = sb["vt"]
    for i in range(8):
        psum(f"pb{i}", (128, 512))

    _SBN = {n: "s_" + n for n in ("ident", "maskU", "maskQn", "ba_c", "bb_c", "b1_c", "b2_c", "qcw_c", "kcw_c", "vcw_c", "qcb_c", "kcb_c", "vcb_c", "lnw_c", "lnb_c")}

    def ld_const(name):
        sn = _SBN.get(name, name)
        def fn(eng, name=name, sn=sn):
            return eng.dma_start(out=sb[sn].ap(), in_=dr[name])
        pl.emit("sync", fn, writes=[sn])

    def onesrow_ap(p):
        return sb["onesrow"][p:p + 1, 0:128]

    def bcast_row(bank_name, row_ap, ncols, src_part, reads, pname=None):
        def fn(eng):
            return eng.matmul(sb[bank_name][:, 0:ncols], onesrow_ap(src_part), row_ap,
                              start=True, stop=True)
        pl.emit("tensor", fn, reads=list(reads) + ["onesrow"], writes=[bank_name])

    for n in ("ident", "maskU", "maskQn", "ba_c", "bb_c", "b1_c", "b2_c", "qcw_c",
              "kcw_c", "vcw_c", "qcb_c", "kcb_c", "vcb_c", "lnw_c", "lnb_c"):
        ld_const(n)
    def f_lo(eng):
        return eng.dma_start(out=sb["ones"].ap(), in_=dr["ones_c"])
    pl.emit("sync", f_lo, writes=["ones"])
    def f_lor(eng):
        return eng.dma_start(out=sb["onesrow"].ap(), in_=dr["onesr_c"])
    pl.emit("sync", f_lor, writes=["onesrow"])
    def f_sel(eng):
        return eng.dma_start(out=sb["sel8"].ap(), in_=dr["sel8_c"])
    pl.emit("sync", f_sel, writes=["sel8"])

    def fmemset(name, val, region=None):
        def fn(eng, name=name, val=val, region=region):
            a = sb[name].ap() if region is None else region
            return eng.memset(a, val)
        pl.emit("vector", fn, writes=[name])

    fmemset("sv", 0.0)
    fmemset("mup", 0.0)
    fmemset("kstage", 0.0)
    fmemset("mupH", 0.0)
    for nm in ("qhal", "khal", "vhal"):
        fmemset(nm, 0.0)

    def quant_weights(wsrc, wdst, n_rows, n_cols, mname):
        nslab = n_rows // 128
        for s in range(nslab):
            def fl(eng, s=s):
                return eng.dma_start(out=sb["wbuf"][:, 0:n_cols],
                                     in_=wsrc[s * 128:(s + 1) * 128, :])
            pl.emit("sync", fl, writes=["wbuf"] + ["wbufc%d" % i for i in range(8)])
            for qq in range(n_cols // 512):
                def fa(eng, qq=qq):
                    return eng.activation(sb["tA"][:, 0:512],
                                          sb["wbuf"][:, qq * 512:qq * 512 + 512],
                                          AF.Abs)
                pl.emit("scalar", fa, reads=["wbufc%d" % qq], writes=["tA0"])
                for k4 in range(4):
                    first = (s == 0 and qq == 0 and k4 == 0)
                    last = (s == nslab - 1 and qq == n_cols // 512 - 1 and k4 == 3)
                    def fm(eng, first=first, last=last, k4=k4):
                        return eng.matmul(sb["pb7"][0:1, 0:128], sb["ones"].ap(),
                                          sb["tA"][:, k4 * 128:(k4 + 1) * 128],
                                          start=first, stop=last)
                    pl.emit("tensor", fm, reads=["tA0", "ones"],
                            writes=["pb7"] if (first or last) else [])
        def frow(eng):
            return eng.tensor_copy(sb["stat"][0:1, 16:144], sb["pb7"][0:1, 0:128])
        pl.emit("vector", frow, reads=["pb7"], writes=["stat"])
        def fx2(eng):
            return eng.matmul(sb["pb6"][:, 0:1], sb["stat"][0:1, 16:144],
                              onesrow_ap(0)[0:1, 0:1], start=True, stop=True)
        pl.emit("tensor", fx2, reads=["stat", "onesrow"], writes=["pb6"])
        def fcp1(eng):
            return eng.tensor_copy(sb["swcol"].ap(), sb["pb6"][:, 0:1])
        pl.emit("vector", fcp1, reads=["pb6"], writes=["swcol"])
        def fx3(eng):
            return eng.matmul(sb["pb7"][0:1, 4:5], sb["swcol"].ap(),
                              sb["ones"].ap(), start=True, stop=True)
        pl.emit("tensor", fx3, reads=["swcol", "ones"], writes=["pb7"])
        def fr1q(eng, n_rows=n_rows, n_cols=n_cols):
            return eng.tensor_scalar(sb["stat"][0:1, 1:2], sb["pb7"][0:1, 4:5],
                                     1.0 / (n_rows * n_cols), 1e-5, Mul, MaxOp)
        pl.emit("vector", fr1q, reads=["pb7"], writes=["stat"])
        def fspq(eng):
            return eng.tensor_copy(sb["shufM"][0:64, 0:512], sb["s_maskU"].ap())
        pl.emit("vector", fspq, reads=["s_maskU"], writes=["shufM"])
        def fr2q(eng, mname=mname):
            eng.tensor_scalar(sb[mname].ap(), sb["stat"][0:1, 1:2], 1.0 / 127.0, None, Mul)
            return eng.reciprocal(sb["stat"][0:1, 2:3], sb["stat"][0:1, 1:2])
        pl.emit("vector", fr2q, reads=["stat"], writes=["stat", mname])
        bcast_row("pb6", sb["stat"][0:1, 2:3], 1, 0, ["stat"])
        def fc(eng):
            return eng.tensor_copy(sb["swcol"].ap(), sb["pb6"][:, 0:1])
        pl.emit("vector", fc, reads=["pb6"], writes=["swcol"])
        for s in range(nslab):
            def fl2(eng, s=s):
                return eng.dma_start(out=sb["wbuf"][:, 0:n_cols],
                                     in_=wsrc[s * 128:(s + 1) * 128, :])
            pl.emit("sync", fl2, writes=["wbuf"] + ["wbufc%d" % i for i in range(8)])
            for qq in range(n_cols // 512):
                pp = qq % 2
                def fmu(eng, qq=qq, pp=pp):
                    return eng.mul(sb["tA"][:, pp * 512:pp * 512 + 512],
                                   sb["wbuf"][:, qq * 512:qq * 512 + 512],
                                   sb["swcol"].ap())
                pl.emit("scalar", fmu, reads=["wbufc%d" % qq, "swcol"],
                        writes=["tA%d" % pp])
                def fst(eng, pp=pp):
                    eng.tensor_scalar(sb["tB"][:, 0:512],
                                      sb["tA"][:, pp * 512:pp * 512 + 512],
                                      12582912.0, None, Add)
                    return eng.tensor_scalar(sb["tB"][:, 0:512], sb["tB"][:, 0:512],
                                             -12582912.0, None, Add)
                pl.emit("vector", fst, reads=["tA%d" % pp], writes=["tB0"])
                def fci(eng):
                    return eng.tensor_copy(sb["tB"].ap().bitcast(I32)[:, 512:1024], sb["tB"][:, 0:512])
                pl.emit("vector", fci, reads=["tB0"], writes=["tB1"])
                def fcl(eng, qq=qq):
                    xqv = sb["kt"].ap().bitcast(BF16)
                    return eng.tensor_scalar(xqv[:, qq * 512:qq * 512 + 512],
                                             sb["tB"].ap().bitcast(I32)[:, 512:1024], -1.0, 1.0, MaxOp, MinOp)
                pl.emit("vector", fcl, reads=["tB1"], writes=["xqs%d" % qq])
            def fw(eng, s=s, n_cols=n_cols, wdst=wdst):
                xqv = sb["kt"].ap().bitcast(BF16)
                return eng.dma_start(out=wdst[s * 128:(s + 1) * 128, :],
                                     in_=xqv[:, 0:n_cols])
            pl.emit("gpsimd", fw,
                    reads=["xqs%d" % q for q in range(n_cols // 512)] + ["kt"])

    quant_weights(dr["w1T"], w1q, D, FF, "m127")
    quant_weights(dr["w2T"], w2q, FF, D, "m227")
    wq_done = pl.count["dG"]
    if dbg:
        def fwl0(eng):
            return eng.dma_start(out=sb["wbuf"].ap().bitcast(BF16)[:, 0:4096],
                                 in_=w1q[0:128, :])
        pl.emit("sync", fwl0, writes=["wbuf"] + ["wbufc%d" % i for i in range(8)],
                extra_waits=[("dG", wq_done)])
        def fn(eng):
            return eng.dma_start(out=dbg_aps["d_w1q"],
                                 in_=sb["wbuf"].ap().bitcast(BF16)[:, 0:4096])
        fn._dma_out = True
        sem = "dD"
        waits = {}
        r = pl._get("wbuf")
        if r["w"] is not None:
            sx, vx = r["w"]
            waits[sx] = vx
        pl.count[sem] += 16
        r["r"].append((sem, pl.count[sem]))
        pl.steps["scalar"].append((fn, sorted(waits.items())))

    def ln_to_zg(src_name):
        for c in range(8):
            pp = c % 2
            def fsq(eng, c=c, pp=pp):
                return eng.activation(sb["tA"][:, pp * 512:pp * 512 + 512],
                                      sb[src_name][:, c * TOK:c * TOK + 512], AF.Square)
            pl.emit("scalar", fsq, reads=[src_name], writes=["tA%d" % pp])
            def fm1(eng, c=c):
                return eng.matmul(sb["pb0"][0:1, 0:512], sb["ones"].ap(),
                                  sb[src_name][:, c * TOK:c * TOK + 512],
                                  start=(c == 0), stop=(c == 7))
            pl.emit("tensor", fm1, reads=[src_name, "ones"],
                    writes=["pb0"] if c in (0, 7) else [])
            def fm2(eng, c=c, pp=pp):
                return eng.matmul(sb["pb1"][0:1, 0:512], sb["ones"].ap(),
                                  sb["tA"][:, pp * 512:pp * 512 + 512],
                                  start=(c == 0), stop=(c == 7))
            pl.emit("tensor", fm2, reads=["tA%d" % pp, "ones"],
                    writes=["pb1"] if c in (0, 7) else [])
        def fv(eng):
            st = sb["stat"]
            eng.tensor_scalar(st[0:1, 0:512], sb["pb0"][0:1, 0:512], 1.0 / D, None, Mul)
            eng.tensor_tensor(sb["strow"][0:1, 0:512], st[0:1, 0:512], st[0:1, 0:512], Mul)
            eng.scalar_tensor_tensor(st[0:1, 512:1024], sb["pb1"][0:1, 0:512],
                                     1.0 / D, sb["strow"][0:1, 0:512], Mul, Sub)
            eng.tensor_scalar(st[0:1, 512:1024], st[0:1, 512:1024], EPS, None, Add)
            return eng.reciprocal(st[0:1, 512:1024], st[0:1, 512:1024])
        pl.emit("vector", fv, reads=["pb0", "pb1"], writes=["stat"])
        def fs(eng):
            return eng.activation(sb["stat"][0:1, 1024:1536], sb["stat"][0:1, 512:1024],
                                  AF.Sqrt)
        pl.emit("scalar", fs, reads=["stat"], writes=["stat"])
        def fmr(eng):
            return eng.tensor_tensor(sb["stat2"][0:1, 0:512], sb["stat"][0:1, 0:512],
                                     sb["stat"][0:1, 1024:1536], Mul)
        pl.emit("vector", fmr, reads=["stat"], writes=["stat", "stat2"])
        bcast_row("pb2", sb["stat"][0:1, 1024:1536], 512, 0, ["stat"])
        bcast_row("pb3", sb["stat2"][0:1, 0:512], 512, 0, ["stat"])
        for c in range(8):
            def fap(eng, c=c):
                eng.tensor_tensor(sb["tB"][:, 0:512], sb[src_name][:, c * TOK:c * TOK + 512],
                                  sb["pb2"][:, 0:512], Mul)
                return eng.scalar_tensor_tensor(sb["zg"][:, c * TOK:c * TOK + 512],
                                                sb["pb3"][:, 0:512], -1.0,
                                                sb["tB"][:, 0:512], Mul, Add)
            pl.emit("vector", fap, reads=[src_name, "pb2", "pb3"],
                    writes=["tB0", "zgc%d" % c])
        return ["zgc%d" % c for c in range(8)]

    def project(wt_name, fout, evac, bof=4):
        if fout == 512:
            for c in range(8):
                def fw(eng, c=c, wt_name=wt_name):
                    return eng.dma_start(out=sb["wbuf"][:, c * 512:(c + 1) * 512],
                                         in_=dr[wt_name][c * 128:(c + 1) * 128, :])
                pl.emit("sync", fw, writes=["wbuf", "wbufc%d" % c])
            for fi in range(4):
                bank = "pb%d" % (bof + fi % 2)
                for c in range(8):
                    def fm(eng, c=c, fi=fi, bank=bank):
                        return eng.matmul(
                            sb[bank][:, 0:512],
                            sb["wbuf"][:, c * 512 + fi * 128:c * 512 + fi * 128 + 128],
                            sb["zg"][:, c * TOK:c * TOK + 512],
                            start=(c == 0), stop=(c == 7))
                    pl.emit("tensor", fm, reads=["wbufc%d" % c, "zgc%d" % c],
                            writes=[bank] if c in (0, 7) else [])
                evac(fi, bank)
        else:
            assert fout == 1024
            for half in range(2):
                for cc in range(4):
                    c = half * 4 + cc
                    def fw(eng, c=c, cc=cc, wt_name=wt_name):
                        return eng.dma_start(out=sb["wbuf"][:, cc * 1024:(cc + 1) * 1024],
                                             in_=dr[wt_name][c * 128:(c + 1) * 128, :])
                    pl.emit("sync", fw, writes=["wbuf", "wbufc%d" % (cc * 2),
                                                "wbufc%d" % (cc * 2 + 1)])
                for fi in range(8):
                    bank = "pb%d" % fi
                    for cc in range(4):
                        c = half * 4 + cc
                        def fm(eng, c=c, cc=cc, fi=fi, bank=bank, half=half):
                            return eng.matmul(
                                sb[bank][:, 0:512],
                                sb["wbuf"][:, cc * 1024 + fi * 128:cc * 1024 + fi * 128 + 128],
                                sb["zg"][:, c * TOK:c * TOK + 512],
                                start=(half == 0 and cc == 0), stop=(half == 1 and cc == 3))
                        pl.emit("tensor", fm,
                                reads=["wbufc%d" % (2 * cc + (1 if fi >= 4 else 0)),
                                       "zgc%d" % c],
                                writes=[bank] if (half == 0 and cc == 0) or (half == 1 and cc == 3) else [])
            for fi in range(8):
                evac(fi, "pb%d" % fi)

    def dump(dname, src_name, cols, extra=()):
        if not dbg:
            return
        def fn(eng, dname=dname, src_name=src_name, cols=cols):
            return eng.dma_start(out=dbg_aps[dname], in_=sb[src_name][:, 0:cols])
        fn._dma_out = True
        sem = "dD"
        waits = {}
        pl.count[sem] += 16
        for nm in (src_name,) + tuple(extra):
            r = pl._get(nm)
            if r["w"] is not None:
                sx, vx = r["w"]
                waits[sx] = max(waits.get(sx, 0), vx)
            r["r"].append((sem, pl.count[sem]))
        pl.steps["scalar"].append((fn, sorted(waits.items())))

    if dbg:
        dump("d_m1", "m127", 1)
        dump("d_m2", "m227", 1)

    def group_body(g):
        t0 = g * TOK
        for c in range(8):
            def fx(eng, c=c, t0=t0):
                return eng.dma_start(out=sb["xg"][:, c * TOK:(c + 1) * TOK],
                                     in_=dr["xT"][c * 128:(c + 1) * 128, t0:t0 + TOK])
            pl.emit("sync", fx, writes=["xg"])

        ln_to_zg("xg")

        w = TOK + 4

        def halo_in(hal, nch):
            def fn(eng, hal=hal, nch=nch):
                a = sb["pre"].ap().rearrange("p (c w) -> p c w", w=w)
                hh = sb[hal].ap().rearrange("p (c f) -> p c f", f=4)
                return eng.tensor_copy(a[:, 0:nch, 0:4], hh[:, 0:nch, :])
            pl.emit("vector", fn, reads=[hal], writes=["pre"])

        def halo_out(hal, nch):
            def fsp(eng):
                return eng.tensor_copy(sb["shufM"][:, 0:512], sb["zg"][:, 0:512])
            pl.emit("vector", fsp, reads=["zgc0"], writes=["shufM"])
            def fn(eng, hal=hal, nch=nch):
                a = sb["pre"].ap().rearrange("p (c w) -> p c w", w=w)
                hh = sb[hal].ap().rearrange("p (c f) -> p c f", f=4)
                return eng.tensor_copy(hh[:, 0:nch, :], a[:, 0:nch, TOK:TOK + 4])
            pl.emit("vector", fn, reads=["pre"], writes=[hal])

        def ev_pre(fi, bank):
            def fe(eng, fi=fi, bank=bank):
                return eng.tensor_copy(sb["pre"][:, fi * w + 4:fi * w + 4 + 512],
                                       sb[bank][:, 0:512])
            pl.emit("vector", fe, reads=[bank], writes=["pre"])

        def conv(name_t, nch, wc, bc, scale=None):
            for c in range(nch):
                pp = c % 2
                def f0(eng, c=c, pp=pp, wc=wc):
                    return eng.mul(sb["tA"][:, pp * 512:pp * 512 + TOK],
                                   sb["pre"][:, c * w + 1:c * w + 1 + TOK],
                                   sb[wc][:, c * 4:c * 4 + 1])
                pl.emit("scalar", f0, reads=["pre", wc], writes=["tA%d" % pp])
                for j in (1, 2, 3):
                    def fj(eng, c=c, j=j, pp=pp, wc=wc):
                        return eng.scalar_tensor_tensor(
                            sb["tA"][:, pp * 512:pp * 512 + TOK],
                            sb["pre"][:, c * w + 1 + j:c * w + 1 + j + TOK],
                            sb[wc][:, c * 4 + j:c * 4 + j + 1],
                            sb["tA"][:, pp * 512:pp * 512 + TOK], Mul, Add)
                    pl.emit("vector", fj, reads=["pre", wc, "tA%d" % pp],
                            writes=["tA%d" % pp])
                def fs(eng, c=c, pp=pp, name_t=name_t, bc=bc):
                    return eng.activation(sb[name_t][:, c * TOK:(c + 1) * TOK],
                                          sb["tA"][:, pp * 512:pp * 512 + TOK],
                                          AF.Silu, bias=sb[bc][:, c:c + 1])
                pl.emit("scalar", fs, reads=["tA%d" % pp, bc], writes=[name_t])
                if scale is not None:
                    def fsc(eng, c=c, name_t=name_t, scale=scale):
                        return eng.tensor_scalar(sb[name_t][:, c * TOK:(c + 1) * TOK],
                                                 sb[name_t][:, c * TOK:(c + 1) * TOK],
                                                 scale, None, Mul)
                    pl.emit("vector", fsc, reads=[name_t], writes=[name_t])

        halo_in("qhal", 4)
        project("wqT", 512, ev_pre, bof=4)
        halo_out("qhal", 4)
        conv("qt", 4, "s_qcw_c", "s_qcb_c")

        dump("d_zg", "zg", 8 * TOK, extra=tuple("zgc%d" % c for c in range(8)))
        halo_in("khal", 4)
        project("wkT", 512, ev_pre, bof=6)
        halo_out("khal", 4)
        conv("kt", 4, "s_kcw_c", "s_kcb_c", scale=DK ** (-0.5))

        halo_in("vhal", 8)
        project("wvT", 1024, ev_pre)
        halo_out("vhal", 8)
        conv("vt", 8, "s_vcw_c", "s_vcb_c")

        dump("d_qt", "qt", 4 * TOK)
        dump("d_kt", "kt", 4 * TOK, extra=tuple("xqs%d" % q for q in range(8)))
        dump("d_vt", "vt", 8 * TOK)

        def ev_a(fi, bank):
            def fe(eng, fi=fi, bank=bank):
                return eng.activation(sb["at"][:, fi * TOK:fi * TOK + 512],
                                      sb[bank][:, 0:512], AF.Sigmoid,
                                      bias=sb["s_ba_c"][:, fi:fi + 1])
            pl.emit("scalar", fe, reads=[bank, "s_ba_c"], writes=["at"])
        project("waT", 1024, ev_a)

        def ev_g(fi, bank):
            def fe(eng, fi=fi, bank=bank):
                return eng.activation(sb["gt"][:, fi * TOK:fi * TOK + 512],
                                      sb[bank][:, 0:512], AF.Sigmoid)
            pl.emit("scalar", fe, reads=[bank], writes=["gt"])
        project("wgT", 1024, ev_g)

        def fwb(eng):
            src = bass.AP(dr["wbT"].tensor, 0, [[8, 128], [1024, 8], [1, 8]])
            return eng.dma_start(out=sb["wbuf"][:, 0:64], in_=src)
        pl.emit("sync", fwb, writes=["wbuf", "wbufc0"])
        for c in range(8):
            def fm(eng, c=c):
                return eng.matmul(sb["pb6"][0:8, 0:512], sb["wbuf"][:, c * 8:c * 8 + 8],
                                  sb["zg"][:, c * TOK:c * TOK + 512],
                                  start=(c == 0), stop=(c == 7))
            pl.emit("tensor", fm, reads=["wbufc0", "zgc%d" % c],
                    writes=["pb6"] if c in (0, 7) else [])
        def fb2(eng):
            return eng.activation(sb["bb8"][0:8, 0:TOK], sb["pb6"][0:8, 0:512],
                                  AF.Sigmoid, bias=sb["s_bb_c"][0:8, 0:1])
        pl.emit("scalar", fb2, reads=["pb6", "s_bb_c"], writes=["bb8"])
        for h in range(H):
            bank = "pb%d" % (6 + h % 2)
            def fbb(eng, h=h, bank=bank):
                return eng.matmul(sb[bank][:, 0:512],
                                  sb["sel8"][0:8, h * 128:(h + 1) * 128],
                                  sb["bb8"][0:8, 0:TOK], start=True, stop=True)
            pl.emit("tensor", fbb, reads=["sel8", "bb8"], writes=[bank])
            def fr1(eng, h=h, bank=bank):
                eng.reciprocal(sb["tA"][:, 0:512], sb["at"][:, h * TOK:h * TOK + 512])
                eng.tensor_tensor(sb["tA"][:, 512:1024], sb["tA"][:, 0:512],
                                  sb[bank][:, 0:512], Mul)
                return eng.tensor_scalar(sb["rng"][:, h * TOK:h * TOK + 512],
                                         sb["tA"][:, 512:1024], -1.0, None, Mul)
            pl.emit("vector", fr1, reads=[bank, "at"], writes=["tA0", "tA1", "rng"])

        dump("d_at", "at", 8 * TOK)
        dump("d_rng", "rng", 8 * TOK)
        if KPH <= 1:
            return
        def kslice(name, h, c0, n=64):
            return sb[name][64 * (h % 2):64 * (h % 2) + 64,
                            (h // 2) * TOK + c0:(h // 2) * TOK + c0 + n]

        for ch in range(min(NCH, NCHLIM)):
            c0 = ch * 64
            for p in range(4):
                bk = "pb%d" % p
                def ftr(eng, p=p, bk=bk):
                    return eng.transpose(sb[bk][:, 0:128],
                                         sb["sv"][:, p * 128:(p + 1) * 128],
                                         sb["s_ident"].ap())
                pl.emit("tensor", ftr, reads=["sv", "s_ident"], writes=[bk])
                def fev(eng, p=p, bk=bk):
                    return eng.tensor_copy(sb["sk"][:, p * 128:(p + 1) * 128],
                                           sb[bk][:, 0:128])
                pl.emit("vector", fev, reads=[bk], writes=["sk"])

            for h in range(H):
                kap = kslice("kt", h, c0)
                qap = kslice("qt", h, c0)
                bank = "pb%d" % (4 + h % 2)
                def fmm(eng, kap=kap, bank=bank):
                    return eng.matmul(sb[bank][0:64, 0:64], kap, kap, start=True, stop=True)
                pl.emit("tensor", fmm, reads=["kt"], writes=[bank])
                def fme(eng, h=h, bank=bank):
                    dst = bass.AP(sb["mup"], h, [[512, 64], [64, 8], [8, 8]])
                    src = bass.AP(sb[bank], 0, [[512, 64], [8, 8], [1, 8]])
                    msk = bass.AP(sb["s_maskU"], h * 64, [[512, 64], [8, 8], [1, 8]])
                    return eng.tensor_tensor(dst, src, msk, Mul)
                pl.emit("vector", fme, reads=[bank, "s_maskU"], writes=["mup"])
                def fmeh(eng, h=h, bank=bank):
                    hh = h % 2
                    dst = sb["mupH"][64 * hh:64 * hh + 64,
                                     (h // 2) * 128 + 64 * hh:(h // 2) * 128 + 64 * hh + 64]
                    return eng.tensor_tensor(dst, sb[bank][0:64, 0:64],
                                             sb["s_maskU"][:, h * 64:(h + 1) * 64], Mul)
                pl.emit("vector", fmeh, reads=[bank, "s_maskU"], writes=["mupH"])
                bank2 = "pb%d" % (6 + h % 2)
                def fmq(eng, kap=kap, qap=qap, bank2=bank2):
                    return eng.matmul(sb[bank2][0:64, 0:64], kap, qap, start=True, stop=True)
                pl.emit("tensor", fmq, reads=["kt", "qt"], writes=[bank2])
                def fqe(eng, h=h, bank2=bank2):
                    eng.tensor_tensor(sb["mqn"][0:64, h * 64:(h + 1) * 64],
                                      sb[bank2][0:64, 0:64],
                                      sb["s_maskQn"][:, h * 64:(h + 1) * 64], Mul)
                    return eng.tensor_copy(sb["mqn"][64:128, h * 64:(h + 1) * 64],
                                           sb["mqn"][0:64, h * 64:(h + 1) * 64])
                pl.emit("vector", fqe, reads=[bank2, "s_maskQn"], writes=["mqn"])

            for p in range(4):
                def fks(eng, p=p, c0=c0):
                    eng.tensor_copy(sb["kstage"][0:64, 0:64],
                                    sb["kt"][0:64, p * TOK + c0:p * TOK + c0 + 64])
                    return eng.tensor_copy(sb["kstage"][64:128, 64:128],
                                           sb["kt"][64:128, p * TOK + c0:p * TOK + c0 + 64])
                pl.emit("vector", fks, reads=["kt"], writes=["kstage"])
                bk = "pb%d" % (4 + p % 2)
                def ftr2(eng, bk=bk):
                    return eng.transpose(sb[bk][:, 0:128], sb["kstage"].ap(),
                                         sb["s_ident"].ap())
                pl.emit("tensor", ftr2, reads=["kstage", "s_ident"], writes=[bk])
                def fke(eng, p=p, bk=bk):
                    return eng.tensor_copy(sb[f"kthBD{p}"].ap(), sb[bk][:, 0:128])
                pl.emit("vector", fke, reads=[bk], writes=[f"kthBD{p}"])

            for h in range(H):
                kap = kslice("kt", h, c0)
                skap = sb["sk"][64 * (h % 2):64 * (h % 2) + 64,
                                (h // 2) * 128:(h // 2) * 128 + 128]
                bank = "pb%d" % (h % 4)
                def fp(eng, skap=skap, kap=kap, bank=bank):
                    return eng.matmul(sb[bank][:, 0:64], skap, kap, start=True, stop=True)
                pl.emit("tensor", fp, reads=["sk", "kt"], writes=[bank])
                def fpe(eng, h=h, bank=bank):
                    return eng.tensor_copy(sb["pva"][:, h * 64:(h + 1) * 64],
                                           sb[bank][:, 0:64])
                pl.emit("vector", fpe, reads=[bank], writes=["pva"])

            def hs3(name, cc0):
                return sb[name].ap().rearrange("p (h t) -> p h t", t=TOK)[:, :, cc0:cc0 + 64]

            def fcp(eng, c0=c0):
                cp = sb["cpr"].ap().rearrange("p (h t) -> p h t", t=64)
                av = hs3("at", c0)
                last = None
                for h in range(H):
                    last = eng.tensor_tensor_scan(cp[:, h, :], av[:, h, :],
                                                  sb["onesrow"][:, 0:64], 1.0, Mul, Mul)
                return last
            pl.emit("vector", fcp, reads=["at", "onesrow"], writes=["cpr"])
            def fgn(eng, c0=c0):
                g3 = sb["gng"].ap().rearrange("p (h t) -> p h t", t=64)
                t3 = sb["tA"].ap().rearrange("p (h t) -> p h t", t=128)[:, :, 0:64]
                p3 = sb["pva"].ap().rearrange("p (h t) -> p h t", t=64)
                eng.reciprocal(sb["tB"][:, 0:512], sb["cpr"][:, 0:512])
                cr = sb["tB"][:, 0:512].rearrange("p (h t) -> p h t", t=64)
                eng.tensor_tensor(t3, hs3("at", c0), cr, Mul)
                eng.tensor_tensor(t3, t3, hs3("vt", c0), Mul)
                eng.tensor_tensor(t3, p3, t3, Sub)
                return eng.tensor_tensor(g3, hs3("rng", c0), t3, Mul)
            pl.emit("vector", fgn, reads=["cpr", "at", "vt", "rng", "pva"],
                    writes=["gng", "tA0", "tA1", "tB0", "tB1"])

            def fwz(eng):
                return eng.memset(sb["wv"].ap(), 0.0)
            pl.emit("vector", fwz, writes=["wv"])
            wv3 = sb["wv"].ap().rearrange("p (h t) -> p h t", t=64)
            rng3 = hs3("rng", c0)
            gng3 = sb["gng"].ap().rearrange("p (h t) -> p h t", t=64)
            mupr = sb["mup"].ap().rearrange("s (h t) -> s h t", t=64)
            for j in range(8):
                jt = j * 8
                if SKMB:
                    continue
                q32 = (jt // 32) * 32
                r8 = jt - q32
                if not SKMB2:
                    def fshf(eng, r8=r8):
                        mask = [(r8 + i) % 32 for i in range(32)]
                        return eng.stream_shuffle(sb["shufM"][:, 0:512],
                                                  sb["mup"][:, 0:512], mask)
                    pl.emit("vector", fshf, reads=["mup"], writes=["shufM"])
                for sp in range(7 if not SKMB2 else 0):
                    ncol = 56 - 8 * sp
                    def fmb(eng, sp=sp, ncol=ncol, jt=jt, q32=q32):
                        return eng.matmul(sb["pb6"][:, sp * 64:sp * 64 + ncol],
                                          sb["sel8"][q32:q32 + 8, sp * 128:sp * 128 + 128],
                                          sb["shufM"][q32:q32 + 8,
                                                      (jt // 8) * 64 + 8 * (sp + 1):
                                                      (jt // 8) * 64 + 64],
                                          start=True, stop=True)
                    pl.emit("tensor", fmb, reads=["sel8", "shufM"],
                            writes=["pb6"])
                if not SKMB2:
                    def ffold(eng, jt=jt, c0=c0):
                        last = None
                        for sp in range(7):
                            n_t = 7 - sp
                            mbs = bass.AP(sb["pb6"], sp * 64, [[512, 128], [8, n_t], [1, 8]])
                            rsl = bass.AP(sb["rng"], c0 + jt + sp + 1,
                                          [[8 * TOK, 128], [1, n_t], [TOK, 8]])
                            dst = bass.AP(sb["pva"], sp * 64, [[512, 128], [8, n_t], [1, 8]])
                            last = eng.tensor_tensor(dst, mbs, rsl, Mul)
                        return last
                    pl.emit("vector", ffold, reads=["pb6", "rng"], writes=["mball"])
                if SKSOL:
                    continue
                if j == 0 or SKACC:
                    def fi0(eng, jt=jt):
                        return eng.tensor_scalar(wv3[:, :, jt:jt + 8], gng3[:, :, jt:jt + 8],
                                                 -1.0, None, Mul)
                    pl.emit("vector", fi0, reads=["gng"], writes=["wv"])
                    def fi0s(eng):
                        return eng.tensor_copy(sb["shufM"][:, 0:512], sb["mup"][:, 0:512])
                    pl.emit("vector", fi0s, reads=["mup"], writes=["shufM"])
                elif FI1ALT:
                    def fi1a(eng, jt=jt, c0=c0):
                        last = None
                        for h in range(8):
                            p = h // 2
                            hh = h % 2
                            accs2 = sb["pb%d" % (2 + p)][:, 64 * hh + jt:64 * hh + jt + 8]
                            rngs2 = sb["rng"][:, h * TOK + c0 + jt:h * TOK + c0 + jt + 8]
                            t2 = sb["tA"][:, h * 8:h * 8 + 8]
                            eng.tensor_tensor(t2, rngs2, accs2, Mul)
                        for h in range(8):
                            gngs2 = sb["gng"][:, h * 64 + jt:h * 64 + jt + 8]
                            wvs2 = sb["wv"][:, h * 64 + jt:h * 64 + jt + 8]
                            t2 = sb["tA"][:, h * 8:h * 8 + 8]
                            last = eng.tensor_tensor(wvs2, t2, gngs2, Sub)
                        return last
                    pl.emit("vector", fi1a,
                            reads=["rng", "pb2", "pb3", "pb4", "pb5", "gng"],
                            writes=["wv", "tA0"])
                    if dbg and ch == 0 and j == 1:
                        dump("d_wvj1", "wv", 512)
                else:
                    def fi1(eng, jt=jt, c0=c0):
                        last = None
                        for p in range(4):
                            accb = sb["pb%d" % (2 + p)]
                            accs = bass.AP(accb, jt, [[512, 128], [64, 2], [1, 8]])
                            rngs = bass.AP(sb["rng"], 2 * p * TOK + c0 + jt,
                                           [[8 * TOK, 128], [TOK, 2], [1, 8]])
                            gngs = bass.AP(sb["gng"], 2 * p * 64 + jt,
                                           [[512, 128], [64, 2], [1, 8]])
                            wvs = bass.AP(sb["wv"], p * 128 + jt,
                                          [[512, 128], [64, 2], [1, 8]])
                            t3 = bass.AP(sb["tA"], p * 16, [[1024, 128], [8, 2], [1, 8]])
                            eng.tensor_tensor(t3, rngs, accs, Mul)
                            last = eng.tensor_tensor(wvs, t3, gngs, Sub)
                        return last
                    pl.emit("vector", fi1,
                            reads=["rng", "pb2", "pb3", "pb4", "pb5", "gng"],
                            writes=["wv", "tA0", "tA1"])
                for sp in range(7 if not SKUPD else 0):
                    n_t = 7 - sp
                    if FUPALT:
                        def fupa(eng, jt=jt, sp=sp, n_t=n_t):
                            last = None
                            for h in range(8):
                                mrow_h = bass.AP(sb["pva"], sp * 64 + h,
                                                 [[512, 128], [8, n_t]])
                                wcol = sb["wv"][:, 64 * h + jt + sp:64 * h + jt + sp + 1]
                                tgt_h = sb["wv"][:, 64 * h + jt + sp + 1:
                                                 64 * h + jt + sp + 1 + n_t]
                                last = eng.scalar_tensor_tensor(
                                    tgt_h, mrow_h, wcol, tgt_h, Mul, Add)
                            return last
                        pl.emit("vector", fupa, reads=["pva", "wv", "mball"],
                                writes=["wv"])
                        continue
                    def fup(eng, jt=jt, sp=sp, n_t=n_t):
                        mrow = bass.AP(sb["pva"], sp * 64, [[512, 128], [8, n_t], [1, 8]])
                        wrep = bass.AP(sb["wv"], jt + sp, [[512, 128], [0, n_t], [64, 8]])
                        t3 = bass.AP(sb["tB"], 0, [[1024, 128], [8, n_t], [1, 8]])
                        tgt = bass.AP(sb["wv"], jt + sp + 1, [[512, 128], [1, n_t], [64, 8]])
                        eng.tensor_tensor(t3, mrow, wrep, Mul)
                        return eng.tensor_tensor(tgt, tgt, t3, Add)
                    pl.emit("vector", fup, reads=["pva", "wv"],
                            writes=["wv", "tB0", "tB1"])
                if j < 7 and not SKBND:
                    for p in range(4):
                        bank7 = ["pb0", "pb1", "pb6", "pb7"][p]
                        accb = "pb%d" % (2 + p)
                        def ftbh(eng, p=p, bank7=bank7):
                            return eng.transpose(sb[bank7][:, 0:128],
                                                 sb["wv"][:, p * 128:(p + 1) * 128],
                                                 sb["s_ident"].ap())
                        pl.emit("tensor", ftbh, reads=["wv", "s_ident"], writes=[bank7])
                        def dump_psum(dname, bank):
                            def fcpy(eng, bank=bank):
                                return eng.tensor_copy(sb["deqb"][:, 0:128],
                                                       sb[bank][:, 0:128])
                            pl.emit("vector", fcpy, reads=[bank], writes=["deqb"])
                            dump(dname, "deqb", 128)
                        if dbg and ch == 0 and j == 0 and p == 3:
                            dump_psum("d_pb7t", bank7)
                            dump("d_wvj0", "wv", 512)
                        def fteh(eng, bank7=bank7):
                            return eng.tensor_copy(sb["wtbP"].ap(), sb[bank7][:, 0:128])
                        pl.emit("vector", fteh, reads=[bank7], writes=["wtbP"])
                        if dbg and ch == 0 and j == 0 and p == 0:
                            dump("d_wtb0", "wtbP", 128)
                        if dbg and ch == 0 and j == 0 and p == 3:
                            dump("d_wtb3", "wtbP", 128)
                        if SKFAC:
                            continue
                        def fac(eng, p=p, accb=accb):
                            return eng.matmul(sb[accb][:, 0:128], sb["wtbP"].ap(),
                                              sb["mupH"][:, p * 128:(p + 1) * 128],
                                              start=True, stop=True)
                        pl.emit("tensor", fac, reads=["wtbP", "mupH"], writes=[accb])
                        if dbg and ch == 0 and j == 0 and p == 0:
                            dump_psum("d_acc2", accb)
                        if dbg and ch == 0 and j == 0 and p == 3:
                            dump_psum("d_acc5", accb)
                        if dbg and ch == 0 and j == 1 and p == 3:
                            dump("d_wvj1b", "wv", 512)

            if SKOST:
                continue
            for p in range(4):
                bk = ["pb2", "pb3", "pb6", "pb7"][p]
                def ftw(eng, p=p, bk=bk):
                    return eng.transpose(sb[bk][:, 0:128],
                                         sb["wv"][:, p * 128:(p + 1) * 128],
                                         sb["s_ident"].ap())
                pl.emit("tensor", ftw, reads=["wv", "s_ident"], writes=[bk])
                def fwe(eng, p=p, bk=bk):
                    return eng.tensor_copy(sb[f"wthP{p}"].ap(), sb[bk][:, 0:128])
                pl.emit("vector", fwe, reads=[bk], writes=[f"wthP{p}"])

            sbanks = ["pb6", "pb7", "pb0", "pb1"]

            def emit_o(h):
                qap = kslice("qt", h, c0)
                skap = sb["sk"][64 * (h % 2):64 * (h % 2) + 64,
                                (h // 2) * 128:(h // 2) * 128 + 128]
                bank = "pb%d" % (2 + h % 4)
                def fo1(eng, skap=skap, qap=qap, bank=bank):
                    return eng.matmul(sb[bank][:, 0:64], skap, qap, start=True, stop=False)
                pl.emit("tensor", fo1, reads=["sk", "qt"], writes=[bank])
                def fo2(eng, h=h, bank=bank):
                    wth = sb["wthP%d" % (h // 2)][64 * (h % 2):64 * (h % 2) + 64, :]
                    b0 = 64 * (h % 2)
                    return eng.matmul(sb[bank][:, 0:64], wth,
                                      sb["mqn"][b0:b0 + 64, h * 64:(h + 1) * 64],
                                      start=False, stop=True)
                pl.emit("tensor", fo2, reads=["wthP%d" % (h // 2), "mqn"], writes=[bank])
                def foe(eng, h=h, bank=bank, c0=c0):
                    cp3 = sb["cpr"].ap().rearrange("p (h t) -> p h t", t=64)
                    av = sb["at"].ap().rearrange("p (h t) -> p h t", t=TOK)
                    return eng.tensor_tensor(av[:, h, c0:c0 + 64], sb[bank][:, 0:64],
                                             cp3[:, h, :], Mul)
                pl.emit("vector", foe, reads=[bank, "cpr"], writes=["at"])
                bank2 = sbanks[h % 4]
                def fs1(eng, h=h, bank2=bank2):
                    wth = sb["wthP%d" % (h // 2)][64 * (h % 2):64 * (h % 2) + 64, :]
                    kbd = sb["kthBD%d" % (h // 2)][64 * (h % 2):64 * (h % 2) + 64,
                                                   64 * (h % 2):64 * (h % 2) + 64]
                    return eng.matmul(sb[bank2][:, 0:64], wth, kbd, start=True, stop=True)
                pl.emit("tensor", fs1, reads=["wthP%d" % (h // 2), "kthBD%d" % (h // 2)],
                        writes=[bank2])

            def emit_phA(h):
                bank2 = sbanks[h % 4]
                def fsa(eng, h=h, bank2=bank2):
                    return eng.tensor_scalar(sb["tA"][:, h * 64:h * 64 + 64],
                                             sb[bank2][:, 0:64],
                                             sb["cpr"][:, h * 64 + 63:h * 64 + 64],
                                             None, Mul)
                pl.emit("vector", fsa, reads=[bank2, "cpr"], writes=["tA0"])

            def emit_phB(h):
                def fsb(eng, h=h):
                    return eng.scalar_tensor_tensor(
                        sb["sv"][:, h * 64:h * 64 + 64],
                        sb["sv"][:, h * 64:h * 64 + 64],
                        sb["cpr"][:, h * 64 + 63:h * 64 + 64],
                        sb["tA"][:, h * 64:h * 64 + 64], Mul, Sub)
                pl.emit("vector", fsb, reads=["sv", "cpr", "tA0"], writes=["sv"])

            for h in range(4):
                emit_o(h)
            for h in range(4):
                emit_phA(h)
            for h in range(4, 8):
                emit_o(h)
            for h in range(4, 8):
                emit_phA(h)
            for h in range(8):
                emit_phB(h)

        dump("d_o", "at", 8 * TOK)
        dump("d_sv", "sv", 512)
        dump("d_cpr", "cpr", 512)
        dump("d_gng", "gng", 512)
        dump("d_wv", "wv", 512)
        dump("d_pva", "pva", 512)
        dump("d_mqn", "mqn", 512)
        dump("d_mupH", "mupH", 512)
        if KPH <= 2:
            return
        for h in range(H):
            pp = h % 2
            def fsq2(eng, h=h, pp=pp):
                return eng.activation(sb["tA"][:, pp * 512:pp * 512 + 512],
                                      sb["at"][:, h * TOK:h * TOK + 512], AF.Square)
            pl.emit("scalar", fsq2, reads=["at"], writes=["tA%d" % pp])
            def fgm1(eng, h=h):
                return eng.matmul(sb["pb0"][0:1, 0:512], sb["ones"].ap(),
                                  sb["at"][:, h * TOK:h * TOK + 512], start=True, stop=True)
            pl.emit("tensor", fgm1, reads=["at", "ones"], writes=["pb0"])
            def fgm2(eng, pp=pp):
                return eng.matmul(sb["pb1"][0:1, 0:512], sb["ones"].ap(),
                                  sb["tA"][:, pp * 512:pp * 512 + 512], start=True, stop=True)
            pl.emit("tensor", fgm2, reads=["tA%d" % pp, "ones"], writes=["pb1"])
            def fgr(eng):
                st = sb["stat"]
                eng.tensor_scalar(st[0:1, 0:512], sb["pb0"][0:1, 0:512], 1.0 / DV, None, Mul)
                eng.tensor_tensor(sb["strow"][0:1, 0:512], st[0:1, 0:512], st[0:1, 0:512], Mul)
                eng.scalar_tensor_tensor(st[0:1, 512:1024], sb["pb1"][0:1, 0:512],
                                         1.0 / DV, sb["strow"][0:1, 0:512], Mul, Sub)
                eng.tensor_scalar(st[0:1, 512:1024], st[0:1, 512:1024], EPS, None, Add)
                return eng.reciprocal(st[0:1, 512:1024], st[0:1, 512:1024])
            pl.emit("vector", fgr, reads=["pb0", "pb1"], writes=["stat"])
            def fgs(eng):
                return eng.activation(sb["stat"][0:1, 1024:1536], sb["stat"][0:1, 512:1024],
                                      AF.Sqrt)
            pl.emit("scalar", fgs, reads=["stat"], writes=["stat"])
            bcast_row("pb2", sb["stat"][0:1, 0:512], 512, 0, ["stat"])
            bcast_row("pb3", sb["stat"][0:1, 1024:1536], 512, 0, ["stat"])
            def fga(eng, h=h):
                eng.tensor_tensor(sb["tB"][:, 0:512], sb["at"][:, h * TOK:h * TOK + 512],
                                  sb["pb2"][:, 0:512], Sub)
                eng.tensor_tensor(sb["tB"][:, 0:512], sb["tB"][:, 0:512],
                                  sb["pb3"][:, 0:512], Mul)
                eng.tensor_scalar(sb["tB"][:, 0:512], sb["tB"][:, 0:512],
                                  sb["s_lnw_c"].ap(), sb["s_lnb_c"].ap(), Mul, Add)
                return eng.tensor_tensor(sb["gt"][:, h * TOK:h * TOK + 512],
                                         sb["tB"][:, 0:512],
                                         sb["gt"][:, h * TOK:h * TOK + 512], Mul)
            pl.emit("vector", fga, reads=["at", "pb2", "pb3", "s_lnw_c", "s_lnb_c", "gt"],
                    writes=["tB0", "gt"])

        dump("d_gt", "gt", 8 * TOK)
        for half in range(2):
            for cc in range(4):
                c = half * 4 + cc
                def fw(eng, c=c, cc=cc):
                    return eng.dma_start(out=sb["wbuf"][:, cc * 1024:(cc + 1) * 1024],
                                         in_=dr["woT"][c * 128:(c + 1) * 128, :])
                pl.emit("sync", fw, writes=["wbuf", "wbufc%d" % (cc * 2),
                                            "wbufc%d" % (cc * 2 + 1)])
            for fi in range(8):
                bank = "pb%d" % fi
                for cc in range(4):
                    c = half * 4 + cc
                    def fm(eng, c=c, cc=cc, fi=fi, bank=bank, half=half):
                        return eng.matmul(
                            sb[bank][:, 0:512],
                            sb["wbuf"][:, cc * 1024 + fi * 128:cc * 1024 + fi * 128 + 128],
                            sb["gt"][:, c * TOK:c * TOK + 512],
                            start=(half == 0 and cc == 0), stop=(half == 1 and cc == 3))
                    pl.emit("tensor", fm,
                            reads=["wbufc%d" % (2 * cc + (1 if fi >= 4 else 0)), "gt"],
                            writes=[bank] if (half == 0 and cc == 0)
                            or (half == 1 and cc == 3) else [])
        for fi in range(8):
            bank = "pb%d" % fi
            def fres(eng, fi=fi, bank=bank):
                return eng.tensor_tensor(sb["xg"][:, fi * TOK:fi * TOK + 512],
                                         sb["xg"][:, fi * TOK:fi * TOK + 512],
                                         sb[bank][:, 0:512], Add)
            pl.emit("vector", fres, reads=[bank, "xg"], writes=["xg"])

        ln_to_zg("xg")

        def xqv():
            return sb["kt"].ap().bitcast(BF16)

        def ffqv():
            return sb["qt"].ap().bitcast(BF16)

        def part_max(eng, t, u, n, dst):
            eng.tensor_copy(u[0:64, 0:n], t[64:128, 0:n])
            eng.tensor_tensor(t[0:64, 0:n], t[0:64, 0:n], u[0:64, 0:n], MaxOp)
            eng.tensor_copy(u[0:32, 0:n], t[32:64, 0:n])
            eng.tensor_tensor(t[0:32, 0:n], t[0:32, 0:n], u[0:32, 0:n], MaxOp)
            for k in (16, 8, 4, 2, 1):
                mask = [(i + k) % 32 for i in range(32)]
                eng.stream_shuffle(u[:, 0:n], t[:, 0:n], mask)
                eng.tensor_tensor(t[0:32, 0:n], t[0:32, 0:n], u[0:32, 0:n], MaxOp)
            return eng.tensor_copy(dst, t[0:1, 0:n])

        dump("d_z2", "zg", 8 * TOK, extra=tuple("zgc%d" % c for c in range(8)))
        if KPH <= 3:
            return
        for c in range(8):
            pp = c % 2
            def fsq3(eng, c=c, pp=pp):
                return eng.activation(sb["tB"][:, pp * 512:pp * 512 + 512],
                                      sb["zg"][:, c * TOK:c * TOK + 512], AF.Square)
            pl.emit("scalar", fsq3, reads=["zgc%d" % c], writes=["tB%d" % pp])
            def fm3(eng, c=c, pp=pp):
                return eng.matmul(sb["pb1"][0:1, 0:512], sb["ones"].ap(),
                                  sb["tB"][:, pp * 512:pp * 512 + 512],
                                  start=(c == 0), stop=(c == 7))
            pl.emit("tensor", fm3, reads=["tB%d" % pp, "ones"],
                    writes=["pb1"] if c in (0, 7) else [])
            if c == 0:
                def fmx(eng, pp=pp):
                    return eng.tensor_copy(sb["tA"][:, 0:512],
                                           sb["tB"][:, pp * 512:pp * 512 + 512])
            else:
                def fmx(eng, pp=pp):
                    return eng.tensor_tensor(sb["tA"][:, 0:512], sb["tA"][:, 0:512],
                                             sb["tB"][:, pp * 512:pp * 512 + 512], MaxOp)
            pl.emit("vector", fmx, reads=["tB%d" % pp, "tA0"], writes=["tA0"])
        def ft1(eng):
            part_max(eng, sb["tA"], sb["tB"], 512, sb["stat"][0:1, 0:512])
            return eng.activation(sb["stat"][0:1, 0:512], sb["stat"][0:1, 0:512],
                                  AF.Sqrt) if False else eng.tensor_copy(
                sb["stat"][0:1, 0:512], sb["stat"][0:1, 0:512])
        pl.emit("vector", ft1, reads=["tA0"],
                writes=["tA0", "tA1", "tB0", "tB1", "stat"])
        def ft1s(eng):
            return eng.activation(sb["stat"][0:1, 0:512], sb["stat"][0:1, 0:512], AF.Sqrt)
        pl.emit("scalar", ft1s, reads=["stat"], writes=["stat"])
        def fqv(eng):
            st = sb["stat"]
            eng.tensor_scalar(sb["strow"][0:1, 0:512], sb["pb1"][0:1, 0:512], 1.0 / D, EPS,
                              Mul, Add)
            return eng.reciprocal(st[0:1, 512:1024], sb["strow"][0:1, 0:512])
        pl.emit("vector", fqv, reads=["pb1"], writes=["stat"])
        def fqs(eng):
            return eng.activation(sb["stat"][0:1, 512:1024], sb["stat"][0:1, 512:1024],
                                  AF.Sqrt)
        pl.emit("scalar", fqs, reads=["stat"], writes=["stat"])
        def fq2(eng):
            st = sb["stat"]
            eng.tensor_tensor(st[0:1, 1024:1536], st[0:1, 512:1024], st[0:1, 0:512], Mul)
            eng.tensor_scalar(st[0:1, 1024:1536], st[0:1, 1024:1536], 1e-5, None, MaxOp)
            eng.reciprocal(sb["stat2"][0:1, 0:512], st[0:1, 1024:1536])
            eng.tensor_tensor(sb["stat2"][0:1, 0:512], sb["stat2"][0:1, 0:512], st[0:1, 512:1024], Mul)
            eng.tensor_scalar(sb["stat2"][0:1, 0:512], sb["stat2"][0:1, 0:512], 127.0, None, Mul)
            return eng.tensor_scalar(sb["strow"][0:1, 0:512], st[0:1, 1024:1536],
                                     sb["m127"][0:1, 0:1], None, Mul)
        pl.emit("vector", fq2, reads=["stat", "m127"], writes=["stat", "stat2", "strow"])
        bcast_row("pb2", sb["stat2"][0:1, 0:512], 512, 0, ["stat"])
        bcast_row("pb3", sb["strow"][0:1, 0:512], 512, 0, ["strow"])
        def fdq(eng):
            return eng.tensor_copy(sb["deqb"][:, 0:512], sb["pb3"][:, 0:512])
        pl.emit("vector", fdq, reads=["pb3"], writes=["deqb"])
        for c in range(8):
            pp = c % 2
            def fz1(eng, c=c, pp=pp):
                return eng.tensor_tensor(sb["tA"][:, pp * 512:pp * 512 + 512],
                                         sb["zg"][:, c * TOK:c * TOK + 512],
                                         sb["pb2"][:, 0:512], Mul)
            pl.emit("vector", fz1, reads=["zgc%d" % c, "pb2"], writes=["tA%d" % pp])
            def fz3(eng, pp=pp):
                eng.tensor_scalar(sb["tB"][:, 0:512],
                                  sb["tA"][:, pp * 512:pp * 512 + 512],
                                  12582912.0, None, Add)
                return eng.tensor_scalar(sb["tB"][:, 0:512], sb["tB"][:, 0:512],
                                         -12582912.0, None, Add)
            pl.emit("vector", fz3, reads=["tA%d" % pp], writes=["tB0"])
            def fz4(eng):
                return eng.tensor_copy(sb["tB"].ap().bitcast(I32)[:, 512:1024], sb["tB"][:, 0:512])
            pl.emit("vector", fz4, reads=["tB0"], writes=["tB1"])
            def fz5(eng, c=c):
                return eng.tensor_scalar(xqv()[:, c * TOK:c * TOK + 512],
                                         sb["tB"].ap().bitcast(I32)[:, 512:1024], -128.0, 127.0, MaxOp, MinOp)
            pl.emit("vector", fz5, reads=["tB1"],
                    writes=["kt"] + (["xqs%d" % q for q in range(8)] if c == 0 else []))
        dump("d_xq", "kt", 2048)
        dump("d_deq", "deqb", 512, extra=("deqb2",))

        for qtr in range(TOK // 128):
            ht0 = qtr * 128
            for fs in range(8):
                hw = fs % 2
                def fwl(eng, fs=fs, hw=hw):
                    src = bass.AP(w1q.tensor, fs * 512,
                                  [[4096, 128], [524288, 8], [1, 512]])
                    return eng.dma_start(
                        out=sb["wbuf"].ap().bitcast(BF16)[:, hw * 4096:hw * 4096 + 4096],
                        in_=src)
                pl.emit("sync", fwl,
                        writes=["wbuf"] + ["wbufc%d" % (hw * 4 + i) for i in range(4)],
                        extra_waits=[("dG", wq_done)])
                for fi in range(4):
                    bank = "pb%d" % (4 + fi % 2)
                    for c in range(8):
                        def fm(eng, c=c, fi=fi, bank=bank, ht0=ht0, hw=hw):
                            wb = sb["wbuf"].ap().bitcast(BF16)
                            return eng.matmul(
                                sb[bank][:, 0:128],
                                wb[:, hw * 4096 + c * 512 + fi * 128:
                                   hw * 4096 + c * 512 + fi * 128 + 128],
                                xqv()[:, c * TOK + ht0:c * TOK + ht0 + 128],
                                start=(c == 0), stop=(c == 7))
                        pl.emit("tensor", fm,
                                reads=["wbufc%d" % (hw * 4 + c // 2), "kt"],
                                writes=[bank] if c in (0, 7) else [])
                    fidx = fs * 4 + fi
                    def fev(eng, fidx=fidx, bank=bank, ht0=ht0):
                        return eng.tensor_tensor(sb["ffh"][:, fidx * 128:(fidx + 1) * 128],
                                                 sb[bank][:, 0:128],
                                                 sb["deqb"][:, ht0:ht0 + 128], Mul)
                    pl.emit("vector", fev, reads=[bank, "deqb"], writes=["vt"])
                    def fge(eng, fidx=fidx):
                        return eng.activation(sb["ffh"][:, fidx * 128:(fidx + 1) * 128],
                                              sb["ffh"][:, fidx * 128:(fidx + 1) * 128],
                                              AF.Gelu, bias=sb["s_b1_c"][:, fidx:fidx + 1])
                    pl.emit("scalar", fge, reads=["vt", "s_b1_c"], writes=["vt"])

            if qtr == 0:
                dump("d_ffh", "vt", 4096)
            for cg in range(8):
                for ci in range(4):
                    c = cg * 4 + ci
                    def fsq4(eng, c=c, ci=ci):
                        return eng.activation(sb["tB"][:, ci * 128:ci * 128 + 128],
                                              sb["ffh"][:, c * 128:(c + 1) * 128], AF.Square)
                    pl.emit("scalar", fsq4, reads=["vt"], writes=["tB0"])
                    def fm4(eng, c=c, ci=ci):
                        return eng.matmul(sb["pb1"][0:1, 0:128], sb["ones"].ap(),
                                          sb["tB"][:, ci * 128:ci * 128 + 128],
                                          start=(c == 0), stop=(c == 31))
                    pl.emit("tensor", fm4, reads=["tB0", "ones"],
                            writes=["pb1"] if c in (0, 31) else [])
                if cg == 0:
                    def fmx2(eng):
                        return eng.tensor_copy(sb["tA"][:, 0:512], sb["tB"][:, 0:512])
                else:
                    def fmx2(eng):
                        return eng.tensor_tensor(sb["tA"][:, 0:512], sb["tA"][:, 0:512],
                                                 sb["tB"][:, 0:512], MaxOp)
                pl.emit("vector", fmx2, reads=["tB0", "tA0"], writes=["tA0"])
            def fht(eng):
                return part_max(eng, sb["tA"], sb["tB"], 512, sb["stat2"][0:1, 0:512])
            pl.emit("vector", fht, reads=["tA0"],
                    writes=["tA0", "tA1", "tB0", "tB1", "stat2"])
            def fspc1(eng):
                return eng.tensor_copy(sb["shufM"][:, 0:512], sb["rng"][:, 0:512])
            pl.emit("vector", fspc1, reads=["rng"], writes=["shufM"])
            def fr1v(eng):
                return eng.tensor_tensor(sb["strow"][0:1, 256:512],
                                         sb["stat2"][0:1, 0:256],
                                         sb["stat2"][0:1, 256:512], MaxOp)
            pl.emit("vector", fr1v, reads=["stat2"], writes=["strow"])
            def fspc2(eng):
                return eng.tensor_copy(sb["shufM"][:, 0:512], sb["rng"][:, 0:512])
            pl.emit("vector", fspc2, reads=["rng"], writes=["shufM"])
            def fr2v(eng):
                return eng.tensor_tensor(sb["stat"][0:1, 0:128],
                                         sb["strow"][0:1, 256:384],
                                         sb["strow"][0:1, 384:512], MaxOp)
            pl.emit("vector", fr2v, reads=["strow"], writes=["stat"])
            def fhts(eng):
                return eng.activation(sb["stat"][0:1, 0:128], sb["stat"][0:1, 0:128], AF.Sqrt)
            pl.emit("scalar", fhts, reads=["stat"], writes=["stat"])
            def fqv2(eng):
                st = sb["stat"]
                eng.tensor_scalar(sb["strow"][0:1, 0:128], sb["pb1"][0:1, 0:128], 1.0 / FF, EPS,
                                  Mul, Add)
                return eng.reciprocal(st[0:1, 512:640], sb["strow"][0:1, 0:128])
            pl.emit("vector", fqv2, reads=["pb1"], writes=["stat"])
            def fqs2(eng):
                return eng.activation(sb["stat"][0:1, 512:640], sb["stat"][0:1, 512:640],
                                      AF.Sqrt)
            pl.emit("scalar", fqs2, reads=["stat"], writes=["stat"])
            def fq22(eng):
                st = sb["stat"]
                eng.tensor_tensor(st[0:1, 1024:1152], st[0:1, 512:640], st[0:1, 0:128], Mul)
                eng.tensor_scalar(st[0:1, 1024:1152], st[0:1, 1024:1152], 1e-5, None, MaxOp)
                eng.reciprocal(sb["stat2"][0:1, 0:128], st[0:1, 1024:1152])
                eng.tensor_tensor(sb["stat2"][0:1, 0:128], sb["stat2"][0:1, 0:128],
                                  st[0:1, 512:640], Mul)
                eng.tensor_scalar(sb["stat2"][0:1, 0:128], sb["stat2"][0:1, 0:128], 127.0, None, Mul)
                return eng.tensor_scalar(sb["strow"][0:1, 0:128], st[0:1, 1024:1152],
                                         sb["m227"][0:1, 0:1], None, Mul)
            pl.emit("vector", fq22, reads=["stat", "m227"], writes=["stat", "stat2", "strow"])
            bcast_row("pb2", sb["stat2"][0:1, 0:128], 128, 0, ["stat"])
            bcast_row("pb3", sb["strow"][0:1, 0:128], 128, 0, ["strow"])
            def fdq2(eng):
                return eng.tensor_copy(sb["deqb"][:, 0:128], sb["pb3"][:, 0:128])
            pl.emit("vector", fdq2, reads=["pb3"], writes=["deqb2"])
            if qtr == 0:
                dump("d_dq2", "deqb", 128, extra=("deqb2",))
            for cg in range(8):
                c4 = cg * 4
                def fz1b(eng, c4=c4):
                    last = None
                    for ci in range(4):
                        last = eng.tensor_tensor(
                            sb["tA"][:, ci * 128:ci * 128 + 128],
                            sb["ffh"][:, (c4 + ci) * 128:(c4 + ci + 1) * 128],
                            sb["pb2"][:, 0:128], Mul)
                    return last
                pl.emit("vector", fz1b, reads=["vt", "pb2"], writes=["tA0"])
                def fz3b(eng):
                    eng.tensor_scalar(sb["tB"][:, 0:512], sb["tA"][:, 0:512],
                                      12582912.0, None, Add)
                    return eng.tensor_scalar(sb["tB"][:, 0:512], sb["tB"][:, 0:512],
                                             -12582912.0, None, Add)
                pl.emit("vector", fz3b, reads=["tA0"], writes=["tB0"])
                def fz4b(eng):
                    return eng.tensor_copy(sb["tB"].ap().bitcast(I32)[:, 512:1024],
                                           sb["tB"][:, 0:512])
                pl.emit("vector", fz4b, reads=["tB0"], writes=["tB1"])
                def fz5b(eng, c4=c4):
                    return eng.tensor_scalar(ffqv()[:, c4 * 128:(c4 + 4) * 128],
                                             sb["tB"].ap().bitcast(I32)[:, 512:1024],
                                             -128.0, 127.0, MaxOp, MinOp)
                pl.emit("vector", fz5b, reads=["tB1"], writes=["qt"])
            if qtr == 0:
                dump("d_ffq", "qt", 2048)

            for fs in range(8):
                hw = fs % 2
                def fwl2(eng, fs=fs, hw=hw):
                    src = bass.AP(w2q.tensor, fs * 524288,
                                  [[1024, 128], [131072, 4], [1, 1024]])
                    return eng.dma_start(
                        out=sb["wbuf"].ap().bitcast(BF16)[:, hw * 4096:hw * 4096 + 4096],
                        in_=src)
                pl.emit("sync", fwl2,
                        writes=["wbuf"] + ["wbufc%d" % (hw * 4 + i) for i in range(4)])
                for di in range(8):
                    bank = "pb%d" % di
                    for cc in range(4):
                        c = fs * 4 + cc
                        def fm5(eng, cc=cc, di=di, c=c, bank=bank, fs=fs, hw=hw):
                            wb = sb["wbuf"].ap().bitcast(BF16)
                            return eng.matmul(
                                sb[bank][:, 0:128],
                                wb[:, hw * 4096 + cc * 1024 + di * 128:
                                   hw * 4096 + cc * 1024 + di * 128 + 128],
                                ffqv()[:, c * 128:(c + 1) * 128],
                                start=(fs == 0 and cc == 0), stop=(fs == 7 and cc == 3))
                        pl.emit("tensor", fm5,
                                reads=["wbufc%d" % (hw * 4 + cc), "qt"],
                                writes=[bank] if ((fs == 0 and cc == 0)
                                                  or (fs == 7 and cc == 3)) else [])
            for di in range(8):
                bank = "pb%d" % di
                def foA(eng, di=di, bank=bank):
                    return eng.tensor_tensor(sb["tA"][:, di * 128:di * 128 + 128],
                                             sb[bank][:, 0:128],
                                             sb["deqb"][:, 0:128], Mul)
                pl.emit("vector", foA, reads=[bank, "deqb2"],
                        writes=["tA0" if di < 4 else "tA1"])
            for di in range(8):
                def foB(eng, di=di, ht0=ht0):
                    return eng.scalar_tensor_tensor(
                        sb["xg"][:, di * TOK + ht0:di * TOK + ht0 + 128],
                        sb["tA"][:, di * 128:di * 128 + 128],
                        sb["s_b2_c"][:, di:di + 1],
                        sb["xg"][:, di * TOK + ht0:di * TOK + ht0 + 128], Add, Add)
                pl.emit("vector", foB, reads=["tA0", "tA1", "s_b2_c", "xg"],
                        writes=["xg"])

        pl.count["dO"] = pl.count.get("dO", 0)
        for c in range(8):
            def fod(eng, c=c, t0=t0):
                return eng.dma_start(out=out_ap[c * 128:(c + 1) * 128, t0:t0 + TOK],
                                     in_=sb["xg"][:, c * TOK:(c + 1) * TOK])
            sem = "dO"
            waits = {}
            r = pl._get("xg")
            if r["w"] is not None:
                s, v = r["w"]
                waits[s] = max(waits.get(s, 0), v)
            pl.count[sem] += 16
            r["r"].append((sem, pl.count[sem]))
            pl.steps["gpsimd"].append((fod, sorted(waits.items())))

    for g in range(GROUPS):
        group_body(g)
    if pl.count.get("dO", 0) == 0:
        for c in range(8):
            def fod(eng, c=c):
                return eng.dma_start(out=out_ap[c * 128:(c + 1) * 128, 0:TOK],
                                     in_=sb["xg"][:, c * TOK:(c + 1) * TOK])
            waits = {}
            r = pl._get("xg")
            if r["w"] is not None:
                sx, vx = r["w"]
                waits[sx] = vx
            pl.count["dO"] += 16
            r["r"].append(("dO", pl.count["dO"]))
            pl.steps["gpsimd"].append((fod, sorted(waits.items())))

    fin_waits = [("dO", pl.count["dO"])]
    if pl.count.get("dD", 0):
        fin_waits.append(("dD", pl.count["dD"]))
    pl.steps["gpsimd"].append((None, fin_waits))

    sem_objs = {}
    sem_guards = []
    for s in sorted(set(list(ENG_SEM.values()) + ["dO", "dD"])):
        gd = nc.semaphore(s)
        sem_objs[s] = gd.__enter__()
        sem_guards.append(gd)
    blk = nc.Block()
    block = blk.__enter__()

    def stream(eng_name):
        def body(eng):
            own = ENG_SEM[eng_name]
            amt = ENG_AMT[eng_name]
            for item in pl.steps[eng_name]:
                fn, waits = item
                for (s, v) in waits:
                    eng.wait_ge(sem_objs[s], v)
                if fn is None:
                    continue
                inst = fn(eng)
                if fn.__name__ in ("fod", "fn") and eng_name in ("gpsimd", "scalar")                         and getattr(fn, "_dma_out", False):
                    inst.then_inc(sem_objs["dD"], 16)
                elif eng_name == "gpsimd" and fn.__name__ == "fod":
                    inst.then_inc(sem_objs["dO"], 16)
                elif inst is None:
                    eng.sem_inc(sem_objs[own], amt)
                else:
                    inst.then_inc(sem_objs[own], amt)
        return body

    block.sync(stream("sync"))
    block.tensor(stream("tensor"))
    block.vector(stream("vector"))
    block.scalar(stream("scalar"))
    block.gpsimd(stream("gpsimd"))

    blk.__exit__(None, None, None)
    if _RACE_HOOK is not None:
        _RACE_HOOK(pl, sb, nc)
    for gd in reversed(sem_guards):
        gd.__exit__(None, None, None)
    for gd in reversed(ctx_stack):
        gd.__exit__(None, None, None)
    return nc


def bank_ap(x):
    return x


def _host_inputs(core_x, w):
    d = {}
    d["xT"] = np.ascontiguousarray(core_x.T)
    for nm, key in (("wqT", "Wq"), ("wkT", "Wk"), ("wvT", "Wv"), ("waT", "Wa"),
                    ("wbT", "Wb"), ("wgT", "Wg"), ("woT", "Wo"), ("w1T", "w1"),
                    ("w2T", "w2")):
        d[nm] = np.ascontiguousarray(w[key].T)
    d["ba_c"] = np.ascontiguousarray(w["ba"].reshape(8, 128).T)
    d["bb_c"] = np.ascontiguousarray(w["bb"].reshape(8, 1))
    d["b1_c"] = np.ascontiguousarray(w["b1"].reshape(32, 128).T)
    d["b2_c"] = np.ascontiguousarray(w["b2"].reshape(8, 128).T)
    d["qcw_c"] = np.ascontiguousarray(w["qc_w"].reshape(4, 128, 4).transpose(1, 0, 2).reshape(128, 16))
    d["kcw_c"] = np.ascontiguousarray(w["kc_w"].reshape(4, 128, 4).transpose(1, 0, 2).reshape(128, 16))
    d["vcw_c"] = np.ascontiguousarray(w["vc_w"].reshape(8, 128, 4).transpose(1, 0, 2).reshape(128, 32))
    d["qcb_c"] = np.ascontiguousarray(w["qc_b"].reshape(4, 128).T)
    d["kcb_c"] = np.ascontiguousarray(w["kc_b"].reshape(4, 128).T)
    d["vcb_c"] = np.ascontiguousarray(w["vc_b"].reshape(8, 128).T)
    d["lnw_c"] = np.ascontiguousarray(w["ln_w"].reshape(128, 1))
    d["lnb_c"] = np.ascontiguousarray(w["ln_b"].reshape(128, 1))
    d["ident"] = np.eye(128, dtype=np.float32)
    s = np.arange(64)[:, None]
    t = np.arange(64)[None, :]
    d["maskU"] = np.ascontiguousarray(np.tile((s < t).astype(np.float32), (1, 8)))
    d["maskQn"] = np.ascontiguousarray(np.tile(-(s <= t).astype(np.float32), (1, 8)))
    d["ones_c"] = np.ones((128, 1), np.float32)
    d["onesr_c"] = np.ones((128, 128), np.float32)
    sel = np.zeros((64, 1024), np.float32)
    for i in range(8):
        sel[i, i * 128:(i + 1) * 128] = 1.0
        sel[32 + i, i * 128:(i + 1) * 128] = 1.0
    d["sel8_c"] = sel
    for k in d:
        d[k] = np.ascontiguousarray(d[k], dtype=np.float32)
    return d


def kernel(L0_attn_x, L0_attn_Wq, L0_attn_Wk, L0_attn_Wv, L0_attn_Wa, L0_attn_ba,
           L0_attn_Wb, L0_attn_bb, L0_attn_Wg, L0_attn_Wo, L0_attn_qc_w, L0_attn_qc_b,
           L0_attn_kc_w, L0_attn_kc_b, L0_attn_vc_w, L0_attn_vc_b, L0_attn_ln_w,
           L0_attn_ln_b, L0_ffn_w1, L0_ffn_b1, L0_ffn_w2, L0_ffn_b2):
    B, T, _ = L0_attn_x.shape
    w = {"Wq": L0_attn_Wq, "Wk": L0_attn_Wk, "Wv": L0_attn_Wv, "Wa": L0_attn_Wa,
         "ba": L0_attn_ba, "Wb": L0_attn_Wb, "bb": L0_attn_bb, "Wg": L0_attn_Wg,
         "Wo": L0_attn_Wo, "qc_w": np.asarray(L0_attn_qc_w).reshape(512, 4),
         "qc_b": L0_attn_qc_b, "kc_w": np.asarray(L0_attn_kc_w).reshape(512, 4),
         "kc_b": L0_attn_kc_b, "vc_w": np.asarray(L0_attn_vc_w).reshape(1024, 4),
         "vc_b": L0_attn_vc_b, "ln_w": L0_attn_ln_w, "ln_b": L0_attn_ln_b,
         "w1": L0_ffn_w1, "b1": L0_ffn_b1, "w2": L0_ffn_w2, "b2": L0_ffn_b2}
    w = {k: np.asarray(v, dtype=np.float32) for k, v in w.items()}

    TOK = 512
    GROUPS = T // TOK
    nc = build_program(TOK, GROUPS)
    in_maps = [_host_inputs(np.asarray(L0_attn_x[b], np.float32), w) for b in range(B)]
    res = run_bass_kernel_spmd(nc, in_maps, core_ids=list(range(B)))
    out = np.stack([np.asarray(res.results[b]["outT"], np.float32).T for b in range(B)])
    return out


# version 04
# speedup vs baseline: 1.2262x; 1.0820x over previous
import numpy as np
import concourse.bass as bass
import concourse.mybir as mybir
from concourse.alu_op_type import AluOpType
from concourse.bass_utils import run_bass_kernel_spmd

F32 = mybir.dt.float32
BF16 = mybir.dt.bfloat16
I32 = mybir.dt.int32
AF = mybir.ActivationFunctionType
AX = mybir.AxisListType.X
Mul = AluOpType.mult
Add = AluOpType.add
Sub = AluOpType.subtract
Div = AluOpType.divide
MaxOp = AluOpType.max
MinOp = AluOpType.min
AMax = AluOpType.abs_max

H, DK, DV = 8, 64, 128
D, FF = 1024, 4096
EPS = 1e-5
_RACE_HOOK = None

ENG_SEM = {"sync": "dI", "tensor": "tT", "vector": "tV", "scalar": "tS",
           "gpsimd": "dG"}
ENG_AMT = {"sync": 16, "tensor": 1, "vector": 1, "scalar": 1, "gpsimd": 16}


class Plan:

    def __init__(self):
        self.steps = {e: [] for e in ENG_SEM}
        self.count = {s: 0 for s in set(ENG_SEM.values())}
        self.res = {}

    def _get(self, name):
        return self.res.setdefault(name, {"w": None, "r": []})

    def emit(self, eng, fn, reads=(), writes=(), extra_waits=()):
        sem = ENG_SEM[eng]
        amt = ENG_AMT[eng]
        waits = {}
        for nm in reads:
            r = self._get(nm)
            if r["w"] is not None:
                s, v = r["w"]
                waits[s] = max(waits.get(s, 0), v)
        for nm in writes:
            r = self._get(nm)
            if r["w"] is not None:
                s, v = r["w"]
                waits[s] = max(waits.get(s, 0), v)
            for (s, v) in r["r"]:
                waits[s] = max(waits.get(s, 0), v)
        for (s, v) in extra_waits:
            waits[s] = max(waits.get(s, 0), v)
        waits.pop(sem, None)
        self.count[sem] += amt
        post = self.count[sem]
        for nm in reads:
            self._get(nm)["r"].append((sem, post))
        for nm in writes:
            r = self._get(nm)
            r["w"] = (sem, post)
            r["r"] = []
        self.steps[eng].append((fn, sorted(waits.items())))
        return post


def build_program(TOK, GROUPS, dbg=False):
    import os as _os
    KPH = int(_os.environ.get("KPHASE", "9"))
    NCHLIM = int(_os.environ.get("NCHLIM", "99"))
    SKMB = _os.environ.get("SKMB") == "1"
    SKSOL = _os.environ.get("SKSOL") == "1"
    SKBND = _os.environ.get("SKBND") == "1"
    SKOST = _os.environ.get("SKOST") == "1"
    SKFAC = _os.environ.get("SKFAC") == "1"
    SKUPD = _os.environ.get("SKUPD") == "1"
    SKACC = _os.environ.get("SKACC") == "1"
    FUPALT = _os.environ.get("FUPOLD") != "1"
    FI1ALT = _os.environ.get("FI1OLD") != "1"
    SKMB2 = _os.environ.get("SKMB2") == "1"

    T_ALL = TOK * GROUPS
    NCH = TOK // 64
    nc = bass.Bass("TRN2", target_bir_lowering=False,
                   detect_race_conditions=(_os.environ.get("RACEDET") == "1"))

    dr = {}

    def din(name, shape, dtype=F32):
        dr[name] = nc.dram_tensor(name, list(shape), dtype, kind="ExternalInput").ap()

    din("xT", (D, T_ALL))
    din("wqT", (D, 512)); din("wkT", (D, 512)); din("wvT", (D, 1024))
    din("waT", (D, 1024)); din("wbT", (D, 8)); din("wgT", (D, 1024))
    din("woT", (1024, D))
    din("w1T", (D, FF)); din("w2T", (FF, D))
    din("ba_c", (128, 8)); din("bb_c", (8, 1)); din("b1_c", (128, 32)); din("b2_c", (128, 8))
    din("qcw_c", (128, 16)); din("kcw_c", (128, 16)); din("vcw_c", (128, 32))
    din("qcb_c", (128, 4)); din("kcb_c", (128, 4)); din("vcb_c", (128, 8))
    din("lnw_c", (128, 1)); din("lnb_c", (128, 1))
    din("ident", (128, 128)); din("maskU", (64, 512)); din("maskQn", (64, 512))
    din("ones_c", (128, 1)); din("onesr_c", (128, 128)); din("sel8_c", (64, 1024))
    out_ap = nc.dram_tensor("outT", [D, T_ALL], F32, kind="ExternalOutput").ap()
    dbg_aps = {}
    if dbg:
        for n, shape in (("d_zg", (128, 8 * TOK)), ("d_qt", (128, 4 * TOK)),
                         ("d_kt", (128, 4 * TOK)), ("d_vt", (128, 8 * TOK)),
                         ("d_at", (128, 8 * TOK)), ("d_o", (128, 8 * TOK)),
                         ("d_gt", (128, 8 * TOK)), ("d_sv", (128, 512)),
                         ("d_rng", (128, 8 * TOK)), ("d_z2", (128, 8 * TOK)),
                         ("d_cpr", (128, 512)), ("d_gng", (128, 512)),
                         ("d_wv", (128, 512)), ("d_pva", (128, 512)),
                         ("d_mqn", (128, 512)), ("d_mupH", (128, 512)),
                         ("d_pb7t", (128, 128)), ("d_wtb3", (128, 128)),
                         ("d_acc5", (128, 128)), ("d_wtb0", (128, 128)),
                         ("d_acc2", (128, 128)), ("d_wvj0", (128, 512)),
                         ("d_wvj1", (128, 512)), ("d_wvj1b", (128, 512)),
                         ("d_xq", (128, 2048)), ("d_ffh", (128, 4096)),
                         ("d_ffq", (128, 2048)), ("d_deq", (128, 512)),
                         ("d_dq2", (128, 128)), ("d_m1", (1, 1)), ("d_m2", (1, 1))):
            dbg_aps[n] = nc.dram_tensor(n, list(shape), F32, kind="ExternalOutput").ap()
        dbg_aps["d_w1q"] = nc.dram_tensor("d_w1q", [128, 4096], BF16,
                                          kind="ExternalOutput").ap()
    w1q = nc.dram_tensor("w1q", [D, FF], BF16, kind="Internal").ap()
    w2q = nc.dram_tensor("w2q", [FF, D], BF16, kind="Internal").ap()

    pl = Plan()
    pl.count["dO"] = 0
    pl.count["dD"] = 0
    ctx_stack = []
    sb = {}

    def sbuf(name, shape, dtype=F32):
        g = nc.sbuf_tensor(name, list(shape), dtype)
        sb[name] = g.__enter__()
        ctx_stack.append(g)

    def psum(name, shape):
        g = nc.psum_tensor(name, list(shape), F32)
        sb[name] = g.__enter__()
        ctx_stack.append(g)

    sbuf("xg", (128, 8 * TOK))
    sbuf("zg", (128, 8 * TOK))
    sbuf("pre", (128, 8 * (TOK + 4)))
    sbuf("qhal", (128, 16))
    sbuf("khal", (128, 16))
    sbuf("vhal", (128, 32))
    sbuf("qt", (128, 4 * TOK))
    sbuf("kt", (128, 4 * TOK))
    sbuf("vt", (128, 8 * TOK))
    sbuf("at", (128, 8 * TOK))
    sbuf("gt", (128, 8 * TOK))
    sbuf("bb8", (8, TOK))
    sbuf("rng", (128, 8 * TOK))
    sbuf("cpr", (128, 512))
    sbuf("gng", (128, 512))
    sbuf("pva", (128, 512))
    sbuf("wv", (128, 512))
    sbuf("mup", (128, 512))
    sbuf("mupH", (128, 512))
    sbuf("mqn", (128, 512))
    sbuf("wtbP", (128, 128))
    for p in range(1, 4):
        sbuf(f"wtbP{p}", (128, 128))
    sbuf("kstage", (128, 128))
    for p in range(4):
        sbuf(f"wthP{p}", (128, 128))
        sbuf(f"kthBD{p}", (128, 128))
    sbuf("sv", (128, 8 * DK))
    sbuf("sk", (128, 4 * 128))
    sbuf("s_ident", (128, 128))
    sbuf("onesrow", (128, 128))
    sbuf("s_maskU", (64, 512))
    sbuf("s_maskQn", (64, 512))
    sbuf("ones", (128, 1))
    sbuf("sel8", (64, 1024))
    sbuf("shufM", (128, 512))
    for n, shp in (("s_ba_c", (128, 8)), ("s_bb_c", (8, 1)), ("s_b1_c", (128, 32)),
                   ("s_b2_c", (128, 8)), ("s_qcw_c", (128, 16)), ("s_kcw_c", (128, 16)),
                   ("s_vcw_c", (128, 32)), ("s_qcb_c", (128, 4)), ("s_kcb_c", (128, 4)),
                   ("s_vcb_c", (128, 8)), ("s_lnw_c", (128, 1)), ("s_lnb_c", (128, 1))):
        sbuf(n, shp)
    sbuf("m127", (1, 1))
    sbuf("m227", (1, 1))
    sbuf("swcol", (128, 1))
    sbuf("stat", (1, 1536))
    sbuf("stat2", (1, 512))
    sbuf("strow", (1, 512))
    sbuf("deqb", (128, 512))
    sbuf("wbuf", (128, 4096))
    sbuf("sgn", (128, 512), BF16)
    sbuf("tA", (128, 1024))
    sbuf("tB", (128, 1024))
    sb["ffh"] = sb["vt"]
    for i in range(8):
        psum(f"pb{i}", (128, 512))

    _SBN = {n: "s_" + n for n in ("ident", "maskU", "maskQn", "ba_c", "bb_c", "b1_c", "b2_c", "qcw_c", "kcw_c", "vcw_c", "qcb_c", "kcb_c", "vcb_c", "lnw_c", "lnb_c")}

    def ld_const(name):
        sn = _SBN.get(name, name)
        def fn(eng, name=name, sn=sn):
            return eng.dma_start(out=sb[sn].ap(), in_=dr[name])
        pl.emit("sync", fn, writes=[sn])

    def onesrow_ap(p):
        return sb["onesrow"][p:p + 1, 0:128]

    def bcast_row(bank_name, row_ap, ncols, src_part, reads, pname=None):
        def fn(eng):
            return eng.matmul(sb[bank_name][:, 0:ncols], onesrow_ap(src_part), row_ap,
                              start=True, stop=True)
        pl.emit("tensor", fn, reads=list(reads) + ["onesrow"], writes=[bank_name])

    for n in ("ident", "maskU", "maskQn", "ba_c", "bb_c", "b1_c", "b2_c", "qcw_c",
              "kcw_c", "vcw_c", "qcb_c", "kcb_c", "vcb_c", "lnw_c", "lnb_c"):
        ld_const(n)
    def f_lo(eng):
        return eng.dma_start(out=sb["ones"].ap(), in_=dr["ones_c"])
    pl.emit("sync", f_lo, writes=["ones"])
    def f_lor(eng):
        return eng.dma_start(out=sb["onesrow"].ap(), in_=dr["onesr_c"])
    pl.emit("sync", f_lor, writes=["onesrow"])
    def f_sel(eng):
        return eng.dma_start(out=sb["sel8"].ap(), in_=dr["sel8_c"])
    pl.emit("sync", f_sel, writes=["sel8"])

    def fmemset(name, val, region=None):
        def fn(eng, name=name, val=val, region=region):
            a = sb[name].ap() if region is None else region
            return eng.memset(a, val)
        pl.emit("vector", fn, writes=[name])

    fmemset("sv", 0.0)
    fmemset("mup", 0.0)
    fmemset("kstage", 0.0)
    fmemset("mupH", 0.0)
    for nm in ("qhal", "khal", "vhal"):
        fmemset(nm, 0.0)

    def quant_weights(wsrc, wdst, n_rows, n_cols, mname):
        nslab = n_rows // 128
        for s in range(nslab):
            def fl(eng, s=s):
                return eng.dma_start(out=sb["wbuf"][:, 0:n_cols],
                                     in_=wsrc[s * 128:(s + 1) * 128, :])
            pl.emit("sync", fl, writes=["wbuf"] + ["wbufc%d" % i for i in range(8)])
            for qq in range(n_cols // 512):
                def fa(eng, qq=qq):
                    return eng.activation(sb["tA"][:, 0:512],
                                          sb["wbuf"][:, qq * 512:qq * 512 + 512],
                                          AF.Abs)
                pl.emit("scalar", fa, reads=["wbufc%d" % qq], writes=["tA0"])
                for k4 in range(4):
                    first = (s == 0 and qq == 0 and k4 == 0)
                    last = (s == nslab - 1 and qq == n_cols // 512 - 1 and k4 == 3)
                    def fm(eng, first=first, last=last, k4=k4):
                        return eng.matmul(sb["pb7"][0:1, 0:128], sb["ones"].ap(),
                                          sb["tA"][:, k4 * 128:(k4 + 1) * 128],
                                          start=first, stop=last)
                    pl.emit("tensor", fm, reads=["tA0", "ones"],
                            writes=["pb7"] if (first or last) else [])
        def frow(eng):
            return eng.tensor_copy(sb["stat"][0:1, 16:144], sb["pb7"][0:1, 0:128])
        pl.emit("vector", frow, reads=["pb7"], writes=["stat"])
        def fx2(eng):
            return eng.matmul(sb["pb6"][:, 0:1], sb["stat"][0:1, 16:144],
                              onesrow_ap(0)[0:1, 0:1], start=True, stop=True)
        pl.emit("tensor", fx2, reads=["stat", "onesrow"], writes=["pb6"])
        def fcp1(eng):
            return eng.tensor_copy(sb["swcol"].ap(), sb["pb6"][:, 0:1])
        pl.emit("vector", fcp1, reads=["pb6"], writes=["swcol"])
        def fx3(eng):
            return eng.matmul(sb["pb7"][0:1, 4:5], sb["swcol"].ap(),
                              sb["ones"].ap(), start=True, stop=True)
        pl.emit("tensor", fx3, reads=["swcol", "ones"], writes=["pb7"])
        def fr1q(eng, n_rows=n_rows, n_cols=n_cols):
            return eng.tensor_scalar(sb["stat"][0:1, 1:2], sb["pb7"][0:1, 4:5],
                                     1.0 / (n_rows * n_cols), 1e-5, Mul, MaxOp)
        pl.emit("vector", fr1q, reads=["pb7"], writes=["stat"])
        def fspq(eng):
            return eng.tensor_copy(sb["shufM"][0:64, 0:512], sb["s_maskU"].ap())
        pl.emit("vector", fspq, reads=["s_maskU"], writes=["shufM"])
        def fr2q(eng, mname=mname):
            eng.tensor_scalar(sb[mname].ap(), sb["stat"][0:1, 1:2], 1.0 / 127.0, None, Mul)
            return eng.reciprocal(sb["stat"][0:1, 2:3], sb["stat"][0:1, 1:2])
        pl.emit("vector", fr2q, reads=["stat"], writes=["stat", mname])
        bcast_row("pb6", sb["stat"][0:1, 2:3], 1, 0, ["stat"])
        def fc(eng):
            return eng.tensor_copy(sb["swcol"].ap(), sb["pb6"][:, 0:1])
        pl.emit("vector", fc, reads=["pb6"], writes=["swcol"])
        for s in range(nslab):
            def fl2(eng, s=s):
                return eng.dma_start(out=sb["wbuf"][:, 0:n_cols],
                                     in_=wsrc[s * 128:(s + 1) * 128, :])
            pl.emit("sync", fl2, writes=["wbuf"] + ["wbufc%d" % i for i in range(8)])
            for qq in range(n_cols // 512):
                pp = qq % 2
                def fmu(eng, qq=qq, pp=pp):
                    return eng.mul(sb["tA"][:, pp * 512:pp * 512 + 512],
                                   sb["wbuf"][:, qq * 512:qq * 512 + 512],
                                   sb["swcol"].ap())
                pl.emit("scalar", fmu, reads=["wbufc%d" % qq, "swcol"],
                        writes=["tA%d" % pp])
                def fst(eng, pp=pp):
                    eng.tensor_scalar(sb["tB"][:, 0:512],
                                      sb["tA"][:, pp * 512:pp * 512 + 512],
                                      12582912.0, None, Add)
                    return eng.tensor_scalar(sb["tB"][:, 0:512], sb["tB"][:, 0:512],
                                             -12582912.0, None, Add)
                pl.emit("vector", fst, reads=["tA%d" % pp], writes=["tB0"])
                def fci(eng):
                    return eng.tensor_copy(sb["tB"].ap().bitcast(I32)[:, 512:1024], sb["tB"][:, 0:512])
                pl.emit("vector", fci, reads=["tB0"], writes=["tB1"])
                def fcl(eng, qq=qq):
                    xqv = sb["kt"].ap().bitcast(BF16)
                    return eng.tensor_scalar(xqv[:, qq * 512:qq * 512 + 512],
                                             sb["tB"].ap().bitcast(I32)[:, 512:1024], -1.0, 1.0, MaxOp, MinOp)
                pl.emit("vector", fcl, reads=["tB1"], writes=["xqs%d" % qq])
            def fw(eng, s=s, n_cols=n_cols, wdst=wdst):
                xqv = sb["kt"].ap().bitcast(BF16)
                return eng.dma_start(out=wdst[s * 128:(s + 1) * 128, :],
                                     in_=xqv[:, 0:n_cols])
            pl.emit("gpsimd", fw,
                    reads=["xqs%d" % q for q in range(n_cols // 512)] + ["kt"])

    quant_weights(dr["w1T"], w1q, D, FF, "m127")
    quant_weights(dr["w2T"], w2q, FF, D, "m227")
    wq_done = pl.count["dG"]
    if dbg:
        def fwl0(eng):
            return eng.dma_start(out=sb["wbuf"].ap().bitcast(BF16)[:, 0:4096],
                                 in_=w1q[0:128, :])
        pl.emit("sync", fwl0, writes=["wbuf"] + ["wbufc%d" % i for i in range(8)],
                extra_waits=[("dG", wq_done)])
        def fn(eng):
            return eng.dma_start(out=dbg_aps["d_w1q"],
                                 in_=sb["wbuf"].ap().bitcast(BF16)[:, 0:4096])
        fn._dma_out = True
        sem = "dD"
        waits = {}
        r = pl._get("wbuf")
        if r["w"] is not None:
            sx, vx = r["w"]
            waits[sx] = vx
        pl.count[sem] += 16
        r["r"].append((sem, pl.count[sem]))
        pl.steps["scalar"].append((fn, sorted(waits.items())))

    def ln_to_zg(src_name):
        for c in range(8):
            pp = c % 2
            def fsq(eng, c=c, pp=pp):
                return eng.activation(sb["tA"][:, pp * 512:pp * 512 + 512],
                                      sb[src_name][:, c * TOK:c * TOK + 512], AF.Square)
            pl.emit("scalar", fsq, reads=[src_name], writes=["tA%d" % pp])
            def fm1(eng, c=c):
                return eng.matmul(sb["pb0"][0:1, 0:512], sb["ones"].ap(),
                                  sb[src_name][:, c * TOK:c * TOK + 512],
                                  start=(c == 0), stop=(c == 7))
            pl.emit("tensor", fm1, reads=[src_name, "ones"],
                    writes=["pb0"] if c in (0, 7) else [])
            def fm2(eng, c=c, pp=pp):
                return eng.matmul(sb["pb1"][0:1, 0:512], sb["ones"].ap(),
                                  sb["tA"][:, pp * 512:pp * 512 + 512],
                                  start=(c == 0), stop=(c == 7))
            pl.emit("tensor", fm2, reads=["tA%d" % pp, "ones"],
                    writes=["pb1"] if c in (0, 7) else [])
        def fv(eng):
            st = sb["stat"]
            eng.tensor_scalar(st[0:1, 0:512], sb["pb0"][0:1, 0:512], 1.0 / D, None, Mul)
            eng.tensor_tensor(sb["strow"][0:1, 0:512], st[0:1, 0:512], st[0:1, 0:512], Mul)
            eng.scalar_tensor_tensor(st[0:1, 512:1024], sb["pb1"][0:1, 0:512],
                                     1.0 / D, sb["strow"][0:1, 0:512], Mul, Sub)
            eng.tensor_scalar(st[0:1, 512:1024], st[0:1, 512:1024], EPS, None, Add)
            return eng.reciprocal(st[0:1, 512:1024], st[0:1, 512:1024])
        pl.emit("vector", fv, reads=["pb0", "pb1"], writes=["stat"])
        def fs(eng):
            return eng.activation(sb["stat"][0:1, 1024:1536], sb["stat"][0:1, 512:1024],
                                  AF.Sqrt)
        pl.emit("scalar", fs, reads=["stat"], writes=["stat"])
        def fmr(eng):
            return eng.tensor_tensor(sb["stat2"][0:1, 0:512], sb["stat"][0:1, 0:512],
                                     sb["stat"][0:1, 1024:1536], Mul)
        pl.emit("vector", fmr, reads=["stat"], writes=["stat", "stat2"])
        bcast_row("pb2", sb["stat"][0:1, 1024:1536], 512, 0, ["stat"])
        bcast_row("pb3", sb["stat2"][0:1, 0:512], 512, 0, ["stat"])
        for c in range(8):
            def fap(eng, c=c):
                eng.tensor_tensor(sb["tB"][:, 0:512], sb[src_name][:, c * TOK:c * TOK + 512],
                                  sb["pb2"][:, 0:512], Mul)
                return eng.scalar_tensor_tensor(sb["zg"][:, c * TOK:c * TOK + 512],
                                                sb["pb3"][:, 0:512], -1.0,
                                                sb["tB"][:, 0:512], Mul, Add)
            pl.emit("vector", fap, reads=[src_name, "pb2", "pb3"],
                    writes=["tB0", "zgc%d" % c])
        return ["zgc%d" % c for c in range(8)]

    def project(wt_name, fout, evac, bof=4):
        if fout == 512:
            for c in range(8):
                def fw(eng, c=c, wt_name=wt_name):
                    return eng.dma_start(out=sb["wbuf"][:, c * 512:(c + 1) * 512],
                                         in_=dr[wt_name][c * 128:(c + 1) * 128, :])
                pl.emit("sync", fw, writes=["wbuf", "wbufc%d" % c])
            for fi in range(4):
                bank = "pb%d" % (bof + fi % 2)
                for c in range(8):
                    def fm(eng, c=c, fi=fi, bank=bank):
                        return eng.matmul(
                            sb[bank][:, 0:512],
                            sb["wbuf"][:, c * 512 + fi * 128:c * 512 + fi * 128 + 128],
                            sb["zg"][:, c * TOK:c * TOK + 512],
                            start=(c == 0), stop=(c == 7))
                    pl.emit("tensor", fm, reads=["wbufc%d" % c, "zgc%d" % c],
                            writes=[bank] if c in (0, 7) else [])
                evac(fi, bank)
        else:
            assert fout == 1024
            for half in range(2):
                for cc in range(4):
                    c = half * 4 + cc
                    def fw(eng, c=c, cc=cc, wt_name=wt_name):
                        return eng.dma_start(out=sb["wbuf"][:, cc * 1024:(cc + 1) * 1024],
                                             in_=dr[wt_name][c * 128:(c + 1) * 128, :])
                    pl.emit("sync", fw, writes=["wbuf", "wbufc%d" % (cc * 2),
                                                "wbufc%d" % (cc * 2 + 1)])
                for fi in range(8):
                    bank = "pb%d" % fi
                    for cc in range(4):
                        c = half * 4 + cc
                        def fm(eng, c=c, cc=cc, fi=fi, bank=bank, half=half):
                            return eng.matmul(
                                sb[bank][:, 0:512],
                                sb["wbuf"][:, cc * 1024 + fi * 128:cc * 1024 + fi * 128 + 128],
                                sb["zg"][:, c * TOK:c * TOK + 512],
                                start=(half == 0 and cc == 0), stop=(half == 1 and cc == 3))
                        pl.emit("tensor", fm,
                                reads=["wbufc%d" % (2 * cc + (1 if fi >= 4 else 0)),
                                       "zgc%d" % c],
                                writes=[bank] if (half == 0 and cc == 0) or (half == 1 and cc == 3) else [])
            for fi in range(8):
                evac(fi, "pb%d" % fi)

    def dump(dname, src_name, cols, extra=()):
        if not dbg:
            return
        def fn(eng, dname=dname, src_name=src_name, cols=cols):
            return eng.dma_start(out=dbg_aps[dname], in_=sb[src_name][:, 0:cols])
        fn._dma_out = True
        sem = "dD"
        waits = {}
        pl.count[sem] += 16
        for nm in (src_name,) + tuple(extra):
            r = pl._get(nm)
            if r["w"] is not None:
                sx, vx = r["w"]
                waits[sx] = max(waits.get(sx, 0), vx)
            r["r"].append((sem, pl.count[sem]))
        pl.steps["scalar"].append((fn, sorted(waits.items())))

    if dbg:
        dump("d_m1", "m127", 1)
        dump("d_m2", "m227", 1)

    def group_body(g):
        t0 = g * TOK
        for c in range(8):
            def fx(eng, c=c, t0=t0):
                return eng.dma_start(out=sb["xg"][:, c * TOK:(c + 1) * TOK],
                                     in_=dr["xT"][c * 128:(c + 1) * 128, t0:t0 + TOK])
            pl.emit("sync", fx, writes=["xg"])

        ln_to_zg("xg")

        w = TOK + 4

        def halo_in(hal, nch):
            def fn(eng, hal=hal, nch=nch):
                a = sb["pre"].ap().rearrange("p (c w) -> p c w", w=w)
                hh = sb[hal].ap().rearrange("p (c f) -> p c f", f=4)
                return eng.tensor_copy(a[:, 0:nch, 0:4], hh[:, 0:nch, :])
            pl.emit("vector", fn, reads=[hal], writes=["pre"])

        def halo_out(hal, nch):
            def fsp(eng):
                return eng.tensor_copy(sb["shufM"][:, 0:512], sb["zg"][:, 0:512])
            pl.emit("vector", fsp, reads=["zgc0"], writes=["shufM"])
            def fn(eng, hal=hal, nch=nch):
                a = sb["pre"].ap().rearrange("p (c w) -> p c w", w=w)
                hh = sb[hal].ap().rearrange("p (c f) -> p c f", f=4)
                return eng.tensor_copy(hh[:, 0:nch, :], a[:, 0:nch, TOK:TOK + 4])
            pl.emit("vector", fn, reads=["pre"], writes=[hal])

        def ev_pre(fi, bank):
            def fe(eng, fi=fi, bank=bank):
                return eng.tensor_copy(sb["pre"][:, fi * w + 4:fi * w + 4 + 512],
                                       sb[bank][:, 0:512])
            pl.emit("vector", fe, reads=[bank], writes=["pre"])

        def conv(name_t, nch, wc, bc, scale=None):
            for c in range(nch):
                pp = c % 2
                def f0(eng, c=c, pp=pp, wc=wc):
                    return eng.mul(sb["tA"][:, pp * 512:pp * 512 + TOK],
                                   sb["pre"][:, c * w + 1:c * w + 1 + TOK],
                                   sb[wc][:, c * 4:c * 4 + 1])
                pl.emit("scalar", f0, reads=["pre", wc], writes=["tA%d" % pp])
                for j in (1, 2, 3):
                    def fj(eng, c=c, j=j, pp=pp, wc=wc):
                        return eng.scalar_tensor_tensor(
                            sb["tA"][:, pp * 512:pp * 512 + TOK],
                            sb["pre"][:, c * w + 1 + j:c * w + 1 + j + TOK],
                            sb[wc][:, c * 4 + j:c * 4 + j + 1],
                            sb["tA"][:, pp * 512:pp * 512 + TOK], Mul, Add)
                    pl.emit("vector", fj, reads=["pre", wc, "tA%d" % pp],
                            writes=["tA%d" % pp])
                def fs(eng, c=c, pp=pp, name_t=name_t, bc=bc):
                    return eng.activation(sb[name_t][:, c * TOK:(c + 1) * TOK],
                                          sb["tA"][:, pp * 512:pp * 512 + TOK],
                                          AF.Silu, bias=sb[bc][:, c:c + 1])
                pl.emit("scalar", fs, reads=["tA%d" % pp, bc], writes=[name_t])
                if scale is not None:
                    def fsc(eng, c=c, name_t=name_t, scale=scale):
                        return eng.tensor_scalar(sb[name_t][:, c * TOK:(c + 1) * TOK],
                                                 sb[name_t][:, c * TOK:(c + 1) * TOK],
                                                 scale, None, Mul)
                    pl.emit("vector", fsc, reads=[name_t], writes=[name_t])

        halo_in("qhal", 4)
        project("wqT", 512, ev_pre, bof=4)
        halo_out("qhal", 4)
        conv("qt", 4, "s_qcw_c", "s_qcb_c")

        dump("d_zg", "zg", 8 * TOK, extra=tuple("zgc%d" % c for c in range(8)))
        halo_in("khal", 4)
        project("wkT", 512, ev_pre, bof=6)
        halo_out("khal", 4)
        conv("kt", 4, "s_kcw_c", "s_kcb_c", scale=DK ** (-0.5))

        halo_in("vhal", 8)
        project("wvT", 1024, ev_pre)
        halo_out("vhal", 8)
        conv("vt", 8, "s_vcw_c", "s_vcb_c")

        dump("d_qt", "qt", 4 * TOK)
        dump("d_kt", "kt", 4 * TOK, extra=tuple("xqs%d" % q for q in range(8)))
        dump("d_vt", "vt", 8 * TOK)

        def ev_a(fi, bank):
            def fe(eng, fi=fi, bank=bank):
                return eng.activation(sb["at"][:, fi * TOK:fi * TOK + 512],
                                      sb[bank][:, 0:512], AF.Sigmoid,
                                      bias=sb["s_ba_c"][:, fi:fi + 1])
            pl.emit("scalar", fe, reads=[bank, "s_ba_c"], writes=["at"])
        project("waT", 1024, ev_a)

        def ev_g(fi, bank):
            def fe(eng, fi=fi, bank=bank):
                return eng.activation(sb["gt"][:, fi * TOK:fi * TOK + 512],
                                      sb[bank][:, 0:512], AF.Sigmoid)
            pl.emit("scalar", fe, reads=[bank], writes=["gt"])
        project("wgT", 1024, ev_g)

        def fwb(eng):
            src = bass.AP(dr["wbT"].tensor, 0, [[8, 128], [1024, 8], [1, 8]])
            return eng.dma_start(out=sb["wbuf"][:, 0:64], in_=src)
        pl.emit("sync", fwb, writes=["wbuf", "wbufc0"])
        for c in range(8):
            def fm(eng, c=c):
                return eng.matmul(sb["pb6"][0:8, 0:512], sb["wbuf"][:, c * 8:c * 8 + 8],
                                  sb["zg"][:, c * TOK:c * TOK + 512],
                                  start=(c == 0), stop=(c == 7))
            pl.emit("tensor", fm, reads=["wbufc0", "zgc%d" % c],
                    writes=["pb6"] if c in (0, 7) else [])
        def fb2(eng):
            return eng.activation(sb["bb8"][0:8, 0:TOK], sb["pb6"][0:8, 0:512],
                                  AF.Sigmoid, bias=sb["s_bb_c"][0:8, 0:1])
        pl.emit("scalar", fb2, reads=["pb6", "s_bb_c"], writes=["bb8"])
        for h in range(H):
            bank = "pb%d" % (6 + h % 2)
            def fbb(eng, h=h, bank=bank):
                return eng.matmul(sb[bank][:, 0:512],
                                  sb["sel8"][0:8, h * 128:(h + 1) * 128],
                                  sb["bb8"][0:8, 0:TOK], start=True, stop=True)
            pl.emit("tensor", fbb, reads=["sel8", "bb8"], writes=[bank])
            def fr1(eng, h=h, bank=bank):
                eng.reciprocal(sb["tA"][:, 0:512], sb["at"][:, h * TOK:h * TOK + 512])
                eng.tensor_tensor(sb["tA"][:, 512:1024], sb["tA"][:, 0:512],
                                  sb[bank][:, 0:512], Mul)
                return eng.tensor_scalar(sb["rng"][:, h * TOK:h * TOK + 512],
                                         sb["tA"][:, 512:1024], -1.0, None, Mul)
            pl.emit("vector", fr1, reads=[bank, "at"], writes=["tA0", "tA1", "rng"])

        dump("d_at", "at", 8 * TOK)
        dump("d_rng", "rng", 8 * TOK)
        if KPH <= 1:
            return
        def kslice(name, h, c0, n=64):
            return sb[name][64 * (h % 2):64 * (h % 2) + 64,
                            (h // 2) * TOK + c0:(h // 2) * TOK + c0 + n]

        for ch in range(min(NCH, NCHLIM)):
            c0 = ch * 64
            for p in range(4):
                bk = "pb%d" % p
                def ftr(eng, p=p, bk=bk):
                    return eng.transpose(sb[bk][:, 0:128],
                                         sb["sv"][:, p * 128:(p + 1) * 128],
                                         sb["s_ident"].ap())
                pl.emit("tensor", ftr, reads=["sv", "s_ident"], writes=[bk])
                def fev(eng, p=p, bk=bk):
                    return eng.tensor_copy(sb["sk"][:, p * 128:(p + 1) * 128],
                                           sb[bk][:, 0:128])
                pl.emit("vector", fev, reads=[bk], writes=["sk"])

            for h in range(H):
                kap = kslice("kt", h, c0)
                qap = kslice("qt", h, c0)
                bank = "pb%d" % (4 + h % 2)
                def fmm(eng, kap=kap, bank=bank):
                    return eng.matmul(sb[bank][0:64, 0:64], kap, kap, start=True, stop=True)
                pl.emit("tensor", fmm, reads=["kt"], writes=[bank])
                def fme(eng, h=h, bank=bank):
                    dst = bass.AP(sb["mup"], h, [[512, 64], [64, 8], [8, 8]])
                    src = bass.AP(sb[bank], 0, [[512, 64], [8, 8], [1, 8]])
                    msk = bass.AP(sb["s_maskU"], h * 64, [[512, 64], [8, 8], [1, 8]])
                    return eng.tensor_tensor(dst, src, msk, Mul)
                pl.emit("vector", fme, reads=[bank, "s_maskU"], writes=["mup"])
                def fmeh(eng, h=h, bank=bank):
                    hh = h % 2
                    dst = sb["mupH"][64 * hh:64 * hh + 64,
                                     (h // 2) * 128 + 64 * hh:(h // 2) * 128 + 64 * hh + 64]
                    return eng.tensor_tensor(dst, sb[bank][0:64, 0:64],
                                             sb["s_maskU"][:, h * 64:(h + 1) * 64], Mul)
                pl.emit("vector", fmeh, reads=[bank, "s_maskU"], writes=["mupH"])
                bank2 = "pb%d" % (6 + h % 2)
                def fmq(eng, kap=kap, qap=qap, bank2=bank2):
                    return eng.matmul(sb[bank2][0:64, 0:64], kap, qap, start=True, stop=True)
                pl.emit("tensor", fmq, reads=["kt", "qt"], writes=[bank2])
                def fqe(eng, h=h, bank2=bank2):
                    eng.tensor_tensor(sb["mqn"][0:64, h * 64:(h + 1) * 64],
                                      sb[bank2][0:64, 0:64],
                                      sb["s_maskQn"][:, h * 64:(h + 1) * 64], Mul)
                    return eng.tensor_copy(sb["mqn"][64:128, h * 64:(h + 1) * 64],
                                           sb["mqn"][0:64, h * 64:(h + 1) * 64])
                pl.emit("vector", fqe, reads=[bank2, "s_maskQn"], writes=["mqn"])

            for p in range(4):
                def fks(eng, p=p, c0=c0):
                    eng.tensor_copy(sb["kstage"][0:64, 0:64],
                                    sb["kt"][0:64, p * TOK + c0:p * TOK + c0 + 64])
                    return eng.tensor_copy(sb["kstage"][64:128, 64:128],
                                           sb["kt"][64:128, p * TOK + c0:p * TOK + c0 + 64])
                pl.emit("vector", fks, reads=["kt"], writes=["kstage"])
                bk = "pb%d" % (4 + p % 2)
                def ftr2(eng, bk=bk):
                    return eng.transpose(sb[bk][:, 0:128], sb["kstage"].ap(),
                                         sb["s_ident"].ap())
                pl.emit("tensor", ftr2, reads=["kstage", "s_ident"], writes=[bk])
                def fke(eng, p=p, bk=bk):
                    return eng.tensor_copy(sb[f"kthBD{p}"].ap(), sb[bk][:, 0:128])
                pl.emit("vector", fke, reads=[bk], writes=[f"kthBD{p}"])

            for h in range(H):
                kap = kslice("kt", h, c0)
                skap = sb["sk"][64 * (h % 2):64 * (h % 2) + 64,
                                (h // 2) * 128:(h // 2) * 128 + 128]
                bank = "pb%d" % (h % 4)
                def fp(eng, skap=skap, kap=kap, bank=bank):
                    return eng.matmul(sb[bank][:, 0:64], skap, kap, start=True, stop=True)
                pl.emit("tensor", fp, reads=["sk", "kt"], writes=[bank])
                def fpe(eng, h=h, bank=bank):
                    return eng.tensor_copy(sb["pva"][:, h * 64:(h + 1) * 64],
                                           sb[bank][:, 0:64])
                pl.emit("vector", fpe, reads=[bank], writes=["pva"])

            def hs3(name, cc0):
                return sb[name].ap().rearrange("p (h t) -> p h t", t=TOK)[:, :, cc0:cc0 + 64]

            def fcp(eng, c0=c0):
                cp = sb["cpr"].ap().rearrange("p (h t) -> p h t", t=64)
                av = hs3("at", c0)
                last = None
                for h in range(H):
                    last = eng.tensor_tensor_scan(cp[:, h, :], av[:, h, :],
                                                  sb["onesrow"][:, 0:64], 1.0, Mul, Mul)
                return last
            pl.emit("vector", fcp, reads=["at", "onesrow"], writes=["cpr"])
            def fgn(eng, c0=c0):
                g3 = sb["gng"].ap().rearrange("p (h t) -> p h t", t=64)
                t3 = sb["tA"].ap().rearrange("p (h t) -> p h t", t=128)[:, :, 0:64]
                p3 = sb["pva"].ap().rearrange("p (h t) -> p h t", t=64)
                eng.reciprocal(sb["tB"][:, 0:512], sb["cpr"][:, 0:512])
                cr = sb["tB"][:, 0:512].rearrange("p (h t) -> p h t", t=64)
                eng.tensor_tensor(t3, hs3("at", c0), cr, Mul)
                eng.tensor_tensor(t3, t3, hs3("vt", c0), Mul)
                eng.tensor_tensor(t3, p3, t3, Sub)
                return eng.tensor_tensor(g3, hs3("rng", c0), t3, Mul)
            pl.emit("vector", fgn, reads=["cpr", "at", "vt", "rng", "pva"],
                    writes=["gng", "tA0", "tA1", "tB0", "tB1"])

            def fwz(eng):
                return eng.memset(sb["wv"].ap(), 0.0)
            pl.emit("vector", fwz, writes=["wv"])
            wv3 = sb["wv"].ap().rearrange("p (h t) -> p h t", t=64)
            rng3 = hs3("rng", c0)
            gng3 = sb["gng"].ap().rearrange("p (h t) -> p h t", t=64)
            mupr = sb["mup"].ap().rearrange("s (h t) -> s h t", t=64)
            def emit_shf(j):
                jt = j * 8
                r8 = jt - (jt // 32) * 32
                def fshf(eng, r8=r8):
                    mask = [(r8 + i) % 32 for i in range(32)]
                    return eng.stream_shuffle(sb["shufM"][:, 0:512],
                                              sb["mup"][:, 0:512], mask)
                pl.emit("vector", fshf, reads=["mup"], writes=["shufM"])

            def emit_mb(j):
                jt = j * 8
                q32 = (jt // 32) * 32
                for sp in range(7):
                    ncol = 56 - 8 * sp
                    def fmb(eng, sp=sp, ncol=ncol, jt=jt, q32=q32):
                        return eng.matmul(sb["pb6"][:, sp * 64:sp * 64 + ncol],
                                          sb["sel8"][q32:q32 + 8, sp * 128:sp * 128 + 128],
                                          sb["shufM"][q32:q32 + 8,
                                                      (jt // 8) * 64 + 8 * (sp + 1):
                                                      (jt // 8) * 64 + 64],
                                          start=True, stop=True)
                    pl.emit("tensor", fmb, reads=["sel8", "shufM"], writes=["pb6"])
                def ffold(eng, jt=jt, c0=c0):
                    last = None
                    for sp in range(7):
                        n_t = 7 - sp
                        mbs = bass.AP(sb["pb6"], sp * 64, [[512, 128], [8, n_t], [1, 8]])
                        rsl = bass.AP(sb["rng"], c0 + jt + sp + 1,
                                      [[8 * TOK, 128], [1, n_t], [TOK, 8]])
                        dst = bass.AP(sb["pva"], sp * 64, [[512, 128], [8, n_t], [1, 8]])
                        last = eng.tensor_tensor(dst, mbs, rsl, Mul)
                    return last
                pl.emit("vector", ffold, reads=["pb6", "rng"], writes=["mball"])

            def emit_init(j):
                jt = j * 8
                if j == 0:
                    def fi0(eng, jt=jt):
                        return eng.tensor_scalar(wv3[:, :, jt:jt + 8], gng3[:, :, jt:jt + 8],
                                                 -1.0, None, Mul)
                    pl.emit("vector", fi0, reads=["gng"], writes=["wv"])
                    def fi0s(eng):
                        return eng.tensor_copy(sb["shufM"][:, 0:512], sb["mup"][:, 0:512])
                    pl.emit("vector", fi0s, reads=["mup"], writes=["shufM"])
                    return
                def fi1a(eng, jt=jt, c0=c0):
                    last = None
                    for h in range(8):
                        p = h // 2
                        hh = h % 2
                        accs2 = sb["pb%d" % (2 + p)][:, 64 * hh + jt:64 * hh + jt + 8]
                        rngs2 = sb["rng"][:, h * TOK + c0 + jt:h * TOK + c0 + jt + 8]
                        t2 = sb["tA"][:, h * 8:h * 8 + 8]
                        eng.tensor_tensor(t2, rngs2, accs2, Mul)
                    for h in range(8):
                        gngs2 = sb["gng"][:, h * 64 + jt:h * 64 + jt + 8]
                        wvs2 = sb["wv"][:, h * 64 + jt:h * 64 + jt + 8]
                        t2 = sb["tA"][:, h * 8:h * 8 + 8]
                        last = eng.tensor_tensor(wvs2, t2, gngs2, Sub)
                    return last
                pl.emit("vector", fi1a,
                        reads=["rng", "pb2", "pb3", "pb4", "pb5", "gng"],
                        writes=["wv", "tA0"])
                if dbg and ch == 0 and j == 1:
                    dump("d_wvj1", "wv", 512)

            def emit_fup(j):
                jt = j * 8
                for sp in range(7):
                    n_t = 7 - sp
                    def fupa(eng, jt=jt, sp=sp, n_t=n_t):
                        last = None
                        for h in range(8):
                            mrow_h = bass.AP(sb["pva"], sp * 64 + h,
                                             [[512, 128], [8, n_t]])
                            wcol = sb["wv"][:, 64 * h + jt + sp:64 * h + jt + sp + 1]
                            tgt_h = sb["wv"][:, 64 * h + jt + sp + 1:
                                             64 * h + jt + sp + 1 + n_t]
                            last = eng.scalar_tensor_tensor(
                                tgt_h, mrow_h, wcol, tgt_h, Mul, Add)
                        return last
                    pl.emit("vector", fupa, reads=["pva", "wv", "mball"],
                            writes=["wv"])

            def emit_boundary(j):
                WTB = ["wtbP", "wtbP1", "wtbP2", "wtbP3"]
                def dump_psum(dname, bank):
                    def fcpy(eng, bank=bank):
                        return eng.tensor_copy(sb["deqb"][:, 0:128],
                                               sb[bank][:, 0:128])
                    pl.emit("vector", fcpy, reads=[bank], writes=["deqb"])
                    dump(dname, "deqb", 128)
                for p in range(4):
                    bank7 = ["pb0", "pb1", "pb6", "pb7"][p]
                    def ftbh(eng, p=p, bank7=bank7):
                        return eng.transpose(sb[bank7][:, 0:128],
                                             sb["wv"][:, p * 128:(p + 1) * 128],
                                             sb["s_ident"].ap())
                    pl.emit("tensor", ftbh, reads=["wv", "s_ident"], writes=[bank7])
                if dbg and ch == 0 and j == 0:
                    dump_psum("d_pb7t", "pb7")
                    dump("d_wvj0", "wv", 512)
                for p in range(4):
                    bank7 = ["pb0", "pb1", "pb6", "pb7"][p]
                    def fteh(eng, bank7=bank7, p=p):
                        return eng.tensor_copy(sb[WTB[p]].ap(), sb[bank7][:, 0:128])
                    pl.emit("vector", fteh, reads=[bank7], writes=[WTB[p]])
                if dbg and ch == 0 and j == 0:
                    dump("d_wtb0", "wtbP", 128)
                    dump("d_wtb3", "wtbP3", 128)
                for p in range(4):
                    accb = "pb%d" % (2 + p)
                    def fac(eng, p=p, accb=accb):
                        return eng.matmul(sb[accb][:, 0:128], sb[WTB[p]].ap(),
                                          sb["mupH"][:, p * 128:(p + 1) * 128],
                                          start=True, stop=True)
                    pl.emit("tensor", fac, reads=[WTB[p], "mupH"], writes=[accb])
                if dbg and ch == 0 and j == 0:
                    dump_psum("d_acc2", "pb2")
                    dump_psum("d_acc5", "pb5")
                if dbg and ch == 0 and j == 1:
                    dump("d_wvj1b", "wv", 512)

            if not SKMB:
                emit_shf(0)
                emit_mb(0)
                for j in range(8):
                    emit_init(j)
                    emit_fup(j)
                    if j < 7:
                        emit_shf(j + 1)
                        emit_boundary(j)
                        emit_mb(j + 1)

            if SKOST:
                continue
            for p in range(4):
                bk = ["pb2", "pb3", "pb6", "pb7"][p]
                def ftw(eng, p=p, bk=bk):
                    return eng.transpose(sb[bk][:, 0:128],
                                         sb["wv"][:, p * 128:(p + 1) * 128],
                                         sb["s_ident"].ap())
                pl.emit("tensor", ftw, reads=["wv", "s_ident"], writes=[bk])
                def fwe(eng, p=p, bk=bk):
                    return eng.tensor_copy(sb[f"wthP{p}"].ap(), sb[bk][:, 0:128])
                pl.emit("vector", fwe, reads=[bk], writes=[f"wthP{p}"])

            sbanks = ["pb6", "pb7", "pb0", "pb1"]

            def emit_o(h):
                qap = kslice("qt", h, c0)
                skap = sb["sk"][64 * (h % 2):64 * (h % 2) + 64,
                                (h // 2) * 128:(h // 2) * 128 + 128]
                bank = "pb%d" % (2 + h % 4)
                def fo1(eng, skap=skap, qap=qap, bank=bank):
                    return eng.matmul(sb[bank][:, 0:64], skap, qap, start=True, stop=False)
                pl.emit("tensor", fo1, reads=["sk", "qt"], writes=[bank])
                def fo2(eng, h=h, bank=bank):
                    wth = sb["wthP%d" % (h // 2)][64 * (h % 2):64 * (h % 2) + 64, :]
                    b0 = 64 * (h % 2)
                    return eng.matmul(sb[bank][:, 0:64], wth,
                                      sb["mqn"][b0:b0 + 64, h * 64:(h + 1) * 64],
                                      start=False, stop=True)
                pl.emit("tensor", fo2, reads=["wthP%d" % (h // 2), "mqn"], writes=[bank])
                def foe(eng, h=h, bank=bank, c0=c0):
                    cp3 = sb["cpr"].ap().rearrange("p (h t) -> p h t", t=64)
                    av = sb["at"].ap().rearrange("p (h t) -> p h t", t=TOK)
                    return eng.tensor_tensor(av[:, h, c0:c0 + 64], sb[bank][:, 0:64],
                                             cp3[:, h, :], Mul)
                pl.emit("vector", foe, reads=[bank, "cpr"], writes=["at"])
                bank2 = sbanks[h % 4]
                def fs1(eng, h=h, bank2=bank2):
                    wth = sb["wthP%d" % (h // 2)][64 * (h % 2):64 * (h % 2) + 64, :]
                    kbd = sb["kthBD%d" % (h // 2)][64 * (h % 2):64 * (h % 2) + 64,
                                                   64 * (h % 2):64 * (h % 2) + 64]
                    return eng.matmul(sb[bank2][:, 0:64], wth, kbd, start=True, stop=True)
                pl.emit("tensor", fs1, reads=["wthP%d" % (h // 2), "kthBD%d" % (h // 2)],
                        writes=[bank2])

            def emit_phA(h):
                bank2 = sbanks[h % 4]
                def fsa(eng, h=h, bank2=bank2):
                    return eng.tensor_scalar(sb["tA"][:, h * 64:h * 64 + 64],
                                             sb[bank2][:, 0:64],
                                             sb["cpr"][:, h * 64 + 63:h * 64 + 64],
                                             None, Mul)
                pl.emit("vector", fsa, reads=[bank2, "cpr"], writes=["tA0"])

            def emit_phB(h):
                def fsb(eng, h=h):
                    return eng.scalar_tensor_tensor(
                        sb["sv"][:, h * 64:h * 64 + 64],
                        sb["sv"][:, h * 64:h * 64 + 64],
                        sb["cpr"][:, h * 64 + 63:h * 64 + 64],
                        sb["tA"][:, h * 64:h * 64 + 64], Mul, Sub)
                pl.emit("vector", fsb, reads=["sv", "cpr", "tA0"], writes=["sv"])

            for h in range(4):
                emit_o(h)
            for h in range(4):
                emit_phA(h)
            for h in range(4, 8):
                emit_o(h)
            for h in range(4, 8):
                emit_phA(h)
            for h in range(8):
                emit_phB(h)

        dump("d_o", "at", 8 * TOK)
        dump("d_sv", "sv", 512)
        dump("d_cpr", "cpr", 512)
        dump("d_gng", "gng", 512)
        dump("d_wv", "wv", 512)
        dump("d_pva", "pva", 512)
        dump("d_mqn", "mqn", 512)
        dump("d_mupH", "mupH", 512)
        if KPH <= 2:
            return
        for h in range(H):
            pp = h % 2
            def fsq2(eng, h=h, pp=pp):
                return eng.activation(sb["tA"][:, pp * 512:pp * 512 + 512],
                                      sb["at"][:, h * TOK:h * TOK + 512], AF.Square)
            pl.emit("scalar", fsq2, reads=["at"], writes=["tA%d" % pp])
            def fgm1(eng, h=h):
                return eng.matmul(sb["pb0"][0:1, 0:512], sb["ones"].ap(),
                                  sb["at"][:, h * TOK:h * TOK + 512], start=True, stop=True)
            pl.emit("tensor", fgm1, reads=["at", "ones"], writes=["pb0"])
            def fgm2(eng, pp=pp):
                return eng.matmul(sb["pb1"][0:1, 0:512], sb["ones"].ap(),
                                  sb["tA"][:, pp * 512:pp * 512 + 512], start=True, stop=True)
            pl.emit("tensor", fgm2, reads=["tA%d" % pp, "ones"], writes=["pb1"])
            def fgr(eng):
                st = sb["stat"]
                eng.tensor_scalar(st[0:1, 0:512], sb["pb0"][0:1, 0:512], 1.0 / DV, None, Mul)
                eng.tensor_tensor(sb["strow"][0:1, 0:512], st[0:1, 0:512], st[0:1, 0:512], Mul)
                eng.scalar_tensor_tensor(st[0:1, 512:1024], sb["pb1"][0:1, 0:512],
                                         1.0 / DV, sb["strow"][0:1, 0:512], Mul, Sub)
                eng.tensor_scalar(st[0:1, 512:1024], st[0:1, 512:1024], EPS, None, Add)
                return eng.reciprocal(st[0:1, 512:1024], st[0:1, 512:1024])
            pl.emit("vector", fgr, reads=["pb0", "pb1"], writes=["stat"])
            def fgs(eng):
                return eng.activation(sb["stat"][0:1, 1024:1536], sb["stat"][0:1, 512:1024],
                                      AF.Sqrt)
            pl.emit("scalar", fgs, reads=["stat"], writes=["stat"])
            bcast_row("pb2", sb["stat"][0:1, 0:512], 512, 0, ["stat"])
            bcast_row("pb3", sb["stat"][0:1, 1024:1536], 512, 0, ["stat"])
            def fga(eng, h=h):
                eng.tensor_tensor(sb["tB"][:, 0:512], sb["at"][:, h * TOK:h * TOK + 512],
                                  sb["pb2"][:, 0:512], Sub)
                eng.tensor_tensor(sb["tB"][:, 0:512], sb["tB"][:, 0:512],
                                  sb["pb3"][:, 0:512], Mul)
                eng.tensor_scalar(sb["tB"][:, 0:512], sb["tB"][:, 0:512],
                                  sb["s_lnw_c"].ap(), sb["s_lnb_c"].ap(), Mul, Add)
                return eng.tensor_tensor(sb["gt"][:, h * TOK:h * TOK + 512],
                                         sb["tB"][:, 0:512],
                                         sb["gt"][:, h * TOK:h * TOK + 512], Mul)
            pl.emit("vector", fga, reads=["at", "pb2", "pb3", "s_lnw_c", "s_lnb_c", "gt"],
                    writes=["tB0", "gt"])

        dump("d_gt", "gt", 8 * TOK)
        for half in range(2):
            for cc in range(4):
                c = half * 4 + cc
                def fw(eng, c=c, cc=cc):
                    return eng.dma_start(out=sb["wbuf"][:, cc * 1024:(cc + 1) * 1024],
                                         in_=dr["woT"][c * 128:(c + 1) * 128, :])
                pl.emit("sync", fw, writes=["wbuf", "wbufc%d" % (cc * 2),
                                            "wbufc%d" % (cc * 2 + 1)])
            for fi in range(8):
                bank = "pb%d" % fi
                for cc in range(4):
                    c = half * 4 + cc
                    def fm(eng, c=c, cc=cc, fi=fi, bank=bank, half=half):
                        return eng.matmul(
                            sb[bank][:, 0:512],
                            sb["wbuf"][:, cc * 1024 + fi * 128:cc * 1024 + fi * 128 + 128],
                            sb["gt"][:, c * TOK:c * TOK + 512],
                            start=(half == 0 and cc == 0), stop=(half == 1 and cc == 3))
                    pl.emit("tensor", fm,
                            reads=["wbufc%d" % (2 * cc + (1 if fi >= 4 else 0)), "gt"],
                            writes=[bank] if (half == 0 and cc == 0)
                            or (half == 1 and cc == 3) else [])
        for fi in range(8):
            bank = "pb%d" % fi
            def fres(eng, fi=fi, bank=bank):
                return eng.tensor_tensor(sb["xg"][:, fi * TOK:fi * TOK + 512],
                                         sb["xg"][:, fi * TOK:fi * TOK + 512],
                                         sb[bank][:, 0:512], Add)
            pl.emit("vector", fres, reads=[bank, "xg"], writes=["xg"])

        ln_to_zg("xg")

        def xqv():
            return sb["kt"].ap().bitcast(BF16)

        def ffqv():
            return sb["qt"].ap().bitcast(BF16)

        def part_max(eng, t, u, n, dst):
            eng.tensor_copy(u[0:64, 0:n], t[64:128, 0:n])
            eng.tensor_tensor(t[0:64, 0:n], t[0:64, 0:n], u[0:64, 0:n], MaxOp)
            eng.tensor_copy(u[0:32, 0:n], t[32:64, 0:n])
            eng.tensor_tensor(t[0:32, 0:n], t[0:32, 0:n], u[0:32, 0:n], MaxOp)
            for k in (16, 8, 4, 2, 1):
                mask = [(i + k) % 32 for i in range(32)]
                eng.stream_shuffle(u[:, 0:n], t[:, 0:n], mask)
                eng.tensor_tensor(t[0:32, 0:n], t[0:32, 0:n], u[0:32, 0:n], MaxOp)
            return eng.tensor_copy(dst, t[0:1, 0:n])

        dump("d_z2", "zg", 8 * TOK, extra=tuple("zgc%d" % c for c in range(8)))
        if KPH <= 3:
            return
        for c in range(8):
            pp = c % 2
            def fsq3(eng, c=c, pp=pp):
                return eng.activation(sb["tB"][:, pp * 512:pp * 512 + 512],
                                      sb["zg"][:, c * TOK:c * TOK + 512], AF.Square)
            pl.emit("scalar", fsq3, reads=["zgc%d" % c], writes=["tB%d" % pp])
            def fm3(eng, c=c, pp=pp):
                return eng.matmul(sb["pb1"][0:1, 0:512], sb["ones"].ap(),
                                  sb["tB"][:, pp * 512:pp * 512 + 512],
                                  start=(c == 0), stop=(c == 7))
            pl.emit("tensor", fm3, reads=["tB%d" % pp, "ones"],
                    writes=["pb1"] if c in (0, 7) else [])
            if c == 0:
                def fmx(eng, pp=pp):
                    return eng.tensor_copy(sb["tA"][:, 0:512],
                                           sb["tB"][:, pp * 512:pp * 512 + 512])
            else:
                def fmx(eng, pp=pp):
                    return eng.tensor_tensor(sb["tA"][:, 0:512], sb["tA"][:, 0:512],
                                             sb["tB"][:, pp * 512:pp * 512 + 512], MaxOp)
            pl.emit("vector", fmx, reads=["tB%d" % pp, "tA0"], writes=["tA0"])
        def ft1(eng):
            part_max(eng, sb["tA"], sb["tB"], 512, sb["stat"][0:1, 0:512])
            return eng.activation(sb["stat"][0:1, 0:512], sb["stat"][0:1, 0:512],
                                  AF.Sqrt) if False else eng.tensor_copy(
                sb["stat"][0:1, 0:512], sb["stat"][0:1, 0:512])
        pl.emit("vector", ft1, reads=["tA0"],
                writes=["tA0", "tA1", "tB0", "tB1", "stat"])
        def ft1s(eng):
            return eng.activation(sb["stat"][0:1, 0:512], sb["stat"][0:1, 0:512], AF.Sqrt)
        pl.emit("scalar", ft1s, reads=["stat"], writes=["stat"])
        def fqv(eng):
            st = sb["stat"]
            eng.tensor_scalar(sb["strow"][0:1, 0:512], sb["pb1"][0:1, 0:512], 1.0 / D, EPS,
                              Mul, Add)
            return eng.reciprocal(st[0:1, 512:1024], sb["strow"][0:1, 0:512])
        pl.emit("vector", fqv, reads=["pb1"], writes=["stat"])
        def fqs(eng):
            return eng.activation(sb["stat"][0:1, 512:1024], sb["stat"][0:1, 512:1024],
                                  AF.Sqrt)
        pl.emit("scalar", fqs, reads=["stat"], writes=["stat"])
        def fq2(eng):
            st = sb["stat"]
            eng.tensor_tensor(st[0:1, 1024:1536], st[0:1, 512:1024], st[0:1, 0:512], Mul)
            eng.tensor_scalar(st[0:1, 1024:1536], st[0:1, 1024:1536], 1e-5, None, MaxOp)
            eng.reciprocal(sb["stat2"][0:1, 0:512], st[0:1, 1024:1536])
            eng.tensor_tensor(sb["stat2"][0:1, 0:512], sb["stat2"][0:1, 0:512], st[0:1, 512:1024], Mul)
            eng.tensor_scalar(sb["stat2"][0:1, 0:512], sb["stat2"][0:1, 0:512], 127.0, None, Mul)
            return eng.tensor_scalar(sb["strow"][0:1, 0:512], st[0:1, 1024:1536],
                                     sb["m127"][0:1, 0:1], None, Mul)
        pl.emit("vector", fq2, reads=["stat", "m127"], writes=["stat", "stat2", "strow"])
        bcast_row("pb2", sb["stat2"][0:1, 0:512], 512, 0, ["stat"])
        bcast_row("pb3", sb["strow"][0:1, 0:512], 512, 0, ["strow"])
        def fdq(eng):
            return eng.tensor_copy(sb["deqb"][:, 0:512], sb["pb3"][:, 0:512])
        pl.emit("vector", fdq, reads=["pb3"], writes=["deqb"])
        for c in range(8):
            pp = c % 2
            def fz1(eng, c=c, pp=pp):
                return eng.tensor_tensor(sb["tA"][:, pp * 512:pp * 512 + 512],
                                         sb["zg"][:, c * TOK:c * TOK + 512],
                                         sb["pb2"][:, 0:512], Mul)
            pl.emit("vector", fz1, reads=["zgc%d" % c, "pb2"], writes=["tA%d" % pp])
            def fz3(eng, pp=pp):
                eng.tensor_scalar(sb["tB"][:, 0:512],
                                  sb["tA"][:, pp * 512:pp * 512 + 512],
                                  12582912.0, None, Add)
                return eng.tensor_scalar(sb["tB"][:, 0:512], sb["tB"][:, 0:512],
                                         -12582912.0, None, Add)
            pl.emit("vector", fz3, reads=["tA%d" % pp], writes=["tB0"])
            def fz4(eng):
                return eng.tensor_copy(sb["tB"].ap().bitcast(I32)[:, 512:1024], sb["tB"][:, 0:512])
            pl.emit("vector", fz4, reads=["tB0"], writes=["tB1"])
            def fz5(eng, c=c):
                return eng.tensor_scalar(xqv()[:, c * TOK:c * TOK + 512],
                                         sb["tB"].ap().bitcast(I32)[:, 512:1024], -128.0, 127.0, MaxOp, MinOp)
            pl.emit("vector", fz5, reads=["tB1"],
                    writes=["kt"] + (["xqs%d" % q for q in range(8)] if c == 0 else []))
        dump("d_xq", "kt", 2048)
        dump("d_deq", "deqb", 512, extra=("deqb2",))

        for qtr in range(TOK // 128):
            ht0 = qtr * 128
            for fs in range(8):
                hw = fs % 2
                def fwl(eng, fs=fs, hw=hw):
                    src = bass.AP(w1q.tensor, fs * 512,
                                  [[4096, 128], [524288, 8], [1, 512]])
                    return eng.dma_start(
                        out=sb["wbuf"].ap().bitcast(BF16)[:, hw * 4096:hw * 4096 + 4096],
                        in_=src)
                pl.emit("sync", fwl,
                        writes=["wbuf"] + ["wbufc%d" % (hw * 4 + i) for i in range(4)],
                        extra_waits=[("dG", wq_done)])
                for fi in range(4):
                    bank = "pb%d" % (4 + fi % 2)
                    for c in range(8):
                        def fm(eng, c=c, fi=fi, bank=bank, ht0=ht0, hw=hw):
                            wb = sb["wbuf"].ap().bitcast(BF16)
                            return eng.matmul(
                                sb[bank][:, 0:128],
                                wb[:, hw * 4096 + c * 512 + fi * 128:
                                   hw * 4096 + c * 512 + fi * 128 + 128],
                                xqv()[:, c * TOK + ht0:c * TOK + ht0 + 128],
                                start=(c == 0), stop=(c == 7))
                        pl.emit("tensor", fm,
                                reads=["wbufc%d" % (hw * 4 + c // 2), "kt"],
                                writes=[bank] if c in (0, 7) else [])
                    fidx = fs * 4 + fi
                    def fev(eng, fidx=fidx, bank=bank, ht0=ht0):
                        return eng.tensor_tensor(sb["ffh"][:, fidx * 128:(fidx + 1) * 128],
                                                 sb[bank][:, 0:128],
                                                 sb["deqb"][:, ht0:ht0 + 128], Mul)
                    pl.emit("vector", fev, reads=[bank, "deqb"], writes=["vt"])
                    def fge(eng, fidx=fidx):
                        return eng.activation(sb["ffh"][:, fidx * 128:(fidx + 1) * 128],
                                              sb["ffh"][:, fidx * 128:(fidx + 1) * 128],
                                              AF.Gelu, bias=sb["s_b1_c"][:, fidx:fidx + 1])
                    pl.emit("scalar", fge, reads=["vt", "s_b1_c"], writes=["vt"])

            if qtr == 0:
                dump("d_ffh", "vt", 4096)
            for cg in range(8):
                for ci in range(4):
                    c = cg * 4 + ci
                    def fsq4(eng, c=c, ci=ci):
                        return eng.activation(sb["tB"][:, ci * 128:ci * 128 + 128],
                                              sb["ffh"][:, c * 128:(c + 1) * 128], AF.Square)
                    pl.emit("scalar", fsq4, reads=["vt"], writes=["tB0"])
                    def fm4(eng, c=c, ci=ci):
                        return eng.matmul(sb["pb1"][0:1, 0:128], sb["ones"].ap(),
                                          sb["tB"][:, ci * 128:ci * 128 + 128],
                                          start=(c == 0), stop=(c == 31))
                    pl.emit("tensor", fm4, reads=["tB0", "ones"],
                            writes=["pb1"] if c in (0, 31) else [])
                if cg == 0:
                    def fmx2(eng):
                        return eng.tensor_copy(sb["tA"][:, 0:512], sb["tB"][:, 0:512])
                else:
                    def fmx2(eng):
                        return eng.tensor_tensor(sb["tA"][:, 0:512], sb["tA"][:, 0:512],
                                                 sb["tB"][:, 0:512], MaxOp)
                pl.emit("vector", fmx2, reads=["tB0", "tA0"], writes=["tA0"])
            def fht(eng):
                return part_max(eng, sb["tA"], sb["tB"], 512, sb["stat2"][0:1, 0:512])
            pl.emit("vector", fht, reads=["tA0"],
                    writes=["tA0", "tA1", "tB0", "tB1", "stat2"])
            def fspc1(eng):
                return eng.tensor_copy(sb["shufM"][:, 0:512], sb["rng"][:, 0:512])
            pl.emit("vector", fspc1, reads=["rng"], writes=["shufM"])
            def fr1v(eng):
                return eng.tensor_tensor(sb["strow"][0:1, 256:512],
                                         sb["stat2"][0:1, 0:256],
                                         sb["stat2"][0:1, 256:512], MaxOp)
            pl.emit("vector", fr1v, reads=["stat2"], writes=["strow"])
            def fspc2(eng):
                return eng.tensor_copy(sb["shufM"][:, 0:512], sb["rng"][:, 0:512])
            pl.emit("vector", fspc2, reads=["rng"], writes=["shufM"])
            def fr2v(eng):
                return eng.tensor_tensor(sb["stat"][0:1, 0:128],
                                         sb["strow"][0:1, 256:384],
                                         sb["strow"][0:1, 384:512], MaxOp)
            pl.emit("vector", fr2v, reads=["strow"], writes=["stat"])
            def fhts(eng):
                return eng.activation(sb["stat"][0:1, 0:128], sb["stat"][0:1, 0:128], AF.Sqrt)
            pl.emit("scalar", fhts, reads=["stat"], writes=["stat"])
            def fqv2(eng):
                st = sb["stat"]
                eng.tensor_scalar(sb["strow"][0:1, 0:128], sb["pb1"][0:1, 0:128], 1.0 / FF, EPS,
                                  Mul, Add)
                return eng.reciprocal(st[0:1, 512:640], sb["strow"][0:1, 0:128])
            pl.emit("vector", fqv2, reads=["pb1"], writes=["stat"])
            def fqs2(eng):
                return eng.activation(sb["stat"][0:1, 512:640], sb["stat"][0:1, 512:640],
                                      AF.Sqrt)
            pl.emit("scalar", fqs2, reads=["stat"], writes=["stat"])
            def fq22(eng):
                st = sb["stat"]
                eng.tensor_tensor(st[0:1, 1024:1152], st[0:1, 512:640], st[0:1, 0:128], Mul)
                eng.tensor_scalar(st[0:1, 1024:1152], st[0:1, 1024:1152], 1e-5, None, MaxOp)
                eng.reciprocal(sb["stat2"][0:1, 0:128], st[0:1, 1024:1152])
                eng.tensor_tensor(sb["stat2"][0:1, 0:128], sb["stat2"][0:1, 0:128],
                                  st[0:1, 512:640], Mul)
                eng.tensor_scalar(sb["stat2"][0:1, 0:128], sb["stat2"][0:1, 0:128], 127.0, None, Mul)
                return eng.tensor_scalar(sb["strow"][0:1, 0:128], st[0:1, 1024:1152],
                                         sb["m227"][0:1, 0:1], None, Mul)
            pl.emit("vector", fq22, reads=["stat", "m227"], writes=["stat", "stat2", "strow"])
            bcast_row("pb2", sb["stat2"][0:1, 0:128], 128, 0, ["stat"])
            bcast_row("pb3", sb["strow"][0:1, 0:128], 128, 0, ["strow"])
            def fdq2(eng):
                return eng.tensor_copy(sb["deqb"][:, 0:128], sb["pb3"][:, 0:128])
            pl.emit("vector", fdq2, reads=["pb3"], writes=["deqb2"])
            if qtr == 0:
                dump("d_dq2", "deqb", 128, extra=("deqb2",))
            for cg in range(8):
                c4 = cg * 4
                def fz1b(eng, c4=c4):
                    last = None
                    for ci in range(4):
                        last = eng.tensor_tensor(
                            sb["tA"][:, ci * 128:ci * 128 + 128],
                            sb["ffh"][:, (c4 + ci) * 128:(c4 + ci + 1) * 128],
                            sb["pb2"][:, 0:128], Mul)
                    return last
                pl.emit("vector", fz1b, reads=["vt", "pb2"], writes=["tA0"])
                def fz3b(eng):
                    eng.tensor_scalar(sb["tB"][:, 0:512], sb["tA"][:, 0:512],
                                      12582912.0, None, Add)
                    return eng.tensor_scalar(sb["tB"][:, 0:512], sb["tB"][:, 0:512],
                                             -12582912.0, None, Add)
                pl.emit("vector", fz3b, reads=["tA0"], writes=["tB0"])
                def fz4b(eng):
                    return eng.tensor_copy(sb["tB"].ap().bitcast(I32)[:, 512:1024],
                                           sb["tB"][:, 0:512])
                pl.emit("vector", fz4b, reads=["tB0"], writes=["tB1"])
                def fz5b(eng, c4=c4):
                    return eng.tensor_scalar(ffqv()[:, c4 * 128:(c4 + 4) * 128],
                                             sb["tB"].ap().bitcast(I32)[:, 512:1024],
                                             -128.0, 127.0, MaxOp, MinOp)
                pl.emit("vector", fz5b, reads=["tB1"], writes=["qt"])
            if qtr == 0:
                dump("d_ffq", "qt", 2048)

            for fs in range(8):
                hw = fs % 2
                def fwl2(eng, fs=fs, hw=hw):
                    src = bass.AP(w2q.tensor, fs * 524288,
                                  [[1024, 128], [131072, 4], [1, 1024]])
                    return eng.dma_start(
                        out=sb["wbuf"].ap().bitcast(BF16)[:, hw * 4096:hw * 4096 + 4096],
                        in_=src)
                pl.emit("sync", fwl2,
                        writes=["wbuf"] + ["wbufc%d" % (hw * 4 + i) for i in range(4)])
                for di in range(8):
                    bank = "pb%d" % di
                    for cc in range(4):
                        c = fs * 4 + cc
                        def fm5(eng, cc=cc, di=di, c=c, bank=bank, fs=fs, hw=hw):
                            wb = sb["wbuf"].ap().bitcast(BF16)
                            return eng.matmul(
                                sb[bank][:, 0:128],
                                wb[:, hw * 4096 + cc * 1024 + di * 128:
                                   hw * 4096 + cc * 1024 + di * 128 + 128],
                                ffqv()[:, c * 128:(c + 1) * 128],
                                start=(fs == 0 and cc == 0), stop=(fs == 7 and cc == 3))
                        pl.emit("tensor", fm5,
                                reads=["wbufc%d" % (hw * 4 + cc), "qt"],
                                writes=[bank] if ((fs == 0 and cc == 0)
                                                  or (fs == 7 and cc == 3)) else [])
            for di in range(8):
                bank = "pb%d" % di
                def foA(eng, di=di, bank=bank):
                    return eng.tensor_tensor(sb["tA"][:, di * 128:di * 128 + 128],
                                             sb[bank][:, 0:128],
                                             sb["deqb"][:, 0:128], Mul)
                pl.emit("vector", foA, reads=[bank, "deqb2"],
                        writes=["tA0" if di < 4 else "tA1"])
            for di in range(8):
                def foB(eng, di=di, ht0=ht0):
                    return eng.scalar_tensor_tensor(
                        sb["xg"][:, di * TOK + ht0:di * TOK + ht0 + 128],
                        sb["tA"][:, di * 128:di * 128 + 128],
                        sb["s_b2_c"][:, di:di + 1],
                        sb["xg"][:, di * TOK + ht0:di * TOK + ht0 + 128], Add, Add)
                pl.emit("vector", foB, reads=["tA0", "tA1", "s_b2_c", "xg"],
                        writes=["xg"])

        pl.count["dO"] = pl.count.get("dO", 0)
        for c in range(8):
            def fod(eng, c=c, t0=t0):
                return eng.dma_start(out=out_ap[c * 128:(c + 1) * 128, t0:t0 + TOK],
                                     in_=sb["xg"][:, c * TOK:(c + 1) * TOK])
            sem = "dO"
            waits = {}
            r = pl._get("xg")
            if r["w"] is not None:
                s, v = r["w"]
                waits[s] = max(waits.get(s, 0), v)
            pl.count[sem] += 16
            r["r"].append((sem, pl.count[sem]))
            pl.steps["gpsimd"].append((fod, sorted(waits.items())))

    for g in range(GROUPS):
        group_body(g)
    if pl.count.get("dO", 0) == 0:
        for c in range(8):
            def fod(eng, c=c):
                return eng.dma_start(out=out_ap[c * 128:(c + 1) * 128, 0:TOK],
                                     in_=sb["xg"][:, c * TOK:(c + 1) * TOK])
            waits = {}
            r = pl._get("xg")
            if r["w"] is not None:
                sx, vx = r["w"]
                waits[sx] = vx
            pl.count["dO"] += 16
            r["r"].append(("dO", pl.count["dO"]))
            pl.steps["gpsimd"].append((fod, sorted(waits.items())))

    fin_waits = [("dO", pl.count["dO"])]
    if pl.count.get("dD", 0):
        fin_waits.append(("dD", pl.count["dD"]))
    pl.steps["gpsimd"].append((None, fin_waits))

    sem_objs = {}
    sem_guards = []
    for s in sorted(set(list(ENG_SEM.values()) + ["dO", "dD"])):
        gd = nc.semaphore(s)
        sem_objs[s] = gd.__enter__()
        sem_guards.append(gd)
    blk = nc.Block()
    block = blk.__enter__()

    def stream(eng_name):
        def body(eng):
            own = ENG_SEM[eng_name]
            amt = ENG_AMT[eng_name]
            for item in pl.steps[eng_name]:
                fn, waits = item
                for (s, v) in waits:
                    eng.wait_ge(sem_objs[s], v)
                if fn is None:
                    continue
                inst = fn(eng)
                if fn.__name__ in ("fod", "fn") and eng_name in ("gpsimd", "scalar")                         and getattr(fn, "_dma_out", False):
                    inst.then_inc(sem_objs["dD"], 16)
                elif eng_name == "gpsimd" and fn.__name__ == "fod":
                    inst.then_inc(sem_objs["dO"], 16)
                elif inst is None:
                    eng.sem_inc(sem_objs[own], amt)
                else:
                    inst.then_inc(sem_objs[own], amt)
        return body

    block.sync(stream("sync"))
    block.tensor(stream("tensor"))
    block.vector(stream("vector"))
    block.scalar(stream("scalar"))
    block.gpsimd(stream("gpsimd"))

    blk.__exit__(None, None, None)
    if _RACE_HOOK is not None:
        _RACE_HOOK(pl, sb, nc)
    for gd in reversed(sem_guards):
        gd.__exit__(None, None, None)
    for gd in reversed(ctx_stack):
        gd.__exit__(None, None, None)
    return nc


def bank_ap(x):
    return x


def _host_inputs(core_x, w):
    d = {}
    d["xT"] = np.ascontiguousarray(core_x.T)
    for nm, key in (("wqT", "Wq"), ("wkT", "Wk"), ("wvT", "Wv"), ("waT", "Wa"),
                    ("wbT", "Wb"), ("wgT", "Wg"), ("woT", "Wo"), ("w1T", "w1"),
                    ("w2T", "w2")):
        d[nm] = np.ascontiguousarray(w[key].T)
    d["ba_c"] = np.ascontiguousarray(w["ba"].reshape(8, 128).T)
    d["bb_c"] = np.ascontiguousarray(w["bb"].reshape(8, 1))
    d["b1_c"] = np.ascontiguousarray(w["b1"].reshape(32, 128).T)
    d["b2_c"] = np.ascontiguousarray(w["b2"].reshape(8, 128).T)
    d["qcw_c"] = np.ascontiguousarray(w["qc_w"].reshape(4, 128, 4).transpose(1, 0, 2).reshape(128, 16))
    d["kcw_c"] = np.ascontiguousarray(w["kc_w"].reshape(4, 128, 4).transpose(1, 0, 2).reshape(128, 16))
    d["vcw_c"] = np.ascontiguousarray(w["vc_w"].reshape(8, 128, 4).transpose(1, 0, 2).reshape(128, 32))
    d["qcb_c"] = np.ascontiguousarray(w["qc_b"].reshape(4, 128).T)
    d["kcb_c"] = np.ascontiguousarray(w["kc_b"].reshape(4, 128).T)
    d["vcb_c"] = np.ascontiguousarray(w["vc_b"].reshape(8, 128).T)
    d["lnw_c"] = np.ascontiguousarray(w["ln_w"].reshape(128, 1))
    d["lnb_c"] = np.ascontiguousarray(w["ln_b"].reshape(128, 1))
    d["ident"] = np.eye(128, dtype=np.float32)
    s = np.arange(64)[:, None]
    t = np.arange(64)[None, :]
    d["maskU"] = np.ascontiguousarray(np.tile((s < t).astype(np.float32), (1, 8)))
    d["maskQn"] = np.ascontiguousarray(np.tile(-(s <= t).astype(np.float32), (1, 8)))
    d["ones_c"] = np.ones((128, 1), np.float32)
    d["onesr_c"] = np.ones((128, 128), np.float32)
    sel = np.zeros((64, 1024), np.float32)
    for i in range(8):
        sel[i, i * 128:(i + 1) * 128] = 1.0
        sel[32 + i, i * 128:(i + 1) * 128] = 1.0
    d["sel8_c"] = sel
    for k in d:
        d[k] = np.ascontiguousarray(d[k], dtype=np.float32)
    return d


def kernel(L0_attn_x, L0_attn_Wq, L0_attn_Wk, L0_attn_Wv, L0_attn_Wa, L0_attn_ba,
           L0_attn_Wb, L0_attn_bb, L0_attn_Wg, L0_attn_Wo, L0_attn_qc_w, L0_attn_qc_b,
           L0_attn_kc_w, L0_attn_kc_b, L0_attn_vc_w, L0_attn_vc_b, L0_attn_ln_w,
           L0_attn_ln_b, L0_ffn_w1, L0_ffn_b1, L0_ffn_w2, L0_ffn_b2):
    B, T, _ = L0_attn_x.shape
    w = {"Wq": L0_attn_Wq, "Wk": L0_attn_Wk, "Wv": L0_attn_Wv, "Wa": L0_attn_Wa,
         "ba": L0_attn_ba, "Wb": L0_attn_Wb, "bb": L0_attn_bb, "Wg": L0_attn_Wg,
         "Wo": L0_attn_Wo, "qc_w": np.asarray(L0_attn_qc_w).reshape(512, 4),
         "qc_b": L0_attn_qc_b, "kc_w": np.asarray(L0_attn_kc_w).reshape(512, 4),
         "kc_b": L0_attn_kc_b, "vc_w": np.asarray(L0_attn_vc_w).reshape(1024, 4),
         "vc_b": L0_attn_vc_b, "ln_w": L0_attn_ln_w, "ln_b": L0_attn_ln_b,
         "w1": L0_ffn_w1, "b1": L0_ffn_b1, "w2": L0_ffn_w2, "b2": L0_ffn_b2}
    w = {k: np.asarray(v, dtype=np.float32) for k, v in w.items()}

    TOK = 512
    GROUPS = T // TOK
    nc = build_program(TOK, GROUPS)
    in_maps = [_host_inputs(np.asarray(L0_attn_x[b], np.float32), w) for b in range(B)]
    res = run_bass_kernel_spmd(nc, in_maps, core_ids=list(range(B)))
    out = np.stack([np.asarray(res.results[b]["outT"], np.float32).T for b in range(B)])
    return out
